# version 02
# baseline (speedup 1.0000x reference)
import numpy as np

import concourse.bass as bass
import concourse.mybir as mybir
from concourse.bass_utils import run_bass_kernel_spmd
from concourse.bass_core_rust import IndirectOffsetOnAxis

F32 = mybir.dt.float32
BF16 = mybir.dt.bfloat16
U32 = mybir.dt.uint32
AF = mybir.ActivationFunctionType
ALU = mybir.AluOpType
AX = mybir.AxisListType

NCORES = 8
T = 8192
D = 2048
NH = 16
HD = 128
HPC = NH // NCORES
TPC = T // NCORES
E = 32
K = 6
CAP = 768
EPC = E // NCORES
II = 1024
MSCALE = 1.5
SWIGLU_LIMIT = 10.0
EPS = 1e-5
ATT_SCALE = 1.0 / float(np.sqrt(HD))
EXP_SHIFT = -12.0
NA = T * K

ENGS = ("tensor", "scalar", "vector", "gpsimd", "sync")


class _Sem:
    def __init__(self, name, step=1):
        self.name, self.step, self.n, self.h = name, step, 0, None

    def inc(self):
        self.n += self.step
        return self.n


class _Plan:
    def __init__(self, nc):
        self.ops = {e: [] for e in ENGS}
        self.eng = {"tensor": nc.tensor, "scalar": nc.scalar,
                    "vector": nc.vector, "gpsimd": nc.gpsimd, "sync": nc.sync}

    def add(self, e, waits, mk, inc=None):
        eng = self.eng[e]
        waits = [(s, v) for (s, v) in waits if v > 0]

        def f():
            for (s, v) in waits:
                eng.wait_ge(s.h, v)
            ins = mk()
            if inc is not None:
                ins.then_inc(inc[0].h, inc[0].step)
        self.ops[e].append(f)

    def wait(self, e, waits):
        eng = self.eng[e]
        waits = [(s, v) for (s, v) in waits if v > 0]

        def f():
            for (s, v) in waits:
                eng.wait_ge(s.h, v)
        self.ops[e].append(f)


def _build(stop_after=None):
    nc = bass.Bass("TRN2", target_bir_lowering=False)
    NCT, NCS, NCV, NCP, NCY = nc.tensor, nc.scalar, nc.vector, nc.gpsimd, nc.sync

    def din(name, shape, dt=F32):
        return nc.dram_tensor(name, shape, dt, kind="ExternalInput").ap()

    x_slice = din("x_slice", [TPC, D])
    wq = din("wq", [D, HPC * HD])
    wk = din("wk", [D, HPC * HD])
    wv = din("wv", [D, HPC * HD])
    wo = din("wo", [D, D])
    bo_bc = din("bo_bc", [128, D])
    gwT = din("gwT", [D, E])
    gbias_bc = din("gbias_bc", [128, E])
    wgT = din("wgT", [EPC, D, II])
    wuT = din("wuT", [EPC, D, II])
    wdT = din("wdT", [EPC, II, D])
    sgT = din("sgT", [D, II])
    suT = din("suT", [D, II])
    sdT = din("sdT", [II, D])
    masks = din("masks", [4, 128, 512], BF16)
    iota32_bc = din("iota32_bc", [128, E])
    ident = din("ident", [128, 128], BF16)
    lsta = din("lsta", [2, 128, 64], BF16)
    lstaf = din("lstaf", [2, 128, 64])
    ones1f = din("ones1f", [64, 128])
    rk4 = din("rk4", [128, 1])
    qoff = din("qoff", [1, 1], U32)
    aoff = din("aoff", [1, 1], U32)
    iotacol = din("iotacol", [32, 1])
    out_slice = nc.dram_tensor("out_slice", [TPC, D], F32,
                               kind="ExternalOutput").ap()

    def dint(name, shape, dt):
        return nc.dram_tensor(name, shape, dt).ap()

    xbf_loc = dint("xbf_loc", [TPC, D], BF16)
    xbf_full = dint("xbf_full", [T, D], BF16)
    vt_dram = dint("vt_dram", [HPC, 128, T], BF16)
    ctx_locj = [dint(f"ctx_loc{j}", [HPC * HD, 2048], BF16) for j in range(4)]
    ctxfull_h = nc.dram_tensor("ctxfull", [T, D], BF16)
    ctxfull = ctxfull_h.ap()
    h1_dram = dint("h1_dram", [TPC, D], F32)
    h1loc = dint("h1loc", [TPC, D], BF16)
    h1full = dint("h1full", [T, D], BF16)
    idxloc = dint("idxloc", [TPC, K], F32)
    idxfull_h = nc.dram_tensor("idxfull", [T, K], F32)
    idxfull = idxfull_h.ap()
    posflat_h = nc.dram_tensor("posflat", [1, NA], F32)
    posflat = posflat_h.ap()
    shared_dram = dint("shared_dram", [TPC, D], BF16)
    xb_disp = dint("xb_disp", [EPC * CAP, D], BF16)
    yb_loc = dint("yb_loc", [EPC * CAP, D], BF16)
    ybfull = dint("ybfull", [E * CAP, D], BF16)

    plan = _Plan(nc)
    regs = {}

    def _alloc_regs():
        regs["disp"] = NCP.alloc_register("rb_disp")
        NCP.reg_mov(regs["disp"], EPC * CAP - 1)
        regs["ret"] = NCP.alloc_register("rb_ret")
        NCP.reg_mov(regs["ret"], E * CAP - 1)
    plan.ops["gpsimd"].append(_alloc_regs)

    sT = _Sem("sT")
    sA = _Sem("sA")
    sV = _Sem("sV")
    sP = _Sem("sP")
    dS = _Sem("dS", 16)
    dG = _Sem("dG", 16)
    sC = _Sem("sC")
    sems = [sT, sA, sV, sP, dS, dG, sC]

    from contextlib import ExitStack
    stack = ExitStack()

    def SB(name, shape, dt):
        return stack.enter_context(nc.sbuf_tensor(name, shape, dt))

    def PS(name, shape, dt):
        return stack.enter_context(nc.psum_tensor(name, shape, dt))

    ppA = PS("ppA", [128, 1024], F32)
    ppB = PS("ppB", [128, 1024], F32)
    ppC = PS("ppC", [128, 512], F32)
    ppD = PS("ppD", [128, 512], F32)
    ppE = PS("ppE", [64, 512], F32)
    ppT = PS("ppT", [128, 1024], BF16)
    ppDbf = ppD[:, :].bitcast(BF16)

    def _tslot(i):
        bank = ppT if i % 2 == 0 else ppDbf
        sl = (i // 2) % 4
        return bank[:, sl * 128:(sl + 1) * 128]

    AR = SB("AR", [128, 49152], BF16)
    AF32 = SB("AF32", [128, 5 * D], F32)
    xTh = [SB(f"xTh{i}", [128, 8 * 512], BF16) for i in range(2)]
    wqb = SB("wqb", [128, 16 * 256], BF16)
    wkb = SB("wkb", [128, 16 * 256], BF16)
    wvb = SB("wvb", [128, 16 * 256], BF16)
    PTb = [SB(f"PT{i}", [128, 1024], BF16) for i in range(2)]
    PTraw = SB("PTraw", [128, 512], BF16)
    maskb = SB("maskb", [128, 4 * 512], BF16)
    expb = SB("expb", [128, 1], F32)
    lstab = SB("lstab", [128, 2 * 64], BF16)
    lstafb = SB("lstafb", [128, 2 * 64], F32)
    linv = SB("linv", [64, 512], F32)
    lbcs = SB("lbcs", [128, 512], F32)
    ctxst = SB("ctxst", [128, 1024], BF16)
    onesf = SB("onesf", [64, 128], F32)
    bob = SB("bob", [128, D], BF16)
    stat = SB("stat", [128, 16], F32)
    smal = SB("smal", [128, 512], F32)
    idxu = SB("idxu", [128, 512], U32)
    identb = SB("identb", [128, 128], BF16)
    iotacolb = SB("iotacolb", [32, 1], F32)
    iob = SB("iob", [128, 32], F32)
    gbb = SB("gbb", [128, 32], F32)
    rk4b = SB("rk4b", [128, 1], F32)
    aoffs = SB("aoffs", [1, 1], U32)
    arow = SB("arow", [1, 2 * 512], F32)
    posrow = SB("posrow", [1, 512], F32)
    routb = SB("routb", [128, 48], F32)
    routw = SB("routw", [128, 48], F32)
    weff = SB("weff", [128, 48], F32)
    topv = SB("topv", [128, 8], F32)
    idx8u = SB("idx8u", [128, 8], U32)
    idx8f = SB("idx8f", [128, 8], F32)
    carry32 = SB("carry32", [32, 1], F32)
    z32 = SB("z32", [32, 1], F32)

    lacc = [lbcs[:, :], smal[:, :]]
    QT = [AR[:, 0:8192], AR[:, 8192:16384]]
    KT = [AR[:, 16384:24576], AR[:, 24576:32768]]
    VT = [AR[:, 32768:40960], AR[:, 40960:49152]]
    Vn = VT
    wob = AR[:, 0:32768]
    ctxTmy = AR[:, 32768:49152]

    xcast = [AF32[:, 0:D], AF32[:, D:2 * D]]
    zbuf = [AF32[:, 2 * D:3 * D], AF32[:, 3 * D:4 * D]]
    sqscr = AF32[:, 4 * D:5 * D]
    xcb = [xTh[0][:, 0:2048], xTh[1][:, 0:2048]]
    h1hi = [xTh[0][:, 0:2048], xTh[0][:, 2048:4096],
            xTh[1][:, 0:2048], xTh[1][:, 2048:4096],
            wqb[:, 0:2048], wqb[:, 2048:4096],
            wkb[:, 0:2048], wkb[:, 2048:4096]]

    store_tok = {}
    for c in range(8):
        buf = c % 2
        plan.add("sync", [(sV, sV.n)] if c >= 2 else [],
                 (lambda c=c, buf=buf: NCY.dma_start(
                     out=xcast[buf], in_=x_slice[c * 128:(c + 1) * 128, :])),
                 (dS, dS.inc()))
        plan.add("vector",
                 [(dS, dS.n)] + ([(dG, store_tok[c - 2])] if c >= 2 else []),
                 (lambda buf=buf: NCV.tensor_copy(xcb[buf], xcast[buf])),
                 (sV, sV.inc()))
        plan.add("gpsimd", [(sV, sV.n)],
                 (lambda c=c, buf=buf: NCP.dma_start(
                     out=xbf_loc[c * 128:(c + 1) * 128, :], in_=xcb[buf])),
                 (dG, dG.inc()))
        store_tok[c] = dG.n
    plan.add("gpsimd", [(dG, dG.n)],
             (lambda: NCP.collective_compute(
                 "AllGather", ALU.bypass, replica_groups=[list(range(NCORES))],
                 ins=[xbf_loc], outs=[xbf_full])),
             (sC, sC.inc()))
    xag_done = sC.n

    plan.add("vector", [], lambda: NCV.memset(expb[:, :], EXP_SHIFT),
             (sV, sV.inc()))
    plan.add("sync", [], lambda: NCY.dma_start(
        out=maskb[:, :].rearrange("p (j f) -> p j f", j=4),
        in_=masks.rearrange("j p f -> p j f")),
        (dS, dS.inc()))
    plan.add("sync", [], lambda: NCY.dma_start(
        out=lstab[:, :].rearrange("p (j f) -> p j f", j=2),
        in_=lsta.rearrange("j p f -> p j f")),
        (dS, dS.inc()))
    plan.add("sync", [], lambda: NCY.dma_start(
        out=lstafb[:, :].rearrange("p (j f) -> p j f", j=2),
        in_=lstaf.rearrange("j p f -> p j f")),
        (dS, dS.inc()))
    plan.add("sync", [], lambda: NCY.dma_start(out=onesf[:, :], in_=ones1f),
             (dS, dS.inc()))
    plan.add("gpsimd", [], lambda: NCP.dma_start(out=bob[:, :], in_=bo_bc),
             (dG, dG.inc()))
    bob_done = dG.n
    consts_done = dS.n
    for (wsrc, wdst) in ((wq, wqb), (wk, wkb), (wv, wvb)):
        plan.add("gpsimd", [],
                 (lambda wsrc=wsrc, wdst=wdst: NCP.dma_start(
                     out=wdst[:, :].rearrange("p (c m) -> p c m", c=16),
                     in_=wsrc.rearrange("(c p) m -> p c m", p=128))),
                 (dG, dG.inc()))
    wcast_done = dG.n

    pd_list = [(wqb, 0, QT), (wqb, 1, QT), (wkb, 0, KT), (wkb, 1, KT),
               (wvb, 0, VT), (wvb, 1, VT)]
    pd_ps = [ppA[:, 0:512], ppA[:, 512:1024], ppB[:, 0:512],
             ppB[:, 512:1024], ppC[:, :], ppD[:, :]]
    half_mm_tok = {}
    evac_tok_p2 = 0
    vt_store = 0
    vn_tok = {}
    for tw in range(16):
        t0 = tw * 512
        for hc in range(2):
            hidx = tw * 2 + hc
            hb = hidx % 2
            w = [(sC, xag_done)] if hidx == 0 else []
            if hidx >= 2:
                w.append((sT, half_mm_tok[hidx - 2]))
            for chl in range(8):
                ch = hc * 8 + chl
                plan.add("sync", w if chl == 0 else [],
                         (lambda t0=t0, ch=ch, hb=hb, chl=chl: NCY.dma_start(
                             out=xTh[hb][:, chl * 512:(chl + 1) * 512],
                             in_=xbf_full[t0:t0 + 512,
                                          ch * 128:(ch + 1) * 128],
                             transpose=True)),
                         (dS, dS.inc()))
            xh_tok = dS.n
            for pdi, (wsb, dt_i, dstT) in enumerate(pd_list):
                for chl in range(8):
                    ch = hc * 8 + chl
                    wts = []
                    if pdi == 0 and chl == 0:
                        wts = [(dS, xh_tok), (dG, wcast_done)]
                        if hc == 0 and evac_tok_p2:
                            wts.append((sV, evac_tok_p2))
                    last = (pdi == 5 and chl == 7)
                    plan.add("tensor", wts,
                             (lambda wsb=wsb, dt_i=dt_i, ch=ch, chl=chl, hb=hb,
                                     pdi=pdi: NCT.matmul(
                                 pd_ps[pdi],
                                 wsb[:, ch * 256 + dt_i * 128:
                                     ch * 256 + (dt_i + 1) * 128],
                                 xTh[hb][:, chl * 512:(chl + 1) * 512],
                                 start=(ch == 0), stop=(ch == 15))),
                             (sT, sT.inc()) if last else None)
            half_mm_tok[hidx] = sT.n
        mm_tok = sT.n
        for pdi, (wsb, dt_i, dstT) in enumerate(pd_list):
            plan.add("vector", [(sT, mm_tok)] if pdi == 0 else [],
                     (lambda pdi=pdi, dt_i=dt_i, dstT=dstT, t0=t0:
                      NCV.tensor_copy(dstT[dt_i][:, t0:t0 + 512], pd_ps[pdi])),
                     (sV, sV.inc()))
        evac_tok_p2 = sV.n
        for dt_i in range(2):
            plan.add("gpsimd", [(sV, evac_tok_p2)] if dt_i == 0 else [],
                     (lambda dt_i=dt_i, t0=t0: NCP.dma_start(
                         out=vt_dram[dt_i, :, t0:t0 + 512],
                         in_=VT[dt_i][:, t0:t0 + 512])),
                     (dG, dG.inc()))
        vt_store = dG.n
        for h in range(HPC):
            for kb in range(4 * tw, 4 * tw + 4):
                plan.add("sync", [(dG, vt_store)] if (h == 0 and kb == 4 * tw)
                         else [],
                         (lambda h=h, kb=kb: NCY.dma_start(
                             out=Vn[h][:, kb * 128:(kb + 1) * 128],
                             in_=vt_dram[h, :, kb * 128:(kb + 1) * 128],
                             transpose=True)),
                         (dS, dS.inc()))
                vn_tok[(h, kb)] = dS.n
    qkv_V = sV.n
    vn_ready = dS.n

    last_ctx_dma = 0
    for s in range(8):
        for h in range(HPC):
            nkb = 8 * (s + 1)
            exp_tok = {}
            mask_tok = {}
            sS_tok = {}
            lacc_tok = {}
            ctx_started = [False, False]
            first_of_stripe = True
            plan.add("vector", [(sV, sV.n)],
                     (lambda: NCV.memset(lacc[0], 0.0)), (sV, sV.inc()))
            plan.add("vector", [(sV, sV.n)],
                     (lambda: NCV.memset(lacc[1], 0.0)), (sV, sV.inc()))

            def active(kb, w, s=s):
                return kb <= 8 * s + 4 * w + 3

            for kb in range(nkb + 1):
                par = kb % 2
                sp = ppA if par == 0 else ppB
                if kb < nkb:
                    wts = []
                    if first_of_stripe:
                        wts = [(dS, vn_tok[(h, nkb - 1)]), (sA, sA.n),
                               (sV, sV.n)]
                        first_of_stripe = False
                    elif kb >= 2:
                        wts = [(sA, exp_tok[kb - 2])]
                    acts = [w for w in range(2) if active(kb, w)]
                    for j, w in enumerate(acts):
                        plan.add("tensor", wts if j == 0 else [],
                                 (lambda h=h, kb=kb, w=w, sp=sp, s=s:
                                  NCT.matmul(
                                     sp[:, w * 512:(w + 1) * 512],
                                     KT[h][:, kb * 128:(kb + 1) * 128],
                                     QT[h][:, 1024 * s + w * 512:
                                            1024 * s + (w + 1) * 512],
                                     start=True, stop=True)),
                                 (sT, sT.inc()) if j == len(acts) - 1 else None)
                    sS_tok[kb] = sT.n

                    sw = None
                    for w in acts:
                        if 0 <= kb - 8 * s - 4 * w <= 3:
                            sw = w
                    dws = [w for w in acts if w != sw]
                    swait = [(sT, sS_tok[kb])]
                    if (kb - 1) in mask_tok:
                        swait.append((sV, mask_tok[kb - 1]))
                    if (kb - 2) in lacc_tok:
                        swait.append((sV, lacc_tok[kb - 2]))
                    if dws:
                        lo, hi = min(dws), max(dws)
                        plan.add("scalar", swait,
                                 (lambda par=par, lo=lo, hi=hi, sp=sp:
                                  NCS.activation(
                                      PTb[par][:, lo * 512:(hi + 1) * 512],
                                      sp[:, lo * 512:(hi + 1) * 512],
                                      AF.Exp, bias=expb[:, 0:1],
                                      scale=ATT_SCALE)),
                                 (sA, sA.inc()) if sw is None else None)
                        swait = []
                    if sw is not None:
                        plan.add("scalar", swait,
                                 (lambda par=par, sw=sw, sp=sp: NCS.activation(
                                     PTraw[:, :],
                                     sp[:, sw * 512:(sw + 1) * 512],
                                     AF.Exp, bias=expb[:, 0:1],
                                     scale=ATT_SCALE)),
                                 (sA, sA.inc()))
                        j = kb - 8 * s - 4 * sw
                        plan.add("vector", [(sA, sA.n)],
                                 (lambda par=par, sw=sw, j=j: NCV.tensor_mul(
                                     PTb[par][:, sw * 512:(sw + 1) * 512],
                                     PTraw[:, :],
                                     maskb[:, j * 512:(j + 1) * 512])),
                                 (sV, sV.inc()))
                        mask_tok[kb] = sV.n
                    exp_tok[kb] = sA.n

                if kb >= 1:
                    kprev = kb - 1
                    parp = kprev % 2
                    actsp = [w for w in range(2) if active(kprev, w)]
                    wts = [(sA, exp_tok[kprev])]
                    if kprev in mask_tok:
                        wts.append((sV, mask_tok[kprev]))
                    first = True
                    for w in actsp:
                        ctx_ps = ppC if w == 0 else ppD
                        is_last_ctx = (kprev == 8 * s + 4 * w + 3)
                        plan.add("tensor", wts if first else [],
                                 (lambda h=h, kprev=kprev, w=w, parp=parp,
                                         ctx_ps=ctx_ps, st=not ctx_started[w],
                                         sp_=is_last_ctx: NCT.matmul(
                                     ctx_ps[:, :],
                                     Vn[h][:, kprev * 128:(kprev + 1) * 128],
                                     PTb[parp][:, w * 512:(w + 1) * 512],
                                     start=st, stop=sp_)),
                                 (sT, sT.inc()) if w == actsp[-1] else None)
                        first = False
                        ctx_started[w] = True
                        vw = [(sA, exp_tok[kprev]), (sV, sV.n)]
                        if kprev in mask_tok:
                            vw.append((sV, mask_tok[kprev]))
                        plan.add("vector", vw,
                                 (lambda w=w, parp=parp: NCV.tensor_add(
                                     lacc[w], lacc[w],
                                     PTb[parp][:, w * 512:(w + 1) * 512])),
                                 (sV, sV.inc()))
                        lacc_tok[kprev] = sV.n
            flash_T = sT.n
            lacc_last = sV.n
            for w in range(2):
                plan.add("tensor", [(sV, lacc_last)] if w == 0 else [],
                         (lambda w=w: NCT.matmul(
                             ppE[:, :], lstafb[:, w * 64:(w + 1) * 64],
                             lacc[w], start=(w == 0), stop=(w == 1))),
                         (sT, sT.inc()))
            flash_T = sT.n

            plan.add("vector", [(sT, flash_T), (dG, last_ctx_dma)],
                     (lambda: NCV.reciprocal(linv[:, :], ppE[0:64, :])),
                     (sV, sV.inc()))
            for w in range(2):
                plan.add("tensor", [(sV, sV.n)],
                         (lambda w=w: NCT.matmul(
                             ppA[:, 0:512], onesf[32 * w:32 * w + 1, :],
                             linv[32 * w:32 * w + 1, :],
                             start=True, stop=True)),
                         (sT, sT.inc()))
                plan.add("vector", [(sT, sT.n)],
                         (lambda: NCV.tensor_copy(lbcs[:, :], ppA[:, 0:512])),
                         (sV, sV.inc()))
                ctx_ps = ppC if w == 0 else ppD
                plan.add("vector", [(sV, sV.n)],
                         (lambda w=w, ctx_ps=ctx_ps: NCV.tensor_mul(
                             ctxst[:, w * 512:(w + 1) * 512],
                             ctx_ps[:, :], lbcs[:, :])),
                         (sV, sV.inc()))
            evac_tok = sV.n
            plan.add("gpsimd", [(sV, evac_tok)],
                     (lambda s=s, h=h: NCP.dma_start(
                         out=ctx_locj[s // 2][h * 128:(h + 1) * 128,
                                             (s % 2) * 1024:
                                             (s % 2) * 1024 + 1024],
                         in_=ctxst[:, :])),
                     (dG, dG.inc()))
            last_ctx_dma = dG.n
            plan.wait("tensor", [(sV, evac_tok)])
        if s % 2 == 1:
            j = s // 2
            plan.add("gpsimd", [(dG, last_ctx_dma)],
                     (lambda j=j: NCP.collective_compute(
                         "AllGather", ALU.bypass,
                         replica_groups=[list(range(NCORES))],
                         ins=[ctx_locj[j]],
                         outs=[ctxfull[j * 2048:(j + 1) * 2048, :]])),
                     (sC, sC.inc()))
    ctx_stored = dG.n
    ctxag_done = sC.n

    qoffs = SB("qoffs", [1, 1], U32)
    plan.add("sync", [], lambda: NCY.dma_start(out=qoffs[:, :], in_=qoff),
             (dS, dS.inc()))
    qoff_tok = dS.n

    def _ld_ctxTmy():
        NCP.wait_ge(dS.h, qoff_tok)
        NCP.wait_ge(sC.h, ctxag_done)
        with NCP.register("rq") as rq:
            NCP.reg_load(rq, qoffs[0:1, 0:1])
            return NCP.dma_start(
                out=ctxTmy.rearrange("p (c m) -> p c m", c=16),
                in_=bass.AP(ctxfull_h, rq,
                            [[2048, 128], [2048 * 128, 16], [1, 1024]]))
    plan.add("gpsimd", [], _ld_ctxTmy, (dG, dG.inc()))
    ctxmy_ready = dG.n
    plan.add("gpsimd", [(dG, ctx_stored)],
             (lambda: NCP.dma_start(
                 out=wob.rearrange("p (c m) -> p c m", c=16),
                 in_=wo.rearrange("(c p) m -> p c m", p=128))),
             (dG, dG.inc()))
    wo_ready = dG.n

    h1hi_tok = [0] * 8
    h1_spill = 0
    for c in range(8):
        buf = c % 2
        plan.add("sync", [],
                 (lambda c=c, buf=buf: NCY.dma_start(
                     out=xcast[buf], in_=x_slice[c * 128:(c + 1) * 128, :])),
                 (dS, dS.inc()))
        xrow_tok = dS.n
        wts = [(dG, max(ctxmy_ready, 0)), (dG, wo_ready)]
        if c == 0:
            wts += [(sA, sA.n), (sV, sV.n)]
        for ch in range(16):
            for nhalf in range(2):
                ps = ppA if nhalf == 0 else ppB
                for nn in range(2):
                    plan.add("tensor",
                             wts if (ch == 0 and nhalf == 0 and nn == 0)
                             else [],
                             (lambda c=c, ch=ch, nhalf=nhalf, nn=nn, ps=ps:
                              NCT.matmul(
                                  ps[:, nn * 512:(nn + 1) * 512],
                                  ctxTmy[:, ch * 1024 + c * 128:
                                         ch * 1024 + (c + 1) * 128],
                                  wob[:, ch * 2048 + nhalf * 1024 + nn * 512:
                                      ch * 2048 + nhalf * 1024 +
                                      (nn + 1) * 512],
                                  start=(ch == 0), stop=(ch == 15))),
                             (sT, sT.inc()) if (ch == 15 and nhalf == 1 and
                                                nn == 1) else None)
            wts = []
        att_tok = sT.n
        for nhalf in range(2):
            ps = ppA if nhalf == 0 else ppB
            plan.add("vector",
                     [(sT, att_tok), (dS, xrow_tok)] +
                     ([(dS, consts_done)] if c == 0 else []),
                     (lambda buf=buf, nhalf=nhalf, ps=ps:
                      NCV.scalar_tensor_tensor(
                          zbuf[buf][:, nhalf * 1024:(nhalf + 1) * 1024],
                          ps[:, :], 1.0,
                          xcast[buf][:, nhalf * 1024:(nhalf + 1) * 1024],
                          ALU.mult, ALU.add)),
                     (sV, sV.inc()))
        plan.add("vector", [(sV, sV.n)] + ([(dG, bob_done)] if c == 0 else []),
                 (lambda buf=buf: NCV.scalar_tensor_tensor(
                     zbuf[buf], zbuf[buf], 1.0, bob[:, :],
                     ALU.mult, ALU.add, accum_out=stat[:, 0:1])),
                 (sV, sV.inc()))
        z_tok = sV.n
        plan.wait("tensor", [(sV, z_tok)])
        plan.add("scalar", [(sV, z_tok)],
                 (lambda buf=buf: NCS.activation(
                     sqscr, zbuf[buf], AF.Square, accum_out=stat[:, 1:2])),
                 (sA, sA.inc()))
        sq_tok = sA.n
        plan.add("vector", [(sA, sq_tok)],
                 (lambda: NCV.tensor_scalar(stat[:, 2:3], stat[:, 0:1],
                                            1.0 / D, 0.0, ALU.mult, ALU.add)),
                 (sV, sV.inc()))
        plan.add("vector", [(sV, sV.n)],
                 (lambda: NCV.tensor_scalar(stat[:, 3:4], stat[:, 1:2],
                                            1.0 / D, 0.0, ALU.mult, ALU.add)),
                 (sV, sV.inc()))
        plan.add("vector", [(sV, sV.n)],
                 (lambda: NCV.tensor_mul(stat[:, 4:5], stat[:, 2:3],
                                         stat[:, 2:3])),
                 (sV, sV.inc()))
        plan.add("vector", [(sV, sV.n)],
                 (lambda: NCV.tensor_sub(stat[:, 5:6], stat[:, 3:4],
                                         stat[:, 4:5])),
                 (sV, sV.inc()))
        plan.add("vector", [(sV, sV.n)],
                 (lambda: NCV.tensor_scalar(stat[:, 6:7], stat[:, 5:6],
                                            1.0, EPS, ALU.mult, ALU.add)),
                 (sV, sV.inc()))
        var_tok = sV.n
        plan.add("scalar", [(sV, var_tok)],
                 (lambda: NCS.activation(stat[:, 7:8], stat[:, 6:7], AF.Ln)),
                 (sA, sA.inc()))
        plan.add("scalar", [(sA, sA.n)],
                 (lambda: NCS.activation(stat[:, 8:9], stat[:, 7:8], AF.Exp,
                                         scale=-0.5)),
                 (sA, sA.inc()))
        rstd_tok = sA.n
        plan.add("vector", [(sA, rstd_tok)],
                 (lambda buf=buf: NCV.tensor_scalar(
                     zbuf[buf], zbuf[buf], stat[:, 2:3], stat[:, 8:9],
                     ALU.subtract, ALU.mult)),
                 (sV, sV.inc()))
        h1_tok = sV.n
        plan.add("vector", [(sV, h1_tok)],
                 (lambda c=c, buf=buf: NCV.tensor_copy(h1hi[c], zbuf[buf])),
                 (sV, sV.inc()))
        h1hi_tok[c] = sV.n
        plan.add("sync", [(sV, sV.n)],
                 (lambda c=c, buf=buf: NCY.dma_start(
                     out=h1_dram[c * 128:(c + 1) * 128, :], in_=zbuf[buf])),
                 (dS, dS.inc()))
        h1_spill = dS.n
    h1_V = sV.n

    if stop_after == "h1":
        for c in range(8):
            plan.add("gpsimd", [(dS, h1_spill)] if c == 0 else [],
                     (lambda c=c: NCP.dma_start(
                         out=out_slice[c * 128:(c + 1) * 128, :],
                         in_=h1_dram[c * 128:(c + 1) * 128, :])),
                     (dG, dG.inc()))
        plan.wait("gpsimd", [(dG, dG.n)])
        _finalize(nc, plan, sems, stack)
        return nc

    h1T = AR[:, 0:16384]
    wstr = AR[:, 16384:32768]
    xbT = AR[:, 32768:45056]
    gsil = AR[:, 0:6144]
    intT = AR[:, 6144:12288]
    shg = AR[:, 32768:40960]
    siT = AR[:, 40960:49152]
    obit1 = AR[0:32, 16384:32768].bitcast(F32)
    obit2 = AR[0:32, 32768:49152].bitcast(F32)
    Obuf = obit1[:, 0:4096]
    prodb = obit1[:, 4096:8192]
    cumb = obit2[:, 0:4096]
    loT = maskb[:, 0:2048]
    lo_nat = [PTb[0], PTb[1]]
    gwhib = ctxst[:, 0:512]
    gwlob = ctxst[:, 512:1024]
    gbuf = [wvb[:, 0:2048], wvb[:, 2048:4096]]
    ybev = maskb[:, 0:2048]
    ebuf = AF32[:, 8192:8576]
    pbuf = AF32[:, 8576:8960]
    m3s1 = AF32[:, 8960:9344]
    m3s2 = AF32[:, 9344:9728]
    m3s3 = AF32[:, 9728:10112]
    sq2 = AF32[:, 8192:10240]
    pmy = lbcs[:, 0:48]
    vldm = lbcs[:, 48:96]
    t1v = lbcs[:, 96:144]
    e4f = lbcs[:, 144:192]
    rowf = lbcs[:, 192:240]
    qv = lbcs[:, 240:288]
    zfin = xcast[1]

    p4_T = sT.n
    for c in range(8):
        plan.add("gpsimd", [(sV, h1hi_tok[c])],
                 (lambda c=c: NCP.dma_start(
                     out=h1loc[c * 128:(c + 1) * 128, :], in_=h1hi[c])),
                 (dG, dG.inc()))
    plan.add("gpsimd", [(dG, dG.n)],
             (lambda: NCP.collective_compute(
                 "AllGather", ALU.bypass, replica_groups=[list(range(NCORES))],
                 ins=[h1loc], outs=[h1full])),
             (sC, sC.inc()))
    h1ag_done = sC.n

    if stop_after == "m0a":
        plan.add("gpsimd", [(sC, h1ag_done)],
                 (lambda: NCP.dma_start(out=out_slice[0:512, :],
                                        in_=h1full[0:512, :])),
                 (dG, dG.inc()))
        plan.wait("gpsimd", [(dG, dG.n)])
        _finalize(nc, plan, sems, stack)
        return nc

    plan.add("sync", [], lambda: NCY.dma_start(out=identb[:, :], in_=ident),
             (dS, dS.inc()))
    plan.add("sync", [], lambda: NCY.dma_start(out=iotacolb[:, :], in_=iotacol),
             (dS, dS.inc()))
    plan.add("sync", [], lambda: NCY.dma_start(out=iob[:, :], in_=iota32_bc),
             (dS, dS.inc()))
    plan.add("sync", [], lambda: NCY.dma_start(out=gbb[:, :], in_=gbias_bc),
             (dS, dS.inc()))
    plan.add("sync", [], lambda: NCY.dma_start(out=rk4b[:, :], in_=rk4),
             (dS, dS.inc()))
    plan.add("sync", [], lambda: NCY.dma_start(out=aoffs[:, :], in_=aoff),
             (dS, dS.inc()))
    mconsts = dS.n
    plan.add("sync", [], lambda: NCY.dma_start(
        out=smal[:, :], in_=gwT.rearrange("(c p) m -> p c m", p=128)),
        (dS, dS.inc()))
    gwf_tok = dS.n
    plan.add("vector", [(dS, gwf_tok), (sV, qkv_V)],
             (lambda: NCV.tensor_copy(gwhib, smal[:, :])), (sV, sV.inc()))
    plan.add("vector", [(sV, sV.n)],
             (lambda: NCV.tensor_sub(gwlob, smal[:, :], gwhib)),
             (sV, sV.inc()))
    gw_ready = sV.n

    if stop_after == "m0b":
        plan.add("sync", [(sV, gw_ready), (dS, mconsts)],
                 (lambda: NCY.dma_start(out=out_slice[0:128, 0:512],
                                        in_=smal[:, :])),
                 (dS, dS.inc()))
        plan.wait("gpsimd", [(dS, dS.n), (sC, h1ag_done)])
        _finalize(nc, plan, sems, stack)
        return nc

    tr_ev = {}
    for c in range(8):
        for ch in range(16):
            i = c * 16 + ch
            wts = [(sV, h1hi_tok[c]), (dS, mconsts)]
            if i >= 2:
                wts.append((sV, tr_ev[i - 2]))
            plan.add("tensor", wts,
                     (lambda c=c, ch=ch, i=i: NCT.transpose(
                         _tslot(i),
                         h1hi[c][:, ch * 128:(ch + 1) * 128], identb[:, :])),
                     (sT, sT.inc()))
            plan.add("vector", [(sT, sT.n)] + ([(sT, p4_T)] if i == 0 else []),
                     (lambda c=c, ch=ch, i=i: NCV.tensor_copy(
                         h1T[:, ch * 1024 + c * 128:ch * 1024 + (c + 1) * 128],
                         _tslot(i))),
                     (sV, sV.inc()))
            tr_ev[i] = sV.n
    h1T_done = sV.n

    if stop_after == "m0":
        plan.add("gpsimd", [(sV, h1T_done), (sC, h1ag_done)],
                 (lambda: NCP.dma_start(out=out_slice[0:128, :],
                                        in_=h1T[:, 0:2048])),
                 (dG, dG.inc()))
        plan.wait("gpsimd", [(dG, dG.n)])
        _finalize(nc, plan, sems, stack)
        return nc

    exp_prev = 0
    lt_ev = {}
    for c in range(8):
        buf = c % 2
        plan.add("sync", [(sV, sV.n)],
                 (lambda c=c, buf=buf: NCY.dma_start(
                     out=xcast[buf], in_=h1_dram[c * 128:(c + 1) * 128, :])),
                 (dS, dS.inc()))
        h1f_tok = dS.n
        plan.add("vector", [(dS, h1f_tok)],
                 (lambda c=c, buf=buf: NCV.tensor_sub(
                     zbuf[buf], xcast[buf], h1hi[c])),
                 (sV, sV.inc()))
        plan.add("vector", [(sV, sV.n)],
                 (lambda buf=buf: NCV.tensor_copy(
                     lo_nat[0][:, :], zbuf[buf][:, 0:1024])),
                 (sV, sV.inc()))
        plan.add("vector", [(sV, sV.n)],
                 (lambda buf=buf: NCV.tensor_copy(
                     lo_nat[1][:, :], zbuf[buf][:, 1024:2048])),
                 (sV, sV.inc()))
        lo_tok = sV.n
        for ch in range(16):
            src = lo_nat[0][:, ch * 128:(ch + 1) * 128] if ch < 8 else \
                lo_nat[1][:, (ch - 8) * 128:(ch - 8 + 1) * 128]
            key = c * 16 + ch
            wts = [(sV, lo_tok)]
            if key >= 2:
                wts.append((sV, lt_ev[key - 2]))
            else:
                wts.append((sV, h1T_done))
            plan.add("tensor", wts,
                     (lambda src=src, key=key: NCT.transpose(
                         _tslot(key), src, identb[:, :])),
                     (sT, sT.inc()))
            plan.add("vector", [(sT, sT.n)],
                     (lambda ch=ch, key=key: NCV.tensor_copy(
                         loT[:, ch * 128:(ch + 1) * 128], _tslot(key))),
                     (sV, sV.inc()))
            lt_ev[key] = sV.n
        loT_tok = sV.n
        nmm = 0
        for ps in range(3):
            for ch in range(16):
                wts = []
                if nmm == 0:
                    wts = [(sV, loT_tok), (sV, gw_ready), (sV, h1T_done),
                           (sA, exp_prev)]
                lhs = (h1T[:, ch * 1024 + c * 128:ch * 1024 + (c + 1) * 128]
                       if ps < 2 else loT[:, ch * 128:(ch + 1) * 128])
                rhs = (gwlob if ps == 1 else gwhib)[:, ch * 32:(ch + 1) * 32]
                plan.add("tensor", wts,
                         (lambda lhs=lhs, rhs=rhs, nmm=nmm: NCT.matmul(
                             ppC[:, 0:32], lhs, rhs,
                             start=(nmm == 0), stop=(nmm == 47))),
                         (sT, sT.inc()) if nmm == 47 else None)
                nmm += 1
        gate_T = sT.n
        plan.add("scalar", [(sT, gate_T), (dS, mconsts)],
                 (lambda: NCS.activation(smal[:, 0:32], ppC[:, 0:32], AF.Exp)),
                 (sA, sA.inc()))
        exp_prev = sA.n
        plan.add("vector", [(sA, sA.n)],
                 (lambda: NCV.tensor_scalar(smal[:, 32:64], smal[:, 0:32],
                                            1.0, 1.0, ALU.mult, ALU.add)),
                 (sV, sV.inc()))
        plan.add("scalar", [(sV, sV.n)],
                 (lambda: NCS.activation(smal[:, 64:96], smal[:, 32:64],
                                         AF.Ln)),
                 (sA, sA.inc()))
        plan.add("scalar", [(sA, sA.n)],
                 (lambda: NCS.activation(smal[:, 96:128], smal[:, 64:96],
                                         AF.Ln)),
                 (sA, sA.inc()))
        plan.add("scalar", [(sA, sA.n)],
                 (lambda: NCS.activation(smal[:, 128:160], smal[:, 96:128],
                                         AF.Exp, scale=0.5)),
                 (sA, sA.inc()))
        sc_tok = sA.n
        plan.add("vector", [(sA, sc_tok)],
                 (lambda: NCV.tensor_add(smal[:, 160:192], smal[:, 128:160],
                                         gbb[:, :])),
                 (sV, sV.inc()))
        plan.add("vector", [(sV, sV.n)],
                 (lambda: NCV.max(topv[:, :], smal[:, 160:192])),
                 (sV, sV.inc()))
        plan.add("vector", [(sV, sV.n)],
                 (lambda: NCV.max_index(idx8u[:, :], topv[:, :],
                                        smal[:, 160:192])),
                 (sV, sV.inc()))
        plan.add("vector", [(sV, sV.n)],
                 (lambda: NCV.tensor_copy(idx8f[:, :], idx8u[:, :])),
                 (sV, sV.inc()))
        plan.add("vector", [(sV, sV.n)],
                 (lambda c=c: NCV.tensor_copy(routb[:, c * 6:(c + 1) * 6],
                                              idx8f[:, 0:6])),
                 (sV, sV.inc()))
        for k in range(K):
            plan.add("vector", [(sV, sV.n)],
                     (lambda c=c, k=k: NCV.scalar_tensor_tensor(
                         smal[:, 192:224], iob[:, :], idx8f[:, k:k + 1],
                         smal[:, 128:160], ALU.is_equal, ALU.mult,
                         accum_out=routw[:, c * 6 + k:c * 6 + k + 1])),
                     (sV, sV.inc()))
        plan.add("vector", [(sV, sV.n)],
                 (lambda c=c: NCV.reduce_sum(stat[:, 9:10],
                                             routw[:, c * 6:(c + 1) * 6],
                                             axis=AX.X)),
                 (sV, sV.inc()))
        plan.add("vector", [(sV, sV.n)],
                 (lambda: NCV.reciprocal(stat[:, 10:11], stat[:, 9:10])),
                 (sV, sV.inc()))
        plan.add("vector", [(sV, sV.n)],
                 (lambda c=c: NCV.tensor_scalar(
                     routw[:, c * 6:(c + 1) * 6],
                     routw[:, c * 6:(c + 1) * 6],
                     stat[:, 10:11], MSCALE, ALU.mult, ALU.mult)),
                 (sV, sV.inc()))
        plan.add("gpsimd", [(sV, sV.n)],
                 (lambda c=c: NCP.dma_start(
                     out=idxloc[c * 128:(c + 1) * 128, :],
                     in_=routb[:, c * 6:(c + 1) * 6])),
                 (dG, dG.inc()))
    routw_tok = sV.n
    plan.add("gpsimd", [(dG, dG.n)],
             (lambda: NCP.collective_compute(
                 "AllGather", ALU.bypass, replica_groups=[list(range(NCORES))],
                 ins=[idxloc], outs=[idxfull])),
             (sC, sC.inc()))
    idxag_done = sC.n

    if stop_after == "m1":
        plan.add("vector", [(sV, routw_tok)],
                 (lambda: NCV.tensor_copy(zbuf[0][:, 0:48], routb[:, :])),
                 (sV, sV.inc()))
        plan.add("vector", [(sV, sV.n)],
                 (lambda: NCV.tensor_copy(zbuf[0][:, 48:96], routw[:, :])),
                 (sV, sV.inc()))
        plan.add("sync", [(sV, sV.n), (sC, idxag_done)],
                 (lambda: NCY.dma_start(out=out_slice[0:128, :],
                                        in_=zbuf[0])),
                 (dS, dS.inc()))
        plan.wait("gpsimd", [(dS, dS.n)])
        _finalize(nc, plan, sems, stack)
        return nc

    plan.add("vector", [(sV, sV.n)], lambda: NCV.memset(z32[:, :], 0.0),
             (sV, sV.inc()))
    plan.add("vector", [(sV, sV.n)], lambda: NCV.memset(carry32[:, :], 0.0),
             (sV, sV.inc()))
    CHW = 4096
    NCH = NA // CHW
    pr_tok = [0, 0]
    for chn in range(NCH):
        a0 = chn * CHW
        for n in range(8):
            bps = ppC if n % 2 == 0 else ppD
            ab = n % 2
            plan.add("sync",
                     ([(sC, idxag_done)] if chn == 0 and n == 0 else []) +
                     ([(sT, sT.n)] if (chn > 0 or n >= 2) else []),
                     (lambda a0=a0, n=n, ab=ab: NCY.dma_start(
                         out=arow[:, ab * 512:(ab + 1) * 512],
                         in_=bass.AP(idxfull_h, a0 + n * 512,
                                     [[512, 1], [1, 512]]))),
                     (dS, dS.inc()))
            arow_tok = dS.n
            wts = [(dS, arow_tok)]
            if chn == 0 and n == 0:
                wts.append((sA, exp_prev))
            plan.add("tensor", wts,
                     (lambda n=n, bps=bps, ab=ab: NCT.matmul(
                         bps[0:32, 0:512], onesf[0:1, 0:32],
                         arow[:, ab * 512:(ab + 1) * 512],
                         start=True, stop=True)),
                     (sT, sT.inc()))
            plan.add("vector", [(sT, sT.n)] +
                     ([(sT, p4_T), (sV, h1T_done)] if chn == 0 and n == 0
                      else []),
                     (lambda n=n, bps=bps: NCV.tensor_single_scalar(
                         Obuf[:, n * 512:(n + 1) * 512], bps[0:32, 0:512],
                         iotacolb[:, 0:1], ALU.is_equal)),
                     (sV, sV.inc()))
            plan.wait("tensor", [(sV, sV.n)])
        ob_tok = sV.n
        plan.add("vector", [(sV, ob_tok)],
                 (lambda: NCV.tensor_tensor_scan(
                     cumb[:, :], Obuf[:, :],
                     z32[:, :].broadcast_to([32, CHW]),
                     carry32[:, 0:1], ALU.add, ALU.add)),
                 (sV, sV.inc()))
        plan.add("vector", [(sV, sV.n)],
                 (lambda: NCV.tensor_copy(carry32[:, :],
                                          cumb[:, CHW - 1:CHW])),
                 (sV, sV.inc()))
        plan.add("vector", [(sV, sV.n)],
                 (lambda: NCV.tensor_mul(prodb[:, :], Obuf[:, :], cumb[:, :])),
                 (sV, sV.inc()))
        prod_tok = sV.n
        for n in range(8):
            ab = n % 2
            wts = [(sV, prod_tok)] if n == 0 else []
            plan.add("tensor", wts,
                     (lambda n=n: NCT.matmul(
                         ppE[0:1, 0:512], onesf[0:32, 0:1],
                         prodb[:, n * 512:(n + 1) * 512],
                         start=True, stop=True)),
                     (sT, sT.inc()))
            plan.add("vector", [(sT, sT.n)] +
                     ([(dS, pr_tok[0])] if pr_tok[0] else []),
                     (lambda n=n: NCV.tensor_copy(
                         posrow[:, 0:512], ppE[0:1, 0:512])),
                     (sV, sV.inc()))
            plan.wait("tensor", [(sV, sV.n)])
            plan.add("sync", [(sV, sV.n)],
                     (lambda a0=a0, n=n: NCY.dma_start(
                         out=bass.AP(posflat_h, a0 + n * 512,
                                     [[512, 1], [1, 512]]),
                         in_=posrow[:, 0:512])),
                     (dS, dS.inc()))
            pr_tok[0] = dS.n
    pos_done = dS.n

    plan.add("sync", [(dS, pos_done)],
             (lambda: NCY.dma_start(
                 out=ebuf,
                 in_=bass.AP(idxfull_h, 0, [[6, 128], [768, 64], [1, 6]]))),
             (dS, dS.inc()))
    plan.add("sync", [],
             (lambda: NCY.dma_start(
                 out=pbuf,
                 in_=bass.AP(posflat_h, 0, [[6, 128], [768, 64], [1, 6]]))),
             (dS, dS.inc()))
    ep_tok = dS.n
    plan.add("vector", [(dS, mconsts), (sV, sV.n)],
             (lambda: NCV.tensor_scalar(stat[:, 11:12], rk4b[:, :],
                                        1.0, 3.0, ALU.mult, ALU.add)),
             (sV, sV.inc()))
    plan.add("vector", [(sV, sV.n)],
             (lambda: NCV.tensor_scalar(stat[:, 12:13], rk4b[:, :],
                                        768.0, 1.0, ALU.mult, ALU.add)),
             (sV, sV.inc()))
    plan.add("vector", [(sV, sV.n), (dS, ep_tok)],
             (lambda: NCV.tensor_single_scalar(m3s1, pbuf, 768.0, ALU.is_le)),
             (sV, sV.inc()))
    plan.add("vector", [(sV, sV.n)],
             (lambda: NCV.tensor_scalar(m3s2, ebuf, rk4b[:, 0:1], None,
                                        ALU.is_ge)),
             (sV, sV.inc()))
    plan.add("vector", [(sV, sV.n)],
             (lambda: NCV.tensor_mul(m3s1, m3s1, m3s2)),
             (sV, sV.inc()))
    plan.add("vector", [(sV, sV.n)],
             (lambda: NCV.tensor_scalar(m3s2, ebuf, stat[:, 11:12], None,
                                        ALU.is_le)),
             (sV, sV.inc()))
    plan.add("vector", [(sV, sV.n)],
             (lambda: NCV.tensor_mul(m3s1, m3s1, m3s2)),
             (sV, sV.inc()))
    sel_tok = sV.n
    plan.add("vector", [(sV, sV.n)],
             (lambda: NCV.scalar_tensor_tensor(m3s2, ebuf, 768.0, pbuf,
                                               ALU.mult, ALU.add)),
             (sV, sV.inc()))
    plan.add("vector", [(sV, sV.n)],
             (lambda: NCV.tensor_scalar(m3s2, m3s2, stat[:, 12:13], None,
                                        ALU.subtract)),
             (sV, sV.inc()))
    BIGS = float(1 << 20)
    plan.add("vector", [(sV, sV.n)],
             (lambda: NCV.tensor_scalar(m3s2, m3s2, 1.0, -BIGS,
                                        ALU.mult, ALU.add)),
             (sV, sV.inc()))
    plan.add("vector", [(sV, sV.n)],
             (lambda: NCV.tensor_mul(m3s2, m3s2, m3s1)),
             (sV, sV.inc()))
    plan.add("vector", [(sV, sV.n)],
             (lambda: NCV.tensor_scalar(m3s2, m3s2, 1.0, BIGS,
                                        ALU.mult, ALU.add)),
             (sV, sV.inc()))
    plan.add("vector", [(sV, sV.n)],
             (lambda: NCV.tensor_copy(idxu[:, 0:384], m3s2)),
             (sV, sV.inc()))
    idxd_tok = sV.n

    if stop_after == "m3":
        plan.add("vector", [(sV, idxd_tok)],
                 (lambda: NCV.tensor_copy(zbuf[0][:, 0:48], routb[:, :])),
                 (sV, sV.inc()))
        plan.add("vector", [(sV, sV.n)],
                 (lambda: NCV.tensor_copy(zbuf[0][:, 48:96], routw[:, :])),
                 (sV, sV.inc()))
        plan.add("vector", [(sV, sV.n)],
                 (lambda: NCV.tensor_copy(zbuf[0][:, 96:480], m3s2)),
                 (sV, sV.inc()))
        plan.add("sync", [(sV, sV.n)],
                 (lambda: NCY.dma_start(out=out_slice[0:128, :],
                                        in_=zbuf[0])),
                 (dS, dS.inc()))
        plan.add("sync", [(dS, pos_done)],
                 (lambda: NCY.dma_start(
                     out=out_slice.rearrange("a b -> (a b)")[
                         300000:300000 + NA].rearrange("(a b) -> a b", a=1),
                     in_=posflat)),
                 (dS, dS.inc()))
        plan.wait("gpsimd", [(dS, dS.n)])
        _finalize(nc, plan, sems, stack)
        return nc

    sc_tok = {}
    for c64 in range(64):
        db = c64 % 2
        wts = [(sC, h1ag_done)]
        if c64 >= 2:
            wts.append((dG, sc_tok[c64 - 2]))
        plan.add("sync", wts,
                 (lambda c64=c64, db=db: NCY.dma_start(
                     out=gbuf[db][:, :],
                     in_=h1full[c64 * 128:(c64 + 1) * 128, :])),
                 (dS, dS.inc()))
        src_tok = dS.n
        for k in range(K):
            wts = [(dS, src_tok)] if k == 0 else []
            if c64 == 0 and k == 0:
                wts.append((sV, idxd_tok))
            if k == 0 and c64 >= 4:
                wts.append((dG, sc_tok[c64 - 4]))
            plan.add("gpsimd", wts,
                     (lambda c64=c64, k=k, db=db: NCP.indirect_dma_start(
                         xb_disp,
                         IndirectOffsetOnAxis(idxu[:, c64 * 6 + k:
                                                   c64 * 6 + k + 1], 0),
                         gbuf[db][:, :], None,
                         bounds_check=regs["disp"], oob_is_err=False)),
                     (dG, dG.inc()))
        sc_tok[c64] = dG.n
    disp_done = dG.n

    m3_done_T = sT.n

    def _wload(srcT, nch, wid, extra_w=()):
        toks = []
        for ch in range(nch):
            bufw = ch % 2
            wts = list(extra_w) if ch == 0 else []
            if ch >= 2:
                wts.append((sV, toks[ch - 2]))
            plan.add("sync", wts + ([(sV, sV.n)] if ch < 2 else []),
                     (lambda srcT=srcT, ch=ch, bufw=bufw: NCY.dma_start(
                         out=xcast[bufw][:, 0:wid] if wid <= D else None,
                         in_=srcT[ch * 128:(ch + 1) * 128, :])),
                     (dS, dS.inc()))
            plan.add("vector", [(dS, dS.n)],
                     (lambda ch=ch, bufw=bufw: NCV.tensor_copy(
                         wstr[:, ch * wid:(ch + 1) * wid],
                         xcast[bufw][:, 0:wid])),
                     (sV, sV.inc()))
            toks.append(sV.n)
        return sV.n

    sg_tok = _wload(sgT, 16, 1024, extra_w=[(sT, m3_done_T), (sV, sV.n)])
    sh_mm = 0
    sil_tok = [0, 0]
    for m in range(8):
        mps = ppA if m % 2 == 0 else ppB
        for qh in range(2):
            for ch in range(16):
                wts = []
                if ch == 0:
                    wts = [(sV, sg_tok)]
                    if m >= 2 or (m >= 1 and qh >= 0):
                        wts.append((sA, sil_tok[m % 2]))
                plan.add("tensor", wts,
                         (lambda m=m, qh=qh, ch=ch, mps=mps: NCT.matmul(
                             mps[:, qh * 512:(qh + 1) * 512],
                             wstr[:, ch * 1024 + m * 128:
                                  ch * 1024 + (m + 1) * 128],
                             h1T[:, ch * 1024 + qh * 512:
                                 ch * 1024 + (qh + 1) * 512],
                             start=(ch == 0), stop=(ch == 15))),
                         (sT, sT.inc()) if ch == 15 else None)
            plan.add("scalar", [(sT, sT.n)],
                     (lambda m=m, qh=qh, mps=mps: NCS.activation(
                         shg[:, m * 1024 + qh * 512:
                             m * 1024 + (qh + 1) * 512],
                         mps[:, qh * 512:(qh + 1) * 512], AF.Silu)),
                     (sA, sA.inc()))
        sil_tok[m % 2] = sA.n
    sg_mm_T = sT.n
    su_tok = _wload(suT, 16, 1024, extra_w=[(sT, sg_mm_T)])
    si_ev = [0, 0]
    for m in range(8):
        mps = ppA if m % 2 == 0 else ppB
        for qh in range(2):
            for ch in range(16):
                wts = []
                if ch == 0:
                    wts = [(sV, su_tok), (sA, sil_tok[1])]
                    if m >= 2:
                        wts.append((sV, si_ev[m % 2]))
                plan.add("tensor", wts,
                         (lambda m=m, qh=qh, ch=ch, mps=mps: NCT.matmul(
                             mps[:, qh * 512:(qh + 1) * 512],
                             wstr[:, ch * 1024 + m * 128:
                                  ch * 1024 + (m + 1) * 128],
                             h1T[:, ch * 1024 + qh * 512:
                                 ch * 1024 + (qh + 1) * 512],
                             start=(ch == 0), stop=(ch == 15))),
                         (sT, sT.inc()) if ch == 15 else None)
            plan.add("vector", [(sT, sT.n), (sV, sV.n)],
                     (lambda m=m, qh=qh, mps=mps: NCV.tensor_mul(
                         smal[:, 0:512],
                         shg[:, m * 1024 + qh * 512:m * 1024 + (qh + 1) * 512],
                         mps[:, qh * 512:(qh + 1) * 512])),
                     (sV, sV.inc()))
            plan.add("vector", [(sV, sV.n)],
                     (lambda m=m, qh=qh: NCV.tensor_scalar(
                         siT[:, m * 1024 + qh * 512:m * 1024 + (qh + 1) * 512],
                         smal[:, 0:512], SWIGLU_LIMIT, -SWIGLU_LIMIT,
                         ALU.min, ALU.max)),
                     (sV, sV.inc()))
        si_ev[m % 2] = sV.n
    su_mm_T = sT.n
    si_done = sV.n
    sd_tok = _wload(sdT, 8, 2048, extra_w=[(sT, su_mm_T)])
    sh_store = 0
    for ct in range(8):
        for ich in range(8):
            for n in range(4):
                ps = ppA if n < 2 else ppB
                wts = []
                if ich == 0 and n == 0:
                    wts = [(sV, sd_tok), (sV, si_done)]
                    if ct >= 1:
                        wts.append((sV, sV.n))
                plan.add("tensor", wts,
                         (lambda ct=ct, ich=ich, n=n, ps=ps: NCT.matmul(
                             ps[:, (n % 2) * 512:(n % 2 + 1) * 512],
                             siT[:, ich * 1024 + ct * 128:
                                 ich * 1024 + (ct + 1) * 128],
                             wstr[:, ich * 2048 + n * 512:
                                  ich * 2048 + (n + 1) * 512],
                             start=(ich == 0), stop=(ich == 7))),
                         (sT, sT.inc()) if (ich == 7 and n == 3) else None)
        shd_T = sT.n
        plan.add("vector", [(sT, shd_T)],
                 (lambda: NCV.tensor_copy(ybev[:, 0:1024], ppA[:, :])),
                 (sV, sV.inc()))
        plan.add("vector", [(sV, sV.n)],
                 (lambda: NCV.tensor_copy(ybev[:, 1024:2048], ppB[:, :])),
                 (sV, sV.inc()))
        plan.add("sync", [(sV, sV.n)],
                 (lambda ct=ct: NCY.dma_start(
                     out=shared_dram[ct * 128:(ct + 1) * 128, :],
                     in_=ybev[:, 0:2048])),
                 (dS, dS.inc()))
        sh_store = dS.n
        plan.wait("tensor", [(sV, sV.n)])
    m2c_T = sT.n

    if stop_after == "m2":
        plan.add("gpsimd", [(dS, sh_store), (dG, disp_done)],
                 (lambda: NCP.dma_start(out=out_slice, in_=shared_dram)),
                 (dG, dG.inc()))
        plan.wait("gpsimd", [(dG, dG.n)])
        _finalize(nc, plan, sems, stack)
        return nc

    ag_toks = []
    xb_ev = {}
    for el in range(EPC):
        for ct in range(6):
            bufp = ct % 2
            wts = [(dG, disp_done)] if (el == 0 and ct == 0) else []
            if el == 0 and ct == 0:
                wts.append((sT, m2c_T))
                wts.append((sV, sV.n))
            if ct >= 2 or el > 0:
                wts.append((sT, sT.n))
            plan.add("sync", wts,
                     (lambda el=el, ct=ct, bufp=bufp: NCY.dma_start(
                         out=lo_nat[bufp][:, :],
                         in_=xb_disp[el * CAP + ct * 128:
                                     el * CAP + (ct + 1) * 128, 0:1024])),
                     (dS, dS.inc()))
            plan.add("sync", [],
                     (lambda el=el, ct=ct, bufp=bufp: NCY.dma_start(
                         out=gbuf[bufp][:, 0:1024],
                         in_=xb_disp[el * CAP + ct * 128:
                                     el * CAP + (ct + 1) * 128, 1024:2048])),
                     (dS, dS.inc()))
            xb_tok = dS.n
            for ch in range(16):
                src = (lo_nat[bufp][:, ch * 128:(ch + 1) * 128] if ch < 8
                       else gbuf[bufp][:, (ch - 8) * 128:(ch - 7) * 128])
                i = el * 96 + ct * 16 + ch
                wts2 = [(dS, xb_tok)] if ch == 0 else []
                if i >= 2 and xb_ev.get(i - 2):
                    wts2.append((sV, xb_ev[i - 2]))
                elif i < 2:
                    wts2.append((sV, lt_ev[127]))
                plan.add("tensor", wts2,
                         (lambda src=src, i=i: NCT.transpose(
                             _tslot(i), src, identb[:, :])),
                         (sT, sT.inc()))
                plan.add("vector", [(sT, sT.n)],
                         (lambda ch=ch, ct=ct, i=i: NCV.tensor_copy(
                             xbT[:, ch * 768 + ct * 128:
                                 ch * 768 + (ct + 1) * 128],
                             _tslot(i))),
                         (sV, sV.inc()))
                xb_ev[i] = sV.n
        xbT_tok = sV.n
        plan.add("gpsimd", [(sT, m2c_T) if el == 0 else (sT, sT.n)],
                 (lambda el=el: NCP.dma_start(
                     out=wstr[:, :].rearrange("p (c m) -> p c m", c=16),
                     in_=wgT[el].rearrange("(c p) m -> p c m", p=128))),
                 (dG, dG.inc()))
        wg_tok = dG.n
        gs_tok = [0, 0]
        for m in range(8):
            mps = ppA if m % 2 == 0 else ppB
            for nn in range(2):
                n0, n1 = (0, 512) if nn == 0 else (512, 768)
                for ch in range(16):
                    wts = []
                    if ch == 0 and nn == 0:
                        wts = [(dG, wg_tok), (sV, xbT_tok)]
                        if m >= 2:
                            wts.append((sA, gs_tok[m % 2]))
                    plan.add("tensor", wts,
                             (lambda m=m, ch=ch, n0=n0, n1=n1, mps=mps:
                              NCT.matmul(
                                  mps[:, n0:n1],
                                  wstr[:, ch * 1024 + m * 128:
                                       ch * 1024 + (m + 1) * 128],
                                  xbT[:, ch * 768 + n0:ch * 768 + n1],
                                  start=(ch == 0), stop=(ch == 15))),
                             (sT, sT.inc()) if (ch == 15 and nn == 1)
                             else None)
            plan.add("scalar", [(sT, sT.n)] +
                     ([(sV, h1T_done), (sA, sA.n)] if el == 0 and m == 0
                      else []),
                     (lambda m=m, mps=mps: NCS.activation(
                         gsil[:, m * 768:(m + 1) * 768],
                         mps[:, 0:768], AF.Silu)),
                     (sA, sA.inc()))
            gs_tok[m % 2] = sA.n
        g_mm_T = sT.n
        plan.add("gpsimd", [(sT, g_mm_T)],
                 (lambda el=el: NCP.dma_start(
                     out=wstr[:, :].rearrange("p (c m) -> p c m", c=16),
                     in_=wuT[el].rearrange("(c p) m -> p c m", p=128))),
                 (dG, dG.inc()))
        wu_tok = dG.n
        int_ev = [0, 0]
        for m in range(8):
            mps = ppA if m % 2 == 0 else ppB
            for nn in range(2):
                n0, n1 = (0, 512) if nn == 0 else (512, 768)
                for ch in range(16):
                    wts = []
                    if ch == 0 and nn == 0:
                        wts = [(dG, wu_tok), (sA, gs_tok[m % 2])]
                        if m >= 2:
                            wts.append((sV, int_ev[m % 2]))
                    plan.add("tensor", wts,
                             (lambda m=m, ch=ch, n0=n0, n1=n1, mps=mps:
                              NCT.matmul(
                                  mps[:, n0:n1],
                                  wstr[:, ch * 1024 + m * 128:
                                       ch * 1024 + (m + 1) * 128],
                                  xbT[:, ch * 768 + n0:ch * 768 + n1],
                                  start=(ch == 0), stop=(ch == 15))),
                             (sT, sT.inc()) if (ch == 15 and nn == 1)
                             else None)
            plan.add("vector", [(sT, sT.n)],
                     (lambda m=m, mps=mps: NCV.tensor_mul(
                         intT[:, m * 768:(m + 1) * 768],
                         gsil[:, m * 768:(m + 1) * 768],
                         mps[:, 0:768])),
                     (sV, sV.inc()))
            int_ev[m % 2] = sV.n
        u_mm_T = sT.n
        int_tok = sV.n
        plan.add("gpsimd", [(sT, u_mm_T)],
                 (lambda el=el: NCP.dma_start(
                     out=wstr[:, :].rearrange("p (c m) -> p c m", c=8),
                     in_=wdT[el].rearrange("(c p) m -> p c m", p=128))),
                 (dG, dG.inc()))
        wd_tok = dG.n
        yb_store = 0
        for ct in range(6):
            for ich in range(8):
                for n in range(4):
                    ps = ppA if n < 2 else ppB
                    wts = []
                    if ich == 0 and n == 0:
                        wts = [(dG, wd_tok), (sV, int_tok)]
                        if ct >= 1:
                            wts.append((sV, sV.n))
                    plan.add("tensor", wts,
                             (lambda ct=ct, ich=ich, n=n, ps=ps: NCT.matmul(
                                 ps[:, (n % 2) * 512:(n % 2 + 1) * 512],
                                 intT[:, ich * 768 + ct * 128:
                                      ich * 768 + (ct + 1) * 128],
                                 wstr[:, ich * 2048 + n * 512:
                                      ich * 2048 + (n + 1) * 512],
                                 start=(ich == 0), stop=(ich == 7))),
                             (sT, sT.inc()) if (ich == 7 and n == 3)
                             else None)
            plan.add("vector", [(sT, sT.n)],
                     (lambda: NCV.tensor_copy(ybev[:, 0:1024], ppA[:, :])),
                     (sV, sV.inc()))
            plan.add("vector", [(sV, sV.n)],
                     (lambda: NCV.tensor_copy(ybev[:, 1024:2048], ppB[:, :])),
                     (sV, sV.inc()))
            plan.add("sync", [(sV, sV.n)],
                     (lambda el=el, ct=ct: NCY.dma_start(
                         out=yb_loc[el * CAP + ct * 128:
                                    el * CAP + (ct + 1) * 128, :],
                         in_=ybev[:, 0:2048])),
                     (dS, dS.inc()))
            yb_store = dS.n
            plan.wait("tensor", [(sV, sV.n)])
        plan.add("gpsimd", [(dS, yb_store)],
                 (lambda el=el: NCP.collective_compute(
                     "AllGather", ALU.bypass,
                     replica_groups=[list(range(NCORES))],
                     ins=[yb_loc[el * CAP:(el + 1) * CAP, :]],
                     outs=[ybfull[el * NCORES * CAP:
                                  (el + 1) * NCORES * CAP, :]])),
                 (sC, sC.inc()))
        ag_toks.append(sC.n)
    yb_all = sC.n

    if stop_after == "m5":
        plan.add("gpsimd", [(sC, yb_all)],
                 (lambda: NCP.dma_start(out=out_slice,
                                        in_=ybfull[0:TPC, :])),
                 (dG, dG.inc()))
        plan.wait("gpsimd", [(dG, dG.n)])
        _finalize(nc, plan, sems, stack)
        return nc

    plan.add("gpsimd", [(dS, pos_done), (dS, mconsts)],
             (lambda: _m6_pmy_load()),
             (dG, dG.inc()))

    def _m6_pmy_load():
        with NCP.register("ra") as ra:
            NCP.reg_load(ra, aoffs[0:1, 0:1])
            return NCP.dma_start(
                out=pmy,
                in_=bass.AP(posflat_h, ra, [[6, 128], [768, 8], [1, 6]]))
    pmy_tok = dG.n
    plan.add("vector", [(dG, pmy_tok), (sV, routw_tok)],
             (lambda: NCV.tensor_single_scalar(vldm, pmy, 768.0, ALU.is_le)),
             (sV, sV.inc()))
    plan.add("vector", [(sV, sV.n)],
             (lambda: NCV.tensor_scalar(qv, routb[:, 0:48], 0.25, -0.375,
                                        ALU.mult, ALU.add)),
             (sV, sV.inc()))
    plan.add("vector", [(sV, sV.n)],
             (lambda: NCV.tensor_copy(idxu[:, 448:496], qv)),
             (sV, sV.inc()))
    plan.add("vector", [(sV, sV.n)],
             (lambda: NCV.tensor_copy(e4f, idxu[:, 448:496])),
             (sV, sV.inc()))
    plan.add("vector", [(sV, sV.n)],
             (lambda: NCV.scalar_tensor_tensor(t1v, routb[:, 0:48], 6144.0,
                                               pmy, ALU.mult, ALU.add)),
             (sV, sV.inc()))
    plan.add("vector", [(sV, sV.n)],
             (lambda: NCV.scalar_tensor_tensor(rowf, e4f, -23808.0, t1v,
                                               ALU.mult, ALU.add)),
             (sV, sV.inc()))
    plan.add("vector", [(sV, sV.n)],
             (lambda: NCV.tensor_scalar(rowf, rowf, 1.0, -1.0 - float(1 << 20),
                                        ALU.mult, ALU.add)),
             (sV, sV.inc()))
    plan.add("vector", [(sV, sV.n)],
             (lambda: NCV.tensor_mul(rowf, rowf, vldm)),
             (sV, sV.inc()))
    plan.add("vector", [(sV, sV.n)],
             (lambda: NCV.tensor_scalar(rowf, rowf, 1.0, float(1 << 20),
                                        ALU.mult, ALU.add)),
             (sV, sV.inc()))
    plan.add("vector", [(sV, sV.n)],
             (lambda: NCV.tensor_copy(idxu[:, 384:432], rowf)),
             (sV, sV.inc()))
    plan.add("vector", [(sV, sV.n)],
             (lambda: NCV.tensor_mul(weff[:, :], routw[:, 0:48], vldm)),
             (sV, sV.inc()))
    idxr_tok = sV.n
    plan.add("vector", [(sV, sV.n)],
             (lambda: NCV.memset(gbuf[0][:, :], 0.0)), (sV, sV.inc()))
    plan.add("vector", [(sV, sV.n)],
             (lambda: NCV.memset(gbuf[1][:, :], 0.0)), (sV, sV.inc()))
    gz_tok = sV.n

    out_tok = 0
    g_done = {}
    comb_tok = {}
    for c in range(8):
        plan.add("sync", [(sV, sV.n)] if c == 0 else [],
                 (lambda c=c: NCY.dma_start(
                     out=xcast[0], in_=h1_dram[c * 128:(c + 1) * 128, :])),
                 (dS, dS.inc()))
        plan.add("sync", [(dS, sh_store)] if c == 0 else [],
                 (lambda c=c: NCY.dma_start(
                     out=ybev[:, 0:2048],
                     in_=shared_dram[c * 128:(c + 1) * 128, :])),
                 (dS, dS.inc()))
        ld_tok = dS.n
        for k in range(K):
            kk = c * 6 + k
            wts = []
            if c == 0 and k == 0:
                wts = [(sC, yb_all), (sV, idxr_tok), (sV, gz_tok)]
            if kk >= 2:
                wts.append((sV, comb_tok.get(kk - 2, 0)))
            plan.add("gpsimd", wts,
                     (lambda c=c, k=k, kk=kk: NCP.indirect_dma_start(
                         gbuf[kk % 2][:, :], None,
                         ybfull,
                         IndirectOffsetOnAxis(idxu[:, 384 + kk:385 + kk], 0),
                         bounds_check=regs["ret"], oob_is_err=False)),
                     (dG, dG.inc()))
            g_done[kk] = dG.n
            if k == 0:
                plan.add("vector", [(dG, g_done[kk])],
                         (lambda c=c, kk=kk: NCV.tensor_scalar(
                             zbuf[0], gbuf[kk % 2][:, :],
                             weff[:, kk:kk + 1], None, ALU.mult)),
                         (sV, sV.inc()))
            else:
                plan.add("vector", [(dG, g_done[kk]), (sV, sV.n)],
                         (lambda c=c, kk=kk: NCV.scalar_tensor_tensor(
                             zbuf[0], gbuf[kk % 2][:, :],
                             weff[:, kk:kk + 1], zbuf[0],
                             ALU.mult, ALU.add)),
                         (sV, sV.inc()))
            comb_tok[kk] = sV.n
        plan.add("vector", [(sV, sV.n), (dS, ld_tok)],
                 (lambda: NCV.scalar_tensor_tensor(
                     zfin, ybev[:, 0:2048], 1.0, zbuf[0],
                     ALU.mult, ALU.add)),
                 (sV, sV.inc()))
        plan.add("vector", [(sV, sV.n)],
                 (lambda: NCV.scalar_tensor_tensor(
                     zbuf[0], xcast[0], 1.0, zfin,
                     ALU.mult, ALU.add, accum_out=stat[:, 0:1])),
                 (sV, sV.inc()))
        z2_tok = sV.n
        plan.add("scalar", [(sV, z2_tok)],
                 (lambda: NCS.activation(sq2[:, 0:2048], zbuf[0], AF.Square,
                                         accum_out=stat[:, 1:2])),
                 (sA, sA.inc()))
        plan.add("vector", [(sA, sA.n)],
                 (lambda: NCV.tensor_scalar(stat[:, 2:3], stat[:, 0:1],
                                            1.0 / D, 0.0, ALU.mult, ALU.add)),
                 (sV, sV.inc()))
        plan.add("vector", [(sV, sV.n)],
                 (lambda: NCV.tensor_scalar(stat[:, 3:4], stat[:, 1:2],
                                            1.0 / D, 0.0, ALU.mult, ALU.add)),
                 (sV, sV.inc()))
        plan.add("vector", [(sV, sV.n)],
                 (lambda: NCV.tensor_mul(stat[:, 4:5], stat[:, 2:3],
                                         stat[:, 2:3])),
                 (sV, sV.inc()))
        plan.add("vector", [(sV, sV.n)],
                 (lambda: NCV.tensor_sub(stat[:, 5:6], stat[:, 3:4],
                                         stat[:, 4:5])),
                 (sV, sV.inc()))
        plan.add("vector", [(sV, sV.n)],
                 (lambda: NCV.tensor_scalar(stat[:, 6:7], stat[:, 5:6],
                                            1.0, EPS, ALU.mult, ALU.add)),
                 (sV, sV.inc()))
        plan.add("scalar", [(sV, sV.n)],
                 (lambda: NCS.activation(stat[:, 7:8], stat[:, 6:7], AF.Ln)),
                 (sA, sA.inc()))
        plan.add("scalar", [(sA, sA.n)],
                 (lambda: NCS.activation(stat[:, 8:9], stat[:, 7:8], AF.Exp,
                                         scale=-0.5)),
                 (sA, sA.inc()))
        plan.add("vector", [(sA, sA.n)],
                 (lambda: NCV.tensor_scalar(zfin, zbuf[0], stat[:, 2:3],
                                            stat[:, 8:9],
                                            ALU.subtract, ALU.mult)),
                 (sV, sV.inc()))
        fin_tok = sV.n
        plan.add("sync", [(sV, fin_tok)],
                 (lambda c=c: NCY.dma_start(
                     out=out_slice[c * 128:(c + 1) * 128, :], in_=zfin)),
                 (dS, dS.inc()))
        out_tok = dS.n
        plan.wait("vector", [(dS, out_tok)]) if c == 7 else None
    plan.wait("gpsimd", [(dS, out_tok)])

    _finalize(nc, plan, sems, stack)
    return nc


def _finalize(nc, plan, sems, stack):
    for s in sems:
        s.h = stack.enter_context(nc.semaphore(s.name))
    with nc.Block() as block:
        @block.tensor
        def _(eng):
            for f in plan.ops["tensor"]:
                f()

        @block.scalar
        def _(eng):
            for f in plan.ops["scalar"]:
                f()

        @block.vector
        def _(eng):
            for f in plan.ops["vector"]:
                f()

        @block.gpsimd
        def _(eng):
            for f in plan.ops["gpsimd"]:
                f()

        @block.sync
        def _(eng):
            for f in plan.ops["sync"]:
                f()
    stack.close()


def _stage(inputs):
    import ml_dtypes
    x = np.asarray(inputs["L0_attn_x"]).reshape(T, D)
    Wq = np.asarray(inputs["L0_attn_Wq"])
    Wk = np.asarray(inputs["L0_attn_Wk"])
    Wv = np.asarray(inputs["L0_attn_Wv"])
    Wo = np.asarray(inputs["L0_attn_Wo"])
    bo = np.asarray(inputs["L0_attn_bo"])
    gw = np.asarray(inputs["L0_ffn_gate_w"])
    gb = np.asarray(inputs["L0_ffn_bias"])
    wg = np.asarray(inputs["L0_ffn_w_gate"])
    wu = np.asarray(inputs["L0_ffn_w_up"])
    wd = np.asarray(inputs["L0_ffn_w_down"])
    sg = np.asarray(inputs["L0_ffn_sg"])
    su = np.asarray(inputs["L0_ffn_su"])
    sd = np.asarray(inputs["L0_ffn_sd"])

    C = np.ascontiguousarray
    mk = np.zeros((4, 128, 512), np.float32)
    for j in range(4):
        kk = 128 * j + np.arange(128)[:, None]
        qq = np.arange(512)[None, :]
        mk[j] = (kk <= qq).astype(np.float32)
    masks = mk.astype(ml_dtypes.bfloat16)
    ident = np.eye(128, dtype=ml_dtypes.bfloat16)
    lsta = np.zeros((2, 128, 64), np.float32)
    lsta[0, :, 0] = 1.0
    lsta[1, :, 32] = 1.0
    lstaf = lsta.astype(np.float32)
    lsta = lsta.astype(ml_dtypes.bfloat16)
    ones1f = np.ones((64, 128), np.float32)
    iota32 = np.broadcast_to(np.arange(E, dtype=np.float32), (128, E)).copy()
    bo_bc = np.broadcast_to(bo.astype(np.float32), (128, D)).copy()
    gb_bc = np.broadcast_to(gb.astype(np.float32), (128, E)).copy()
    gwT = C(np.asarray(gw, np.float32).T)

    maps = []
    for r in range(NCORES):
        hs = slice(r * HPC * HD, (r + 1) * HPC * HD)
        es = slice(r * EPC, (r + 1) * EPC)
        maps.append({
            "x_slice": C(x[r * TPC:(r + 1) * TPC]),
            "wq": C(Wq[:, hs]), "wk": C(Wk[:, hs]), "wv": C(Wv[:, hs]),
            "wo": C(Wo),
            "bo_bc": bo_bc,
            "gwT": gwT,
            "gbias_bc": gb_bc,
            "wgT": C(wg[es].transpose(0, 2, 1)),
            "wuT": C(wu[es].transpose(0, 2, 1)),
            "wdT": C(wd[es].transpose(0, 2, 1)),
            "sgT": C(sg.T), "suT": C(su.T), "sdT": C(sd.T),
            "masks": masks,
            "iota32_bc": iota32,
            "ident": ident,
            "lsta": lsta,
            "lstaf": lstaf,
            "ones1f": ones1f,
            "rk4": np.full((128, 1), 4.0 * r, np.float32),
            "qoff": np.full((1, 1), (r // 2) * 2048 * 2048 + (r % 2) * 1024,
                            np.uint32),
            "aoff": np.full((1, 1), r * TPC * K, np.uint32),
            "iotacol": np.arange(E, dtype=np.float32).reshape(E, 1),
        })
    return maps


def kernel(**inputs):
    maps = _stage(inputs)
    nc = _build(stop_after=None)
    res = run_bass_kernel_spmd(nc, maps, core_ids=list(range(NCORES)))
    out = np.concatenate([res.results[r]["out_slice"] for r in range(NCORES)], 0)
    return out.reshape(1, T, D).astype(np.float32, copy=False)


# version 03
# speedup vs baseline: 1.0658x; 1.0658x over previous
import numpy as np

import concourse.bass as bass
import concourse.mybir as mybir
from concourse.bass_utils import run_bass_kernel_spmd
from concourse.bass_core_rust import IndirectOffsetOnAxis

F32 = mybir.dt.float32
BF16 = mybir.dt.bfloat16
U32 = mybir.dt.uint32
AF = mybir.ActivationFunctionType
ALU = mybir.AluOpType
AX = mybir.AxisListType

NCORES = 8
T = 8192
D = 2048
NH = 16
HD = 128
HPC = NH // NCORES
TPC = T // NCORES
E = 32
K = 6
CAP = 768
EPC = E // NCORES
II = 1024
MSCALE = 1.5
SWIGLU_LIMIT = 10.0
EPS = 1e-5
ATT_SCALE = 1.0 / float(np.sqrt(HD))
EXP_SHIFT = -12.0
NA = T * K

ENGS = ("tensor", "scalar", "vector", "gpsimd", "sync")


class _Sem:
    def __init__(self, name, step=1):
        self.name, self.step, self.n, self.h = name, step, 0, None

    def inc(self):
        self.n += self.step
        return self.n


class _Plan:
    def __init__(self, nc):
        self.ops = {e: [] for e in ENGS}
        self.eng = {"tensor": nc.tensor, "scalar": nc.scalar,
                    "vector": nc.vector, "gpsimd": nc.gpsimd, "sync": nc.sync}

    def add(self, e, waits, mk, inc=None):
        eng = self.eng[e]
        waits = [(s, v) for (s, v) in waits if v > 0]

        def f():
            for (s, v) in waits:
                eng.wait_ge(s.h, v)
            ins = mk()
            if inc is not None:
                ins.then_inc(inc[0].h, inc[0].step)
        self.ops[e].append(f)

    def wait(self, e, waits):
        eng = self.eng[e]
        waits = [(s, v) for (s, v) in waits if v > 0]

        def f():
            for (s, v) in waits:
                eng.wait_ge(s.h, v)
        self.ops[e].append(f)


def _build(stop_after=None):
    nc = bass.Bass("TRN2", target_bir_lowering=False)
    NCT, NCS, NCV, NCP, NCY = nc.tensor, nc.scalar, nc.vector, nc.gpsimd, nc.sync

    def din(name, shape, dt=F32):
        return nc.dram_tensor(name, shape, dt, kind="ExternalInput").ap()

    x_slice = din("x_slice", [TPC, D])
    wq = din("wq", [D, HPC * HD])
    wk = din("wk", [D, HPC * HD])
    wv = din("wv", [D, HPC * HD])
    wo = din("wo", [D, D])
    bo_bc = din("bo_bc", [128, D])
    gwT = din("gwT", [D, E])
    gbias_bc = din("gbias_bc", [128, E])
    wgT = din("wgT", [EPC, D, II])
    wuT = din("wuT", [EPC, D, II])
    wdT = din("wdT", [EPC, II, D])
    sgT = din("sgT", [D, II])
    suT = din("suT", [D, II])
    sdT = din("sdT", [II, D])
    masks = din("masks", [4, 128, 512], BF16)
    iota32_bc = din("iota32_bc", [128, E])
    ident = din("ident", [128, 128], BF16)
    lsta = din("lsta", [2, 128, 64], BF16)
    lstaf = din("lstaf", [2, 128, 64])
    ones1f = din("ones1f", [64, 128])
    rk4 = din("rk4", [128, 1])
    qoff = din("qoff", [1, 1], U32)
    aoff = din("aoff", [1, 1], U32)
    iotacol = din("iotacol", [32, 1])
    out_slice = nc.dram_tensor("out_slice", [TPC, D], F32,
                               kind="ExternalOutput").ap()

    def dint(name, shape, dt):
        return nc.dram_tensor(name, shape, dt).ap()

    xbf_loc = dint("xbf_loc", [TPC, D], BF16)
    xbf_full = dint("xbf_full", [T, D], BF16)
    vt_dram = dint("vt_dram", [HPC, 128, T], BF16)
    ctx_locj = [dint(f"ctx_loc{j}", [HPC * HD, 2048], BF16) for j in range(4)]
    ctxfull_h = nc.dram_tensor("ctxfull", [T, D], BF16)
    ctxfull = ctxfull_h.ap()
    h1_dram = dint("h1_dram", [TPC, D], F32)
    h1loc = dint("h1loc", [TPC, D], BF16)
    h1full = dint("h1full", [T, D], BF16)
    idxloc = dint("idxloc", [TPC, K], F32)
    idxfull_h = nc.dram_tensor("idxfull", [T, K], F32)
    idxfull = idxfull_h.ap()
    posflat_h = nc.dram_tensor("posflat", [1, NA], F32)
    posflat = posflat_h.ap()
    shared_dram = dint("shared_dram", [TPC, D], BF16)
    xb_disp = dint("xb_disp", [EPC * CAP, D], BF16)
    yb_loc = dint("yb_loc", [EPC * CAP, D], BF16)
    ybfull = dint("ybfull", [E * CAP, D], BF16)

    plan = _Plan(nc)
    regs = {}

    def _alloc_regs():
        regs["disp"] = NCP.alloc_register("rb_disp")
        NCP.reg_mov(regs["disp"], EPC * CAP - 1)
        regs["ret"] = NCP.alloc_register("rb_ret")
        NCP.reg_mov(regs["ret"], E * CAP - 1)
    plan.ops["gpsimd"].append(_alloc_regs)

    sT = _Sem("sT")
    sA = _Sem("sA")
    sV = _Sem("sV")
    sP = _Sem("sP")
    dS = _Sem("dS", 16)
    dG = _Sem("dG", 16)
    dA = _Sem("dA", 16)
    sC = _Sem("sC")
    sems = [sT, sA, sV, sP, dS, dG, dA, sC]

    from contextlib import ExitStack
    stack = ExitStack()

    def SB(name, shape, dt):
        return stack.enter_context(nc.sbuf_tensor(name, shape, dt))

    def PS(name, shape, dt):
        return stack.enter_context(nc.psum_tensor(name, shape, dt))

    ppA = PS("ppA", [128, 1024], F32)
    ppB = PS("ppB", [128, 1024], F32)
    ppC = PS("ppC", [128, 512], F32)
    ppD = PS("ppD", [128, 512], F32)
    ppE = PS("ppE", [64, 512], F32)
    ppT = PS("ppT", [128, 1024], BF16)
    ppDbf = ppD[:, :].bitcast(BF16)

    def _tslot(i):
        bank = ppT if i % 2 == 0 else ppDbf
        sl = (i // 2) % 4
        return bank[:, sl * 128:(sl + 1) * 128]

    AR = SB("AR", [128, 49152], BF16)
    AF32 = SB("AF32", [128, 5 * D], F32)
    xTh = [SB(f"xTh{i}", [128, 8 * 512], BF16) for i in range(2)]
    wqb = SB("wqb", [128, 16 * 256], BF16)
    wkb = SB("wkb", [128, 16 * 256], BF16)
    wvb = SB("wvb", [128, 16 * 256], BF16)
    PTb = [SB(f"PT{i}", [128, 1024], BF16) for i in range(2)]
    PTraw = SB("PTraw", [128, 512], BF16)
    maskb = SB("maskb", [128, 4 * 512], BF16)
    expb = SB("expb", [128, 1], F32)
    lstab = SB("lstab", [128, 2 * 64], BF16)
    lstafb = SB("lstafb", [128, 2 * 64], F32)
    linv = SB("linv", [64, 512], F32)
    lbcs = SB("lbcs", [128, 512], F32)
    ctxst = SB("ctxst", [128, 1024], BF16)
    onesf = SB("onesf", [64, 128], F32)
    bob = SB("bob", [128, D], BF16)
    stat = SB("stat", [128, 16], F32)
    smal = SB("smal", [128, 512], F32)
    idxu = SB("idxu", [128, 512], U32)
    identb = SB("identb", [128, 128], BF16)
    iotacolb = SB("iotacolb", [32, 1], F32)
    iob = SB("iob", [128, 32], F32)
    gbb = SB("gbb", [128, 32], F32)
    rk4b = SB("rk4b", [128, 1], F32)
    aoffs = SB("aoffs", [1, 1], U32)
    arow = SB("arow", [1, 2 * 512], F32)
    posrow = SB("posrow", [1, 512], F32)
    routb = SB("routb", [128, 48], F32)
    routw = SB("routw", [128, 48], F32)
    weff = SB("weff", [128, 48], F32)
    topv = SB("topv", [128, 8], F32)
    idx8u = SB("idx8u", [128, 8], U32)
    idx8f = SB("idx8f", [128, 8], F32)
    carry32 = SB("carry32", [32, 1], F32)
    z32 = SB("z32", [32, 1], F32)

    lacc = [lbcs[:, :], smal[:, :]]
    QT = [AR[:, 0:8192], AR[:, 8192:16384]]
    KT = [AR[:, 16384:24576], AR[:, 24576:32768]]
    VT = [AR[:, 32768:40960], AR[:, 40960:49152]]
    Vn = VT
    wob = AR[:, 0:32768]
    ctxTmy = AR[:, 32768:49152]

    xcast = [AF32[:, 0:D], AF32[:, D:2 * D]]
    zbuf = [AF32[:, 2 * D:3 * D], AF32[:, 3 * D:4 * D]]
    sqscr = AF32[:, 4 * D:5 * D]
    xcb = [xTh[0][:, 0:2048], xTh[1][:, 0:2048]]
    h1hi = [xTh[0][:, 0:2048], xTh[0][:, 2048:4096],
            xTh[1][:, 0:2048], xTh[1][:, 2048:4096],
            wqb[:, 0:2048], wqb[:, 2048:4096],
            wkb[:, 0:2048], wkb[:, 2048:4096]]

    store_tok = {}
    for c in range(8):
        buf = c % 2
        plan.add("sync", [(sV, sV.n)] if c >= 2 else [],
                 (lambda c=c, buf=buf: NCY.dma_start(
                     out=xcast[buf], in_=x_slice[c * 128:(c + 1) * 128, :])),
                 (dS, dS.inc()))
        plan.add("vector",
                 [(dS, dS.n)] + ([(dG, store_tok[c - 2])] if c >= 2 else []),
                 (lambda buf=buf: NCV.tensor_copy(xcb[buf], xcast[buf])),
                 (sV, sV.inc()))
        plan.add("gpsimd", [(sV, sV.n)],
                 (lambda c=c, buf=buf: NCP.dma_start(
                     out=xbf_loc[c * 128:(c + 1) * 128, :], in_=xcb[buf])),
                 (dG, dG.inc()))
        store_tok[c] = dG.n
    plan.add("gpsimd", [(dG, dG.n)],
             (lambda: NCP.collective_compute(
                 "AllGather", ALU.bypass, replica_groups=[list(range(NCORES))],
                 ins=[xbf_loc], outs=[xbf_full])),
             (sC, sC.inc()))
    xag_done = sC.n

    plan.add("vector", [], lambda: NCV.memset(expb[:, :], EXP_SHIFT),
             (sV, sV.inc()))
    plan.add("sync", [], lambda: NCY.dma_start(
        out=maskb[:, :].rearrange("p (j f) -> p j f", j=4),
        in_=masks.rearrange("j p f -> p j f")),
        (dS, dS.inc()))
    plan.add("sync", [], lambda: NCY.dma_start(
        out=lstab[:, :].rearrange("p (j f) -> p j f", j=2),
        in_=lsta.rearrange("j p f -> p j f")),
        (dS, dS.inc()))
    plan.add("sync", [], lambda: NCY.dma_start(
        out=lstafb[:, :].rearrange("p (j f) -> p j f", j=2),
        in_=lstaf.rearrange("j p f -> p j f")),
        (dS, dS.inc()))
    plan.add("sync", [], lambda: NCY.dma_start(out=onesf[:, :], in_=ones1f),
             (dS, dS.inc()))
    plan.add("gpsimd", [], lambda: NCP.dma_start(out=bob[:, :], in_=bo_bc),
             (dG, dG.inc()))
    bob_done = dG.n
    consts_done = dS.n
    for (wsrc, wdst) in ((wq, wqb), (wk, wkb), (wv, wvb)):
        plan.add("gpsimd", [],
                 (lambda wsrc=wsrc, wdst=wdst: NCP.dma_start(
                     out=wdst[:, :].rearrange("p (c m) -> p c m", c=16),
                     in_=wsrc.rearrange("(c p) m -> p c m", p=128))),
                 (dG, dG.inc()))
    wcast_done = dG.n

    pd_list = [(wqb, 0, QT), (wqb, 1, QT), (wkb, 0, KT), (wkb, 1, KT),
               (wvb, 0, VT), (wvb, 1, VT)]
    pd_ps = [ppA[:, 0:512], ppA[:, 512:1024], ppB[:, 0:512],
             ppB[:, 512:1024], ppC[:, :], ppD[:, :]]
    half_mm_tok = {}
    evac_tok_p2 = 0
    vt_store = 0
    vn_tok = {}
    for tw in range(16):
        t0 = tw * 512
        for hc in range(2):
            hidx = tw * 2 + hc
            hb = hidx % 2
            w = [(sC, xag_done)] if hidx == 0 else []
            if hidx >= 2:
                w.append((sT, half_mm_tok[hidx - 2]))
            for chl in range(8):
                ch = hc * 8 + chl
                plan.add("sync", w if chl == 0 else [],
                         (lambda t0=t0, ch=ch, hb=hb, chl=chl: NCY.dma_start(
                             out=xTh[hb][:, chl * 512:(chl + 1) * 512],
                             in_=xbf_full[t0:t0 + 512,
                                          ch * 128:(ch + 1) * 128],
                             transpose=True)),
                         (dS, dS.inc()))
            xh_tok = dS.n
            for pdi, (wsb, dt_i, dstT) in enumerate(pd_list):
                for chl in range(8):
                    ch = hc * 8 + chl
                    wts = []
                    if pdi == 0 and chl == 0:
                        wts = [(dS, xh_tok), (dG, wcast_done)]
                        if hc == 0 and evac_tok_p2:
                            wts.append((sV, evac_tok_p2))
                    last = (pdi == 5 and chl == 7)
                    plan.add("tensor", wts,
                             (lambda wsb=wsb, dt_i=dt_i, ch=ch, chl=chl, hb=hb,
                                     pdi=pdi: NCT.matmul(
                                 pd_ps[pdi],
                                 wsb[:, ch * 256 + dt_i * 128:
                                     ch * 256 + (dt_i + 1) * 128],
                                 xTh[hb][:, chl * 512:(chl + 1) * 512],
                                 start=(ch == 0), stop=(ch == 15))),
                             (sT, sT.inc()) if last else None)
            half_mm_tok[hidx] = sT.n
        mm_tok = sT.n
        for pdi, (wsb, dt_i, dstT) in enumerate(pd_list):
            plan.add("vector", [(sT, mm_tok)] if pdi == 0 else [],
                     (lambda pdi=pdi, dt_i=dt_i, dstT=dstT, t0=t0:
                      NCV.tensor_copy(dstT[dt_i][:, t0:t0 + 512], pd_ps[pdi])),
                     (sV, sV.inc()))
        evac_tok_p2 = sV.n
        for dt_i in range(2):
            plan.add("gpsimd", [(sV, evac_tok_p2)] if dt_i == 0 else [],
                     (lambda dt_i=dt_i, t0=t0: NCP.dma_start(
                         out=vt_dram[dt_i, :, t0:t0 + 512],
                         in_=VT[dt_i][:, t0:t0 + 512])),
                     (dG, dG.inc()))
        vt_store = dG.n
        for h in range(HPC):
            for kb in range(4 * tw, 4 * tw + 4):
                plan.add("scalar", [(dG, vt_store)] if (h == 0 and kb == 4 * tw)
                         else [],
                         (lambda h=h, kb=kb: NCS.dma_start(
                             out=Vn[h][:, kb * 128:(kb + 1) * 128],
                             in_=vt_dram[h, :, kb * 128:(kb + 1) * 128],
                             transpose=True)),
                         (dA, dA.inc()))
                vn_tok[(h, kb)] = dA.n
    qkv_V = sV.n
    vn_ready = dA.n

    last_ctx_dma = 0
    for s in range(8):
        for h in range(HPC):
            nkb = 8 * (s + 1)
            exp_tok = {}
            mask_tok = {}
            sS_tok = {}
            lacc_tok = {}
            ctx_started = [False, False]
            l_started = [False]
            first_of_stripe = True
            plan.add("vector", [(sV, sV.n)],
                     (lambda: NCV.memset(lacc[0], 0.0)), (sV, sV.inc()))
            plan.add("vector", [(sV, sV.n)],
                     (lambda: NCV.memset(lacc[1], 0.0)), (sV, sV.inc()))

            def active(kb, w, s=s):
                return kb <= 8 * s + 4 * w + 3

            for kb in range(nkb + 1):
                par = kb % 2
                sp = ppA if par == 0 else ppB
                if kb < nkb:
                    wts = []
                    if first_of_stripe:
                        wts = [(dA, vn_tok[(h, nkb - 1)]), (sA, sA.n),
                               (sV, sV.n)]
                        first_of_stripe = False
                    elif kb >= 2:
                        wts = [(sA, exp_tok[kb - 2])]
                    acts = [w for w in range(2) if active(kb, w)]
                    for j, w in enumerate(acts):
                        plan.add("tensor", wts if j == 0 else [],
                                 (lambda h=h, kb=kb, w=w, sp=sp, s=s:
                                  NCT.matmul(
                                     sp[:, w * 512:(w + 1) * 512],
                                     KT[h][:, kb * 128:(kb + 1) * 128],
                                     QT[h][:, 1024 * s + w * 512:
                                            1024 * s + (w + 1) * 512],
                                     start=True, stop=True)),
                                 (sT, sT.inc()) if j == len(acts) - 1 else None)
                    sS_tok[kb] = sT.n

                    sw = None
                    for w in acts:
                        if 0 <= kb - 8 * s - 4 * w <= 3:
                            sw = w
                    dws = [w for w in acts if w != sw]
                    swait = [(sT, sS_tok[kb])]
                    if (kb - 1) in mask_tok:
                        swait.append((sV, mask_tok[kb - 1]))
                    if (kb - 2) in lacc_tok:
                        swait.append((sV, lacc_tok[kb - 2]))
                    if dws:
                        lo, hi = min(dws), max(dws)
                        plan.add("scalar", swait,
                                 (lambda par=par, lo=lo, hi=hi, sp=sp:
                                  NCS.activation(
                                      PTb[par][:, lo * 512:(hi + 1) * 512],
                                      sp[:, lo * 512:(hi + 1) * 512],
                                      AF.Exp, bias=expb[:, 0:1],
                                      scale=ATT_SCALE)),
                                 (sA, sA.inc()) if sw is None else None)
                        swait = []
                    if sw is not None:
                        plan.add("scalar", swait,
                                 (lambda par=par, sw=sw, sp=sp: NCS.activation(
                                     PTraw[:, :],
                                     sp[:, sw * 512:(sw + 1) * 512],
                                     AF.Exp, bias=expb[:, 0:1],
                                     scale=ATT_SCALE)),
                                 (sA, sA.inc()))
                        j = kb - 8 * s - 4 * sw
                        plan.add("vector", [(sA, sA.n)],
                                 (lambda par=par, sw=sw, j=j: NCV.tensor_mul(
                                     PTb[par][:, sw * 512:(sw + 1) * 512],
                                     PTraw[:, :],
                                     maskb[:, j * 512:(j + 1) * 512])),
                                 (sV, sV.inc()))
                        mask_tok[kb] = sV.n
                    exp_tok[kb] = sA.n

                if kb >= 1:
                    kprev = kb - 1
                    parp = kprev % 2
                    actsp = [w for w in range(2) if active(kprev, w)]
                    wts = [(sA, exp_tok[kprev])]
                    if kprev in mask_tok:
                        wts.append((sV, mask_tok[kprev]))
                    first = True
                    for w in actsp:
                        ctx_ps = ppC if w == 0 else ppD
                        is_last_ctx = (kprev == 8 * s + 4 * w + 3)
                        plan.add("tensor", wts if first else [],
                                 (lambda h=h, kprev=kprev, w=w, parp=parp,
                                         ctx_ps=ctx_ps, st=not ctx_started[w],
                                         sp_=is_last_ctx: NCT.matmul(
                                     ctx_ps[:, :],
                                     Vn[h][:, kprev * 128:(kprev + 1) * 128],
                                     PTb[parp][:, w * 512:(w + 1) * 512],
                                     start=st, stop=sp_)),
                                 (sT, sT.inc()) if w == actsp[-1] else None)
                        first = False
                        ctx_started[w] = True
                        if kprev % 2 == 0:
                            vw = [(sA, exp_tok[kprev]), (sV, sV.n)]
                            if kprev in mask_tok:
                                vw.append((sV, mask_tok[kprev]))
                            plan.add("vector", vw,
                                     (lambda w=w, parp=parp: NCV.tensor_add(
                                         lacc[w], lacc[w],
                                         PTb[parp][:, w * 512:(w + 1) * 512])),
                                     (sV, sV.inc()))
                            lacc_tok[kprev] = sV.n
                        else:
                            plan.add("tensor", [],
                                     (lambda kprev=kprev, w=w, parp=parp,
                                             st=not l_started[0]:
                                      NCT.matmul(
                                          ppE[:, :],
                                          lstab[:, w * 64:(w + 1) * 64],
                                          PTb[parp][:, w * 512:(w + 1) * 512],
                                          start=st, stop=False)))
                            l_started[0] = True
            flash_T = sT.n
            lacc_last = sV.n
            for w in range(2):
                plan.add("tensor", [(sV, lacc_last)] if w == 0 else [],
                         (lambda w=w, st=(w == 0 and not l_started[0]):
                          NCT.matmul(
                             ppE[:, :], lstafb[:, w * 64:(w + 1) * 64],
                             lacc[w], start=st, stop=(w == 1))),
                         (sT, sT.inc()))
            flash_T = sT.n

            plan.add("vector", [(sT, flash_T), (dG, last_ctx_dma)],
                     (lambda: NCV.reciprocal(linv[:, :], ppE[0:64, :])),
                     (sV, sV.inc()))
            for w in range(2):
                plan.add("tensor", [(sV, sV.n)],
                         (lambda w=w: NCT.matmul(
                             ppA[:, 0:512], onesf[32 * w:32 * w + 1, :],
                             linv[32 * w:32 * w + 1, :],
                             start=True, stop=True)),
                         (sT, sT.inc()))
                plan.add("vector", [(sT, sT.n)],
                         (lambda: NCV.tensor_copy(lbcs[:, :], ppA[:, 0:512])),
                         (sV, sV.inc()))
                ctx_ps = ppC if w == 0 else ppD
                plan.add("vector", [(sV, sV.n)],
                         (lambda w=w, ctx_ps=ctx_ps: NCV.tensor_mul(
                             ctxst[:, w * 512:(w + 1) * 512],
                             ctx_ps[:, :], lbcs[:, :])),
                         (sV, sV.inc()))
            evac_tok = sV.n
            plan.add("gpsimd", [(sV, evac_tok)],
                     (lambda s=s, h=h: NCP.dma_start(
                         out=ctx_locj[s // 2][h * 128:(h + 1) * 128,
                                             (s % 2) * 1024:
                                             (s % 2) * 1024 + 1024],
                         in_=ctxst[:, :])),
                     (dG, dG.inc()))
            last_ctx_dma = dG.n
            plan.wait("tensor", [(sV, evac_tok)])
        if s % 2 == 1:
            j = s // 2
            plan.add("gpsimd", [(dG, last_ctx_dma)],
                     (lambda j=j: NCP.collective_compute(
                         "AllGather", ALU.bypass,
                         replica_groups=[list(range(NCORES))],
                         ins=[ctx_locj[j]],
                         outs=[ctxfull[j * 2048:(j + 1) * 2048, :]])),
                     (sC, sC.inc()))
    ctx_stored = dG.n
    ctxag_done = sC.n

    qoffs = SB("qoffs", [1, 1], U32)
    plan.add("sync", [], lambda: NCY.dma_start(out=qoffs[:, :], in_=qoff),
             (dS, dS.inc()))
    qoff_tok = dS.n

    def _ld_ctxTmy():
        NCP.wait_ge(dS.h, qoff_tok)
        NCP.wait_ge(sC.h, ctxag_done)
        with NCP.register("rq") as rq:
            NCP.reg_load(rq, qoffs[0:1, 0:1])
            return NCP.dma_start(
                out=ctxTmy.rearrange("p (c m) -> p c m", c=16),
                in_=bass.AP(ctxfull_h, rq,
                            [[2048, 128], [2048 * 128, 16], [1, 1024]]))
    plan.add("gpsimd", [], _ld_ctxTmy, (dG, dG.inc()))
    ctxmy_ready = dG.n
    plan.add("gpsimd", [(dG, ctx_stored)],
             (lambda: NCP.dma_start(
                 out=wob.rearrange("p (c m) -> p c m", c=16),
                 in_=wo.rearrange("(c p) m -> p c m", p=128))),
             (dG, dG.inc()))
    wo_ready = dG.n

    h1hi_tok = [0] * 8
    h1_spill = 0
    for c in range(8):
        buf = c % 2
        plan.add("sync", [],
                 (lambda c=c, buf=buf: NCY.dma_start(
                     out=xcast[buf], in_=x_slice[c * 128:(c + 1) * 128, :])),
                 (dS, dS.inc()))
        xrow_tok = dS.n
        wts = [(dG, max(ctxmy_ready, 0)), (dG, wo_ready)]
        if c == 0:
            wts += [(sA, sA.n), (sV, sV.n)]
        for ch in range(16):
            for nhalf in range(2):
                ps = ppA if nhalf == 0 else ppB
                for nn in range(2):
                    plan.add("tensor",
                             wts if (ch == 0 and nhalf == 0 and nn == 0)
                             else [],
                             (lambda c=c, ch=ch, nhalf=nhalf, nn=nn, ps=ps:
                              NCT.matmul(
                                  ps[:, nn * 512:(nn + 1) * 512],
                                  ctxTmy[:, ch * 1024 + c * 128:
                                         ch * 1024 + (c + 1) * 128],
                                  wob[:, ch * 2048 + nhalf * 1024 + nn * 512:
                                      ch * 2048 + nhalf * 1024 +
                                      (nn + 1) * 512],
                                  start=(ch == 0), stop=(ch == 15))),
                             (sT, sT.inc()) if (ch == 15 and nhalf == 1 and
                                                nn == 1) else None)
            wts = []
        att_tok = sT.n
        for nhalf in range(2):
            ps = ppA if nhalf == 0 else ppB
            plan.add("vector",
                     [(sT, att_tok), (dS, xrow_tok)] +
                     ([(dS, consts_done)] if c == 0 else []),
                     (lambda buf=buf, nhalf=nhalf, ps=ps:
                      NCV.scalar_tensor_tensor(
                          zbuf[buf][:, nhalf * 1024:(nhalf + 1) * 1024],
                          ps[:, :], 1.0,
                          xcast[buf][:, nhalf * 1024:(nhalf + 1) * 1024],
                          ALU.mult, ALU.add)),
                     (sV, sV.inc()))
        plan.add("vector", [(sV, sV.n)] + ([(dG, bob_done)] if c == 0 else []),
                 (lambda buf=buf: NCV.scalar_tensor_tensor(
                     zbuf[buf], zbuf[buf], 1.0, bob[:, :],
                     ALU.mult, ALU.add, accum_out=stat[:, 0:1])),
                 (sV, sV.inc()))
        z_tok = sV.n
        plan.wait("tensor", [(sV, z_tok)])
        plan.add("scalar", [(sV, z_tok)],
                 (lambda buf=buf: NCS.activation(
                     sqscr, zbuf[buf], AF.Square, accum_out=stat[:, 1:2])),
                 (sA, sA.inc()))
        sq_tok = sA.n
        plan.add("vector", [(sA, sq_tok)],
                 (lambda: NCV.tensor_scalar(stat[:, 2:3], stat[:, 0:1],
                                            1.0 / D, 0.0, ALU.mult, ALU.add)),
                 (sV, sV.inc()))
        plan.add("vector", [(sV, sV.n)],
                 (lambda: NCV.tensor_scalar(stat[:, 3:4], stat[:, 1:2],
                                            1.0 / D, 0.0, ALU.mult, ALU.add)),
                 (sV, sV.inc()))
        plan.add("vector", [(sV, sV.n)],
                 (lambda: NCV.tensor_mul(stat[:, 4:5], stat[:, 2:3],
                                         stat[:, 2:3])),
                 (sV, sV.inc()))
        plan.add("vector", [(sV, sV.n)],
                 (lambda: NCV.tensor_sub(stat[:, 5:6], stat[:, 3:4],
                                         stat[:, 4:5])),
                 (sV, sV.inc()))
        plan.add("vector", [(sV, sV.n)],
                 (lambda: NCV.tensor_scalar(stat[:, 6:7], stat[:, 5:6],
                                            1.0, EPS, ALU.mult, ALU.add)),
                 (sV, sV.inc()))
        var_tok = sV.n
        plan.add("scalar", [(sV, var_tok)],
                 (lambda: NCS.activation(stat[:, 7:8], stat[:, 6:7], AF.Ln)),
                 (sA, sA.inc()))
        plan.add("scalar", [(sA, sA.n)],
                 (lambda: NCS.activation(stat[:, 8:9], stat[:, 7:8], AF.Exp,
                                         scale=-0.5)),
                 (sA, sA.inc()))
        rstd_tok = sA.n
        plan.add("vector", [(sA, rstd_tok)],
                 (lambda buf=buf: NCV.tensor_scalar(
                     zbuf[buf], zbuf[buf], stat[:, 2:3], stat[:, 8:9],
                     ALU.subtract, ALU.mult)),
                 (sV, sV.inc()))
        h1_tok = sV.n
        plan.add("vector", [(sV, h1_tok)],
                 (lambda c=c, buf=buf: NCV.tensor_copy(h1hi[c], zbuf[buf])),
                 (sV, sV.inc()))
        h1hi_tok[c] = sV.n
        plan.add("sync", [(sV, sV.n)],
                 (lambda c=c, buf=buf: NCY.dma_start(
                     out=h1_dram[c * 128:(c + 1) * 128, :], in_=zbuf[buf])),
                 (dS, dS.inc()))
        h1_spill = dS.n
    h1_V = sV.n

    if stop_after == "h1":
        for c in range(8):
            plan.add("gpsimd", [(dS, h1_spill)] if c == 0 else [],
                     (lambda c=c: NCP.dma_start(
                         out=out_slice[c * 128:(c + 1) * 128, :],
                         in_=h1_dram[c * 128:(c + 1) * 128, :])),
                     (dG, dG.inc()))
        plan.wait("gpsimd", [(dG, dG.n)])
        _finalize(nc, plan, sems, stack)
        return nc

    h1T = AR[:, 0:16384]
    wstr = AR[:, 16384:32768]
    xbT = AR[:, 32768:45056]
    gsil = AR[:, 0:6144]
    intT = AR[:, 6144:12288]
    shg = AR[:, 32768:40960]
    siT = AR[:, 40960:49152]
    obit1 = AR[0:32, 16384:32768].bitcast(F32)
    obit2 = AR[0:32, 32768:49152].bitcast(F32)
    Obuf = obit1[:, 0:4096]
    prodb = obit1[:, 4096:8192]
    cumb = obit2[:, 0:4096]
    loT = maskb[:, 0:2048]
    lo_nat = [PTb[0], PTb[1]]
    gwhib = ctxst[:, 0:512]
    gwlob = ctxst[:, 512:1024]
    gbuf = [wvb[:, 0:2048], wvb[:, 2048:4096]]
    ybev = maskb[:, 0:2048]
    ebuf = AF32[:, 8192:8576]
    pbuf = AF32[:, 8576:8960]
    m3s1 = AF32[:, 8960:9344]
    m3s2 = AF32[:, 9344:9728]
    m3s3 = AF32[:, 9728:10112]
    sq2 = AF32[:, 8192:10240]
    pmy = lbcs[:, 0:48]
    vldm = lbcs[:, 48:96]
    t1v = lbcs[:, 96:144]
    e4f = lbcs[:, 144:192]
    rowf = lbcs[:, 192:240]
    qv = lbcs[:, 240:288]
    zfin = xcast[1]

    p4_T = sT.n
    for c in range(8):
        plan.add("gpsimd", [(sV, h1hi_tok[c])],
                 (lambda c=c: NCP.dma_start(
                     out=h1loc[c * 128:(c + 1) * 128, :], in_=h1hi[c])),
                 (dG, dG.inc()))
    plan.add("gpsimd", [(dG, dG.n)],
             (lambda: NCP.collective_compute(
                 "AllGather", ALU.bypass, replica_groups=[list(range(NCORES))],
                 ins=[h1loc], outs=[h1full])),
             (sC, sC.inc()))
    h1ag_done = sC.n

    if stop_after == "m0a":
        plan.add("gpsimd", [(sC, h1ag_done)],
                 (lambda: NCP.dma_start(out=out_slice[0:512, :],
                                        in_=h1full[0:512, :])),
                 (dG, dG.inc()))
        plan.wait("gpsimd", [(dG, dG.n)])
        _finalize(nc, plan, sems, stack)
        return nc

    plan.add("sync", [], lambda: NCY.dma_start(out=identb[:, :], in_=ident),
             (dS, dS.inc()))
    plan.add("sync", [], lambda: NCY.dma_start(out=iotacolb[:, :], in_=iotacol),
             (dS, dS.inc()))
    plan.add("sync", [], lambda: NCY.dma_start(out=iob[:, :], in_=iota32_bc),
             (dS, dS.inc()))
    plan.add("sync", [], lambda: NCY.dma_start(out=gbb[:, :], in_=gbias_bc),
             (dS, dS.inc()))
    plan.add("sync", [], lambda: NCY.dma_start(out=rk4b[:, :], in_=rk4),
             (dS, dS.inc()))
    plan.add("sync", [], lambda: NCY.dma_start(out=aoffs[:, :], in_=aoff),
             (dS, dS.inc()))
    mconsts = dS.n
    plan.add("sync", [], lambda: NCY.dma_start(
        out=smal[:, :], in_=gwT.rearrange("(c p) m -> p c m", p=128)),
        (dS, dS.inc()))
    gwf_tok = dS.n
    plan.add("vector", [(dS, gwf_tok), (sV, qkv_V)],
             (lambda: NCV.tensor_copy(gwhib, smal[:, :])), (sV, sV.inc()))
    plan.add("vector", [(sV, sV.n)],
             (lambda: NCV.tensor_sub(gwlob, smal[:, :], gwhib)),
             (sV, sV.inc()))
    gw_ready = sV.n

    if stop_after == "m0b":
        plan.add("sync", [(sV, gw_ready), (dS, mconsts)],
                 (lambda: NCY.dma_start(out=out_slice[0:128, 0:512],
                                        in_=smal[:, :])),
                 (dS, dS.inc()))
        plan.wait("gpsimd", [(dS, dS.n), (sC, h1ag_done)])
        _finalize(nc, plan, sems, stack)
        return nc

    tr_ev = {}
    for c in range(8):
        for ch in range(16):
            i = c * 16 + ch
            wts = [(sV, h1hi_tok[c]), (dS, mconsts)]
            if i >= 2:
                wts.append((sV, tr_ev[i - 2]))
            plan.add("tensor", wts,
                     (lambda c=c, ch=ch, i=i: NCT.transpose(
                         _tslot(i),
                         h1hi[c][:, ch * 128:(ch + 1) * 128], identb[:, :])),
                     (sT, sT.inc()))
            plan.add("vector", [(sT, sT.n)] + ([(sT, p4_T)] if i == 0 else []),
                     (lambda c=c, ch=ch, i=i: NCV.tensor_copy(
                         h1T[:, ch * 1024 + c * 128:ch * 1024 + (c + 1) * 128],
                         _tslot(i))),
                     (sV, sV.inc()))
            tr_ev[i] = sV.n
    h1T_done = sV.n

    if stop_after == "m0":
        plan.add("gpsimd", [(sV, h1T_done), (sC, h1ag_done)],
                 (lambda: NCP.dma_start(out=out_slice[0:128, :],
                                        in_=h1T[:, 0:2048])),
                 (dG, dG.inc()))
        plan.wait("gpsimd", [(dG, dG.n)])
        _finalize(nc, plan, sems, stack)
        return nc

    exp_prev = 0
    lt_ev = {}
    for c in range(8):
        buf = c % 2
        plan.add("sync", [(sV, sV.n)],
                 (lambda c=c, buf=buf: NCY.dma_start(
                     out=xcast[buf], in_=h1_dram[c * 128:(c + 1) * 128, :])),
                 (dS, dS.inc()))
        h1f_tok = dS.n
        plan.add("vector", [(dS, h1f_tok)],
                 (lambda c=c, buf=buf: NCV.tensor_sub(
                     zbuf[buf], xcast[buf], h1hi[c])),
                 (sV, sV.inc()))
        plan.add("vector", [(sV, sV.n)],
                 (lambda buf=buf: NCV.tensor_copy(
                     lo_nat[0][:, :], zbuf[buf][:, 0:1024])),
                 (sV, sV.inc()))
        plan.add("vector", [(sV, sV.n)],
                 (lambda buf=buf: NCV.tensor_copy(
                     lo_nat[1][:, :], zbuf[buf][:, 1024:2048])),
                 (sV, sV.inc()))
        lo_tok = sV.n
        for ch in range(16):
            src = lo_nat[0][:, ch * 128:(ch + 1) * 128] if ch < 8 else \
                lo_nat[1][:, (ch - 8) * 128:(ch - 8 + 1) * 128]
            key = c * 16 + ch
            wts = [(sV, lo_tok)]
            if key >= 2:
                wts.append((sV, lt_ev[key - 2]))
            else:
                wts.append((sV, h1T_done))
            plan.add("tensor", wts,
                     (lambda src=src, key=key: NCT.transpose(
                         _tslot(key), src, identb[:, :])),
                     (sT, sT.inc()))
            plan.add("vector", [(sT, sT.n)],
                     (lambda ch=ch, key=key: NCV.tensor_copy(
                         loT[:, ch * 128:(ch + 1) * 128], _tslot(key))),
                     (sV, sV.inc()))
            lt_ev[key] = sV.n
        loT_tok = sV.n
        nmm = 0
        for ps in range(3):
            for ch in range(16):
                wts = []
                if nmm == 0:
                    wts = [(sV, loT_tok), (sV, gw_ready), (sV, h1T_done),
                           (sA, exp_prev)]
                lhs = (h1T[:, ch * 1024 + c * 128:ch * 1024 + (c + 1) * 128]
                       if ps < 2 else loT[:, ch * 128:(ch + 1) * 128])
                rhs = (gwlob if ps == 1 else gwhib)[:, ch * 32:(ch + 1) * 32]
                plan.add("tensor", wts,
                         (lambda lhs=lhs, rhs=rhs, nmm=nmm: NCT.matmul(
                             ppC[:, 0:32], lhs, rhs,
                             start=(nmm == 0), stop=(nmm == 47))),
                         (sT, sT.inc()) if nmm == 47 else None)
                nmm += 1
        gate_T = sT.n
        plan.add("scalar", [(sT, gate_T), (dS, mconsts)],
                 (lambda: NCS.activation(smal[:, 0:32], ppC[:, 0:32], AF.Exp)),
                 (sA, sA.inc()))
        exp_prev = sA.n
        plan.add("vector", [(sA, sA.n)],
                 (lambda: NCV.tensor_scalar(smal[:, 32:64], smal[:, 0:32],
                                            1.0, 1.0, ALU.mult, ALU.add)),
                 (sV, sV.inc()))
        plan.add("scalar", [(sV, sV.n)],
                 (lambda: NCS.activation(smal[:, 64:96], smal[:, 32:64],
                                         AF.Ln)),
                 (sA, sA.inc()))
        plan.add("scalar", [(sA, sA.n)],
                 (lambda: NCS.activation(smal[:, 96:128], smal[:, 64:96],
                                         AF.Ln)),
                 (sA, sA.inc()))
        plan.add("scalar", [(sA, sA.n)],
                 (lambda: NCS.activation(smal[:, 128:160], smal[:, 96:128],
                                         AF.Exp, scale=0.5)),
                 (sA, sA.inc()))
        sc_tok = sA.n
        plan.add("vector", [(sA, sc_tok)],
                 (lambda: NCV.tensor_add(smal[:, 160:192], smal[:, 128:160],
                                         gbb[:, :])),
                 (sV, sV.inc()))
        plan.add("vector", [(sV, sV.n)],
                 (lambda: NCV.max(topv[:, :], smal[:, 160:192])),
                 (sV, sV.inc()))
        plan.add("vector", [(sV, sV.n)],
                 (lambda: NCV.max_index(idx8u[:, :], topv[:, :],
                                        smal[:, 160:192])),
                 (sV, sV.inc()))
        plan.add("vector", [(sV, sV.n)],
                 (lambda: NCV.tensor_copy(idx8f[:, :], idx8u[:, :])),
                 (sV, sV.inc()))
        plan.add("vector", [(sV, sV.n)],
                 (lambda c=c: NCV.tensor_copy(routb[:, c * 6:(c + 1) * 6],
                                              idx8f[:, 0:6])),
                 (sV, sV.inc()))
        for k in range(K):
            plan.add("vector", [(sV, sV.n)],
                     (lambda c=c, k=k: NCV.scalar_tensor_tensor(
                         smal[:, 192:224], iob[:, :], idx8f[:, k:k + 1],
                         smal[:, 128:160], ALU.is_equal, ALU.mult,
                         accum_out=routw[:, c * 6 + k:c * 6 + k + 1])),
                     (sV, sV.inc()))
        plan.add("vector", [(sV, sV.n)],
                 (lambda c=c: NCV.reduce_sum(stat[:, 9:10],
                                             routw[:, c * 6:(c + 1) * 6],
                                             axis=AX.X)),
                 (sV, sV.inc()))
        plan.add("vector", [(sV, sV.n)],
                 (lambda: NCV.reciprocal(stat[:, 10:11], stat[:, 9:10])),
                 (sV, sV.inc()))
        plan.add("vector", [(sV, sV.n)],
                 (lambda c=c: NCV.tensor_scalar(
                     routw[:, c * 6:(c + 1) * 6],
                     routw[:, c * 6:(c + 1) * 6],
                     stat[:, 10:11], MSCALE, ALU.mult, ALU.mult)),
                 (sV, sV.inc()))
        plan.add("gpsimd", [(sV, sV.n)],
                 (lambda c=c: NCP.dma_start(
                     out=idxloc[c * 128:(c + 1) * 128, :],
                     in_=routb[:, c * 6:(c + 1) * 6])),
                 (dG, dG.inc()))
    routw_tok = sV.n
    plan.add("gpsimd", [(dG, dG.n)],
             (lambda: NCP.collective_compute(
                 "AllGather", ALU.bypass, replica_groups=[list(range(NCORES))],
                 ins=[idxloc], outs=[idxfull])),
             (sC, sC.inc()))
    idxag_done = sC.n

    if stop_after == "m1":
        plan.add("vector", [(sV, routw_tok)],
                 (lambda: NCV.tensor_copy(zbuf[0][:, 0:48], routb[:, :])),
                 (sV, sV.inc()))
        plan.add("vector", [(sV, sV.n)],
                 (lambda: NCV.tensor_copy(zbuf[0][:, 48:96], routw[:, :])),
                 (sV, sV.inc()))
        plan.add("sync", [(sV, sV.n), (sC, idxag_done)],
                 (lambda: NCY.dma_start(out=out_slice[0:128, :],
                                        in_=zbuf[0])),
                 (dS, dS.inc()))
        plan.wait("gpsimd", [(dS, dS.n)])
        _finalize(nc, plan, sems, stack)
        return nc

    plan.add("vector", [(sV, sV.n)], lambda: NCV.memset(z32[:, :], 0.0),
             (sV, sV.inc()))
    plan.add("vector", [(sV, sV.n)], lambda: NCV.memset(carry32[:, :], 0.0),
             (sV, sV.inc()))
    CHW = 4096
    NCH = NA // CHW
    pr_tok = [0, 0]
    for chn in range(NCH):
        a0 = chn * CHW
        for n in range(8):
            bps = ppC if n % 2 == 0 else ppD
            ab = n % 2
            plan.add("sync",
                     ([(sC, idxag_done)] if chn == 0 and n == 0 else []) +
                     ([(sT, sT.n)] if (chn > 0 or n >= 2) else []),
                     (lambda a0=a0, n=n, ab=ab: NCY.dma_start(
                         out=arow[:, ab * 512:(ab + 1) * 512],
                         in_=bass.AP(idxfull_h, a0 + n * 512,
                                     [[512, 1], [1, 512]]))),
                     (dS, dS.inc()))
            arow_tok = dS.n
            wts = [(dS, arow_tok)]
            if chn == 0 and n == 0:
                wts.append((sA, exp_prev))
            plan.add("tensor", wts,
                     (lambda n=n, bps=bps, ab=ab: NCT.matmul(
                         bps[0:32, 0:512], onesf[0:1, 0:32],
                         arow[:, ab * 512:(ab + 1) * 512],
                         start=True, stop=True)),
                     (sT, sT.inc()))
            plan.add("vector", [(sT, sT.n)] +
                     ([(sT, p4_T), (sV, h1T_done)] if chn == 0 and n == 0
                      else []),
                     (lambda n=n, bps=bps: NCV.tensor_single_scalar(
                         Obuf[:, n * 512:(n + 1) * 512], bps[0:32, 0:512],
                         iotacolb[:, 0:1], ALU.is_equal)),
                     (sV, sV.inc()))
            plan.wait("tensor", [(sV, sV.n)])
        ob_tok = sV.n
        plan.add("vector", [(sV, ob_tok)],
                 (lambda: NCV.tensor_tensor_scan(
                     cumb[:, :], Obuf[:, :],
                     z32[:, :].broadcast_to([32, CHW]),
                     carry32[:, 0:1], ALU.add, ALU.add)),
                 (sV, sV.inc()))
        plan.add("vector", [(sV, sV.n)],
                 (lambda: NCV.tensor_copy(carry32[:, :],
                                          cumb[:, CHW - 1:CHW])),
                 (sV, sV.inc()))
        plan.add("vector", [(sV, sV.n)],
                 (lambda: NCV.tensor_mul(prodb[:, :], Obuf[:, :], cumb[:, :])),
                 (sV, sV.inc()))
        prod_tok = sV.n
        for n in range(8):
            ab = n % 2
            wts = [(sV, prod_tok)] if n == 0 else []
            plan.add("tensor", wts,
                     (lambda n=n: NCT.matmul(
                         ppE[0:1, 0:512], onesf[0:32, 0:1],
                         prodb[:, n * 512:(n + 1) * 512],
                         start=True, stop=True)),
                     (sT, sT.inc()))
            plan.add("vector", [(sT, sT.n)] +
                     ([(dS, pr_tok[0])] if pr_tok[0] else []),
                     (lambda n=n: NCV.tensor_copy(
                         posrow[:, 0:512], ppE[0:1, 0:512])),
                     (sV, sV.inc()))
            plan.wait("tensor", [(sV, sV.n)])
            plan.add("sync", [(sV, sV.n)],
                     (lambda a0=a0, n=n: NCY.dma_start(
                         out=bass.AP(posflat_h, a0 + n * 512,
                                     [[512, 1], [1, 512]]),
                         in_=posrow[:, 0:512])),
                     (dS, dS.inc()))
            pr_tok[0] = dS.n
    pos_done = dS.n

    plan.add("sync", [(dS, pos_done)],
             (lambda: NCY.dma_start(
                 out=ebuf,
                 in_=bass.AP(idxfull_h, 0, [[6, 128], [768, 64], [1, 6]]))),
             (dS, dS.inc()))
    plan.add("sync", [],
             (lambda: NCY.dma_start(
                 out=pbuf,
                 in_=bass.AP(posflat_h, 0, [[6, 128], [768, 64], [1, 6]]))),
             (dS, dS.inc()))
    ep_tok = dS.n
    plan.add("vector", [(dS, mconsts), (sV, sV.n)],
             (lambda: NCV.tensor_scalar(stat[:, 11:12], rk4b[:, :],
                                        1.0, 3.0, ALU.mult, ALU.add)),
             (sV, sV.inc()))
    plan.add("vector", [(sV, sV.n)],
             (lambda: NCV.tensor_scalar(stat[:, 12:13], rk4b[:, :],
                                        768.0, 1.0, ALU.mult, ALU.add)),
             (sV, sV.inc()))
    plan.add("vector", [(sV, sV.n), (dS, ep_tok)],
             (lambda: NCV.tensor_single_scalar(m3s1, pbuf, 768.0, ALU.is_le)),
             (sV, sV.inc()))
    plan.add("vector", [(sV, sV.n)],
             (lambda: NCV.tensor_scalar(m3s2, ebuf, rk4b[:, 0:1], None,
                                        ALU.is_ge)),
             (sV, sV.inc()))
    plan.add("vector", [(sV, sV.n)],
             (lambda: NCV.tensor_mul(m3s1, m3s1, m3s2)),
             (sV, sV.inc()))
    plan.add("vector", [(sV, sV.n)],
             (lambda: NCV.tensor_scalar(m3s2, ebuf, stat[:, 11:12], None,
                                        ALU.is_le)),
             (sV, sV.inc()))
    plan.add("vector", [(sV, sV.n)],
             (lambda: NCV.tensor_mul(m3s1, m3s1, m3s2)),
             (sV, sV.inc()))
    sel_tok = sV.n
    plan.add("vector", [(sV, sV.n)],
             (lambda: NCV.scalar_tensor_tensor(m3s2, ebuf, 768.0, pbuf,
                                               ALU.mult, ALU.add)),
             (sV, sV.inc()))
    plan.add("vector", [(sV, sV.n)],
             (lambda: NCV.tensor_scalar(m3s2, m3s2, stat[:, 12:13], None,
                                        ALU.subtract)),
             (sV, sV.inc()))
    BIGS = float(1 << 20)
    plan.add("vector", [(sV, sV.n)],
             (lambda: NCV.tensor_scalar(m3s2, m3s2, 1.0, -BIGS,
                                        ALU.mult, ALU.add)),
             (sV, sV.inc()))
    plan.add("vector", [(sV, sV.n)],
             (lambda: NCV.tensor_mul(m3s2, m3s2, m3s1)),
             (sV, sV.inc()))
    plan.add("vector", [(sV, sV.n)],
             (lambda: NCV.tensor_scalar(m3s2, m3s2, 1.0, BIGS,
                                        ALU.mult, ALU.add)),
             (sV, sV.inc()))
    plan.add("vector", [(sV, sV.n)],
             (lambda: NCV.tensor_copy(idxu[:, 0:384], m3s2)),
             (sV, sV.inc()))
    idxd_tok = sV.n

    if stop_after == "m3":
        plan.add("vector", [(sV, idxd_tok)],
                 (lambda: NCV.tensor_copy(zbuf[0][:, 0:48], routb[:, :])),
                 (sV, sV.inc()))
        plan.add("vector", [(sV, sV.n)],
                 (lambda: NCV.tensor_copy(zbuf[0][:, 48:96], routw[:, :])),
                 (sV, sV.inc()))
        plan.add("vector", [(sV, sV.n)],
                 (lambda: NCV.tensor_copy(zbuf[0][:, 96:480], m3s2)),
                 (sV, sV.inc()))
        plan.add("sync", [(sV, sV.n)],
                 (lambda: NCY.dma_start(out=out_slice[0:128, :],
                                        in_=zbuf[0])),
                 (dS, dS.inc()))
        plan.add("sync", [(dS, pos_done)],
                 (lambda: NCY.dma_start(
                     out=out_slice.rearrange("a b -> (a b)")[
                         300000:300000 + NA].rearrange("(a b) -> a b", a=1),
                     in_=posflat)),
                 (dS, dS.inc()))
        plan.wait("gpsimd", [(dS, dS.n)])
        _finalize(nc, plan, sems, stack)
        return nc

    sc_tok = {}
    for c64 in range(64):
        db = c64 % 2
        wts = [(sC, h1ag_done)]
        if c64 >= 2:
            wts.append((dG, sc_tok[c64 - 2]))
        plan.add("sync", wts,
                 (lambda c64=c64, db=db: NCY.dma_start(
                     out=gbuf[db][:, :],
                     in_=h1full[c64 * 128:(c64 + 1) * 128, :])),
                 (dS, dS.inc()))
        src_tok = dS.n
        for k in range(K):
            wts = [(dS, src_tok)] if k == 0 else []
            if c64 == 0 and k == 0:
                wts.append((sV, idxd_tok))
            if k == 0 and c64 >= 4:
                wts.append((dG, sc_tok[c64 - 4]))
            plan.add("gpsimd", wts,
                     (lambda c64=c64, k=k, db=db: NCP.indirect_dma_start(
                         xb_disp,
                         IndirectOffsetOnAxis(idxu[:, c64 * 6 + k:
                                                   c64 * 6 + k + 1], 0),
                         gbuf[db][:, :], None,
                         bounds_check=regs["disp"], oob_is_err=False)),
                     (dG, dG.inc()))
        sc_tok[c64] = dG.n
    disp_done = dG.n

    m3_done_T = sT.n

    def _wload(srcT, nch, wid, extra_w=()):
        toks = []
        for ch in range(nch):
            bufw = ch % 2
            wts = list(extra_w) if ch == 0 else []
            if ch >= 2:
                wts.append((sV, toks[ch - 2]))
            plan.add("sync", wts + ([(sV, sV.n)] if ch < 2 else []),
                     (lambda srcT=srcT, ch=ch, bufw=bufw: NCY.dma_start(
                         out=xcast[bufw][:, 0:wid] if wid <= D else None,
                         in_=srcT[ch * 128:(ch + 1) * 128, :])),
                     (dS, dS.inc()))
            plan.add("vector", [(dS, dS.n)],
                     (lambda ch=ch, bufw=bufw: NCV.tensor_copy(
                         wstr[:, ch * wid:(ch + 1) * wid],
                         xcast[bufw][:, 0:wid])),
                     (sV, sV.inc()))
            toks.append(sV.n)
        return sV.n

    sg_tok = _wload(sgT, 16, 1024, extra_w=[(sT, m3_done_T), (sV, sV.n)])
    sh_mm = 0
    sil_tok = [0, 0]
    for m in range(8):
        mps = ppA if m % 2 == 0 else ppB
        for qh in range(2):
            for ch in range(16):
                wts = []
                if ch == 0:
                    wts = [(sV, sg_tok)]
                    if m >= 2 or (m >= 1 and qh >= 0):
                        wts.append((sA, sil_tok[m % 2]))
                plan.add("tensor", wts,
                         (lambda m=m, qh=qh, ch=ch, mps=mps: NCT.matmul(
                             mps[:, qh * 512:(qh + 1) * 512],
                             wstr[:, ch * 1024 + m * 128:
                                  ch * 1024 + (m + 1) * 128],
                             h1T[:, ch * 1024 + qh * 512:
                                 ch * 1024 + (qh + 1) * 512],
                             start=(ch == 0), stop=(ch == 15))),
                         (sT, sT.inc()) if ch == 15 else None)
            plan.add("scalar", [(sT, sT.n)],
                     (lambda m=m, qh=qh, mps=mps: NCS.activation(
                         shg[:, m * 1024 + qh * 512:
                             m * 1024 + (qh + 1) * 512],
                         mps[:, qh * 512:(qh + 1) * 512], AF.Silu)),
                     (sA, sA.inc()))
        sil_tok[m % 2] = sA.n
    sg_mm_T = sT.n
    su_tok = _wload(suT, 16, 1024, extra_w=[(sT, sg_mm_T)])
    si_ev = [0, 0]
    for m in range(8):
        mps = ppA if m % 2 == 0 else ppB
        for qh in range(2):
            for ch in range(16):
                wts = []
                if ch == 0:
                    wts = [(sV, su_tok), (sA, sil_tok[1])]
                    if m >= 2:
                        wts.append((sV, si_ev[m % 2]))
                plan.add("tensor", wts,
                         (lambda m=m, qh=qh, ch=ch, mps=mps: NCT.matmul(
                             mps[:, qh * 512:(qh + 1) * 512],
                             wstr[:, ch * 1024 + m * 128:
                                  ch * 1024 + (m + 1) * 128],
                             h1T[:, ch * 1024 + qh * 512:
                                 ch * 1024 + (qh + 1) * 512],
                             start=(ch == 0), stop=(ch == 15))),
                         (sT, sT.inc()) if ch == 15 else None)
            plan.add("vector", [(sT, sT.n), (sV, sV.n)],
                     (lambda m=m, qh=qh, mps=mps: NCV.tensor_mul(
                         smal[:, 0:512],
                         shg[:, m * 1024 + qh * 512:m * 1024 + (qh + 1) * 512],
                         mps[:, qh * 512:(qh + 1) * 512])),
                     (sV, sV.inc()))
            plan.add("vector", [(sV, sV.n)],
                     (lambda m=m, qh=qh: NCV.tensor_scalar(
                         siT[:, m * 1024 + qh * 512:m * 1024 + (qh + 1) * 512],
                         smal[:, 0:512], SWIGLU_LIMIT, -SWIGLU_LIMIT,
                         ALU.min, ALU.max)),
                     (sV, sV.inc()))
        si_ev[m % 2] = sV.n
    su_mm_T = sT.n
    si_done = sV.n
    sd_tok = _wload(sdT, 8, 2048, extra_w=[(sT, su_mm_T)])
    sh_store = 0
    for ct in range(8):
        for ich in range(8):
            for n in range(4):
                ps = ppA if n < 2 else ppB
                wts = []
                if ich == 0 and n == 0:
                    wts = [(sV, sd_tok), (sV, si_done)]
                    if ct >= 1:
                        wts.append((sV, sV.n))
                plan.add("tensor", wts,
                         (lambda ct=ct, ich=ich, n=n, ps=ps: NCT.matmul(
                             ps[:, (n % 2) * 512:(n % 2 + 1) * 512],
                             siT[:, ich * 1024 + ct * 128:
                                 ich * 1024 + (ct + 1) * 128],
                             wstr[:, ich * 2048 + n * 512:
                                  ich * 2048 + (n + 1) * 512],
                             start=(ich == 0), stop=(ich == 7))),
                         (sT, sT.inc()) if (ich == 7 and n == 3) else None)
        shd_T = sT.n
        plan.add("vector", [(sT, shd_T)],
                 (lambda: NCV.tensor_copy(ybev[:, 0:1024], ppA[:, :])),
                 (sV, sV.inc()))
        plan.add("vector", [(sV, sV.n)],
                 (lambda: NCV.tensor_copy(ybev[:, 1024:2048], ppB[:, :])),
                 (sV, sV.inc()))
        plan.add("sync", [(sV, sV.n)],
                 (lambda ct=ct: NCY.dma_start(
                     out=shared_dram[ct * 128:(ct + 1) * 128, :],
                     in_=ybev[:, 0:2048])),
                 (dS, dS.inc()))
        sh_store = dS.n
        plan.wait("tensor", [(sV, sV.n)])
    m2c_T = sT.n

    if stop_after == "m2":
        plan.add("gpsimd", [(dS, sh_store), (dG, disp_done)],
                 (lambda: NCP.dma_start(out=out_slice, in_=shared_dram)),
                 (dG, dG.inc()))
        plan.wait("gpsimd", [(dG, dG.n)])
        _finalize(nc, plan, sems, stack)
        return nc

    ag_toks = []
    xb_ev = {}
    for el in range(EPC):
        for ct in range(6):
            bufp = ct % 2
            wts = [(dG, disp_done)] if (el == 0 and ct == 0) else []
            if el == 0 and ct == 0:
                wts.append((sT, m2c_T))
                wts.append((sV, sV.n))
            if ct >= 2 or el > 0:
                wts.append((sT, sT.n))
            plan.add("sync", wts,
                     (lambda el=el, ct=ct, bufp=bufp: NCY.dma_start(
                         out=lo_nat[bufp][:, :],
                         in_=xb_disp[el * CAP + ct * 128:
                                     el * CAP + (ct + 1) * 128, 0:1024])),
                     (dS, dS.inc()))
            plan.add("sync", [],
                     (lambda el=el, ct=ct, bufp=bufp: NCY.dma_start(
                         out=gbuf[bufp][:, 0:1024],
                         in_=xb_disp[el * CAP + ct * 128:
                                     el * CAP + (ct + 1) * 128, 1024:2048])),
                     (dS, dS.inc()))
            xb_tok = dS.n
            for ch in range(16):
                src = (lo_nat[bufp][:, ch * 128:(ch + 1) * 128] if ch < 8
                       else gbuf[bufp][:, (ch - 8) * 128:(ch - 7) * 128])
                i = el * 96 + ct * 16 + ch
                wts2 = [(dS, xb_tok)] if ch == 0 else []
                if i >= 2 and xb_ev.get(i - 2):
                    wts2.append((sV, xb_ev[i - 2]))
                elif i < 2:
                    wts2.append((sV, lt_ev[127]))
                plan.add("tensor", wts2,
                         (lambda src=src, i=i: NCT.transpose(
                             _tslot(i), src, identb[:, :])),
                         (sT, sT.inc()))
                plan.add("vector", [(sT, sT.n)],
                         (lambda ch=ch, ct=ct, i=i: NCV.tensor_copy(
                             xbT[:, ch * 768 + ct * 128:
                                 ch * 768 + (ct + 1) * 128],
                             _tslot(i))),
                         (sV, sV.inc()))
                xb_ev[i] = sV.n
        xbT_tok = sV.n
        plan.add("gpsimd", [(sT, m2c_T) if el == 0 else (sT, sT.n)],
                 (lambda el=el: NCP.dma_start(
                     out=wstr[:, :].rearrange("p (c m) -> p c m", c=16),
                     in_=wgT[el].rearrange("(c p) m -> p c m", p=128))),
                 (dG, dG.inc()))
        wg_tok = dG.n
        gs_tok = [0, 0]
        for m in range(8):
            mps = ppA if m % 2 == 0 else ppB
            for nn in range(2):
                n0, n1 = (0, 512) if nn == 0 else (512, 768)
                for ch in range(16):
                    wts = []
                    if ch == 0 and nn == 0:
                        wts = [(dG, wg_tok), (sV, xbT_tok)]
                        if m >= 2:
                            wts.append((sA, gs_tok[m % 2]))
                    plan.add("tensor", wts,
                             (lambda m=m, ch=ch, n0=n0, n1=n1, mps=mps:
                              NCT.matmul(
                                  mps[:, n0:n1],
                                  wstr[:, ch * 1024 + m * 128:
                                       ch * 1024 + (m + 1) * 128],
                                  xbT[:, ch * 768 + n0:ch * 768 + n1],
                                  start=(ch == 0), stop=(ch == 15))),
                             (sT, sT.inc()) if (ch == 15 and nn == 1)
                             else None)
            plan.add("scalar", [(sT, sT.n)] +
                     ([(sV, h1T_done), (sA, sA.n)] if el == 0 and m == 0
                      else []),
                     (lambda m=m, mps=mps: NCS.activation(
                         gsil[:, m * 768:(m + 1) * 768],
                         mps[:, 0:768], AF.Silu)),
                     (sA, sA.inc()))
            gs_tok[m % 2] = sA.n
        g_mm_T = sT.n
        plan.add("gpsimd", [(sT, g_mm_T)],
                 (lambda el=el: NCP.dma_start(
                     out=wstr[:, :].rearrange("p (c m) -> p c m", c=16),
                     in_=wuT[el].rearrange("(c p) m -> p c m", p=128))),
                 (dG, dG.inc()))
        wu_tok = dG.n
        int_ev = [0, 0]
        for m in range(8):
            mps = ppA if m % 2 == 0 else ppB
            for nn in range(2):
                n0, n1 = (0, 512) if nn == 0 else (512, 768)
                for ch in range(16):
                    wts = []
                    if ch == 0 and nn == 0:
                        wts = [(dG, wu_tok), (sA, gs_tok[m % 2])]
                        if m >= 2:
                            wts.append((sV, int_ev[m % 2]))
                    plan.add("tensor", wts,
                             (lambda m=m, ch=ch, n0=n0, n1=n1, mps=mps:
                              NCT.matmul(
                                  mps[:, n0:n1],
                                  wstr[:, ch * 1024 + m * 128:
                                       ch * 1024 + (m + 1) * 128],
                                  xbT[:, ch * 768 + n0:ch * 768 + n1],
                                  start=(ch == 0), stop=(ch == 15))),
                             (sT, sT.inc()) if (ch == 15 and nn == 1)
                             else None)
            plan.add("vector", [(sT, sT.n)],
                     (lambda m=m, mps=mps: NCV.tensor_mul(
                         intT[:, m * 768:(m + 1) * 768],
                         gsil[:, m * 768:(m + 1) * 768],
                         mps[:, 0:768])),
                     (sV, sV.inc()))
            int_ev[m % 2] = sV.n
        u_mm_T = sT.n
        int_tok = sV.n
        plan.add("gpsimd", [(sT, u_mm_T)],
                 (lambda el=el: NCP.dma_start(
                     out=wstr[:, :].rearrange("p (c m) -> p c m", c=8),
                     in_=wdT[el].rearrange("(c p) m -> p c m", p=128))),
                 (dG, dG.inc()))
        wd_tok = dG.n
        yb_store = 0
        for ct in range(6):
            for ich in range(8):
                for n in range(4):
                    ps = ppA if n < 2 else ppB
                    wts = []
                    if ich == 0 and n == 0:
                        wts = [(dG, wd_tok), (sV, int_tok)]
                        if ct >= 1:
                            wts.append((sV, sV.n))
                    plan.add("tensor", wts,
                             (lambda ct=ct, ich=ich, n=n, ps=ps: NCT.matmul(
                                 ps[:, (n % 2) * 512:(n % 2 + 1) * 512],
                                 intT[:, ich * 768 + ct * 128:
                                      ich * 768 + (ct + 1) * 128],
                                 wstr[:, ich * 2048 + n * 512:
                                      ich * 2048 + (n + 1) * 512],
                                 start=(ich == 0), stop=(ich == 7))),
                             (sT, sT.inc()) if (ich == 7 and n == 3)
                             else None)
            plan.add("vector", [(sT, sT.n)],
                     (lambda: NCV.tensor_copy(ybev[:, 0:1024], ppA[:, :])),
                     (sV, sV.inc()))
            plan.add("vector", [(sV, sV.n)],
                     (lambda: NCV.tensor_copy(ybev[:, 1024:2048], ppB[:, :])),
                     (sV, sV.inc()))
            plan.add("sync", [(sV, sV.n)],
                     (lambda el=el, ct=ct: NCY.dma_start(
                         out=yb_loc[el * CAP + ct * 128:
                                    el * CAP + (ct + 1) * 128, :],
                         in_=ybev[:, 0:2048])),
                     (dS, dS.inc()))
            yb_store = dS.n
            plan.wait("tensor", [(sV, sV.n)])
        plan.add("gpsimd", [(dS, yb_store)],
                 (lambda el=el: NCP.collective_compute(
                     "AllGather", ALU.bypass,
                     replica_groups=[list(range(NCORES))],
                     ins=[yb_loc[el * CAP:(el + 1) * CAP, :]],
                     outs=[ybfull[el * NCORES * CAP:
                                  (el + 1) * NCORES * CAP, :]])),
                 (sC, sC.inc()))
        ag_toks.append(sC.n)
    yb_all = sC.n

    if stop_after == "m5":
        plan.add("gpsimd", [(sC, yb_all)],
                 (lambda: NCP.dma_start(out=out_slice,
                                        in_=ybfull[0:TPC, :])),
                 (dG, dG.inc()))
        plan.wait("gpsimd", [(dG, dG.n)])
        _finalize(nc, plan, sems, stack)
        return nc

    plan.add("gpsimd", [(dS, pos_done), (dS, mconsts)],
             (lambda: _m6_pmy_load()),
             (dG, dG.inc()))

    def _m6_pmy_load():
        with NCP.register("ra") as ra:
            NCP.reg_load(ra, aoffs[0:1, 0:1])
            return NCP.dma_start(
                out=pmy,
                in_=bass.AP(posflat_h, ra, [[6, 128], [768, 8], [1, 6]]))
    pmy_tok = dG.n
    plan.add("vector", [(dG, pmy_tok), (sV, routw_tok)],
             (lambda: NCV.tensor_single_scalar(vldm, pmy, 768.0, ALU.is_le)),
             (sV, sV.inc()))
    plan.add("vector", [(sV, sV.n)],
             (lambda: NCV.tensor_scalar(qv, routb[:, 0:48], 0.25, -0.375,
                                        ALU.mult, ALU.add)),
             (sV, sV.inc()))
    plan.add("vector", [(sV, sV.n)],
             (lambda: NCV.tensor_copy(idxu[:, 448:496], qv)),
             (sV, sV.inc()))
    plan.add("vector", [(sV, sV.n)],
             (lambda: NCV.tensor_copy(e4f, idxu[:, 448:496])),
             (sV, sV.inc()))
    plan.add("vector", [(sV, sV.n)],
             (lambda: NCV.scalar_tensor_tensor(t1v, routb[:, 0:48], 6144.0,
                                               pmy, ALU.mult, ALU.add)),
             (sV, sV.inc()))
    plan.add("vector", [(sV, sV.n)],
             (lambda: NCV.scalar_tensor_tensor(rowf, e4f, -23808.0, t1v,
                                               ALU.mult, ALU.add)),
             (sV, sV.inc()))
    plan.add("vector", [(sV, sV.n)],
             (lambda: NCV.tensor_scalar(rowf, rowf, 1.0, -1.0 - float(1 << 20),
                                        ALU.mult, ALU.add)),
             (sV, sV.inc()))
    plan.add("vector", [(sV, sV.n)],
             (lambda: NCV.tensor_mul(rowf, rowf, vldm)),
             (sV, sV.inc()))
    plan.add("vector", [(sV, sV.n)],
             (lambda: NCV.tensor_scalar(rowf, rowf, 1.0, float(1 << 20),
                                        ALU.mult, ALU.add)),
             (sV, sV.inc()))
    plan.add("vector", [(sV, sV.n)],
             (lambda: NCV.tensor_copy(idxu[:, 384:432], rowf)),
             (sV, sV.inc()))
    plan.add("vector", [(sV, sV.n)],
             (lambda: NCV.tensor_mul(weff[:, :], routw[:, 0:48], vldm)),
             (sV, sV.inc()))
    idxr_tok = sV.n
    plan.add("vector", [(sV, sV.n)],
             (lambda: NCV.memset(gbuf[0][:, :], 0.0)), (sV, sV.inc()))
    plan.add("vector", [(sV, sV.n)],
             (lambda: NCV.memset(gbuf[1][:, :], 0.0)), (sV, sV.inc()))
    gz_tok = sV.n

    out_tok = 0
    g_done = {}
    comb_tok = {}
    for c in range(8):
        plan.add("sync", [(sV, sV.n)] if c == 0 else [],
                 (lambda c=c: NCY.dma_start(
                     out=xcast[0], in_=h1_dram[c * 128:(c + 1) * 128, :])),
                 (dS, dS.inc()))
        plan.add("sync", [(dS, sh_store)] if c == 0 else [],
                 (lambda c=c: NCY.dma_start(
                     out=ybev[:, 0:2048],
                     in_=shared_dram[c * 128:(c + 1) * 128, :])),
                 (dS, dS.inc()))
        ld_tok = dS.n
        for k in range(K):
            kk = c * 6 + k
            wts = []
            if c == 0 and k == 0:
                wts = [(sC, yb_all), (sV, idxr_tok), (sV, gz_tok)]
            if kk >= 2:
                wts.append((sV, comb_tok.get(kk - 2, 0)))
            plan.add("gpsimd", wts,
                     (lambda c=c, k=k, kk=kk: NCP.indirect_dma_start(
                         gbuf[kk % 2][:, :], None,
                         ybfull,
                         IndirectOffsetOnAxis(idxu[:, 384 + kk:385 + kk], 0),
                         bounds_check=regs["ret"], oob_is_err=False)),
                     (dG, dG.inc()))
            g_done[kk] = dG.n
            if k == 0:
                plan.add("vector", [(dG, g_done[kk])],
                         (lambda c=c, kk=kk: NCV.tensor_scalar(
                             zbuf[0], gbuf[kk % 2][:, :],
                             weff[:, kk:kk + 1], None, ALU.mult)),
                         (sV, sV.inc()))
            else:
                plan.add("vector", [(dG, g_done[kk]), (sV, sV.n)],
                         (lambda c=c, kk=kk: NCV.scalar_tensor_tensor(
                             zbuf[0], gbuf[kk % 2][:, :],
                             weff[:, kk:kk + 1], zbuf[0],
                             ALU.mult, ALU.add)),
                         (sV, sV.inc()))
            comb_tok[kk] = sV.n
        plan.add("vector", [(sV, sV.n), (dS, ld_tok)],
                 (lambda: NCV.scalar_tensor_tensor(
                     zfin, ybev[:, 0:2048], 1.0, zbuf[0],
                     ALU.mult, ALU.add)),
                 (sV, sV.inc()))
        plan.add("vector", [(sV, sV.n)],
                 (lambda: NCV.scalar_tensor_tensor(
                     zbuf[0], xcast[0], 1.0, zfin,
                     ALU.mult, ALU.add, accum_out=stat[:, 0:1])),
                 (sV, sV.inc()))
        z2_tok = sV.n
        plan.add("scalar", [(sV, z2_tok)],
                 (lambda: NCS.activation(sq2[:, 0:2048], zbuf[0], AF.Square,
                                         accum_out=stat[:, 1:2])),
                 (sA, sA.inc()))
        plan.add("vector", [(sA, sA.n)],
                 (lambda: NCV.tensor_scalar(stat[:, 2:3], stat[:, 0:1],
                                            1.0 / D, 0.0, ALU.mult, ALU.add)),
                 (sV, sV.inc()))
        plan.add("vector", [(sV, sV.n)],
                 (lambda: NCV.tensor_scalar(stat[:, 3:4], stat[:, 1:2],
                                            1.0 / D, 0.0, ALU.mult, ALU.add)),
                 (sV, sV.inc()))
        plan.add("vector", [(sV, sV.n)],
                 (lambda: NCV.tensor_mul(stat[:, 4:5], stat[:, 2:3],
                                         stat[:, 2:3])),
                 (sV, sV.inc()))
        plan.add("vector", [(sV, sV.n)],
                 (lambda: NCV.tensor_sub(stat[:, 5:6], stat[:, 3:4],
                                         stat[:, 4:5])),
                 (sV, sV.inc()))
        plan.add("vector", [(sV, sV.n)],
                 (lambda: NCV.tensor_scalar(stat[:, 6:7], stat[:, 5:6],
                                            1.0, EPS, ALU.mult, ALU.add)),
                 (sV, sV.inc()))
        plan.add("scalar", [(sV, sV.n)],
                 (lambda: NCS.activation(stat[:, 7:8], stat[:, 6:7], AF.Ln)),
                 (sA, sA.inc()))
        plan.add("scalar", [(sA, sA.n)],
                 (lambda: NCS.activation(stat[:, 8:9], stat[:, 7:8], AF.Exp,
                                         scale=-0.5)),
                 (sA, sA.inc()))
        plan.add("vector", [(sA, sA.n)],
                 (lambda: NCV.tensor_scalar(zfin, zbuf[0], stat[:, 2:3],
                                            stat[:, 8:9],
                                            ALU.subtract, ALU.mult)),
                 (sV, sV.inc()))
        fin_tok = sV.n
        plan.add("sync", [(sV, fin_tok)],
                 (lambda c=c: NCY.dma_start(
                     out=out_slice[c * 128:(c + 1) * 128, :], in_=zfin)),
                 (dS, dS.inc()))
        out_tok = dS.n
        plan.wait("vector", [(dS, out_tok)]) if c == 7 else None
    plan.wait("gpsimd", [(dS, out_tok)])

    _finalize(nc, plan, sems, stack)
    return nc


def _finalize(nc, plan, sems, stack):
    for s in sems:
        s.h = stack.enter_context(nc.semaphore(s.name))
    with nc.Block() as block:
        @block.tensor
        def _(eng):
            for f in plan.ops["tensor"]:
                f()

        @block.scalar
        def _(eng):
            for f in plan.ops["scalar"]:
                f()

        @block.vector
        def _(eng):
            for f in plan.ops["vector"]:
                f()

        @block.gpsimd
        def _(eng):
            for f in plan.ops["gpsimd"]:
                f()

        @block.sync
        def _(eng):
            for f in plan.ops["sync"]:
                f()
    stack.close()


def _stage(inputs):
    import ml_dtypes
    x = np.asarray(inputs["L0_attn_x"]).reshape(T, D)
    Wq = np.asarray(inputs["L0_attn_Wq"])
    Wk = np.asarray(inputs["L0_attn_Wk"])
    Wv = np.asarray(inputs["L0_attn_Wv"])
    Wo = np.asarray(inputs["L0_attn_Wo"])
    bo = np.asarray(inputs["L0_attn_bo"])
    gw = np.asarray(inputs["L0_ffn_gate_w"])
    gb = np.asarray(inputs["L0_ffn_bias"])
    wg = np.asarray(inputs["L0_ffn_w_gate"])
    wu = np.asarray(inputs["L0_ffn_w_up"])
    wd = np.asarray(inputs["L0_ffn_w_down"])
    sg = np.asarray(inputs["L0_ffn_sg"])
    su = np.asarray(inputs["L0_ffn_su"])
    sd = np.asarray(inputs["L0_ffn_sd"])

    C = np.ascontiguousarray
    mk = np.zeros((4, 128, 512), np.float32)
    for j in range(4):
        kk = 128 * j + np.arange(128)[:, None]
        qq = np.arange(512)[None, :]
        mk[j] = (kk <= qq).astype(np.float32)
    masks = mk.astype(ml_dtypes.bfloat16)
    ident = np.eye(128, dtype=ml_dtypes.bfloat16)
    lsta = np.zeros((2, 128, 64), np.float32)
    lsta[0, :, 0] = 1.0
    lsta[1, :, 32] = 1.0
    lstaf = lsta.astype(np.float32)
    lsta = lsta.astype(ml_dtypes.bfloat16)
    ones1f = np.ones((64, 128), np.float32)
    iota32 = np.broadcast_to(np.arange(E, dtype=np.float32), (128, E)).copy()
    bo_bc = np.broadcast_to(bo.astype(np.float32), (128, D)).copy()
    gb_bc = np.broadcast_to(gb.astype(np.float32), (128, E)).copy()
    gwT = C(np.asarray(gw, np.float32).T)

    maps = []
    for r in range(NCORES):
        hs = slice(r * HPC * HD, (r + 1) * HPC * HD)
        es = slice(r * EPC, (r + 1) * EPC)
        maps.append({
            "x_slice": C(x[r * TPC:(r + 1) * TPC]),
            "wq": C(Wq[:, hs]), "wk": C(Wk[:, hs]), "wv": C(Wv[:, hs]),
            "wo": C(Wo),
            "bo_bc": bo_bc,
            "gwT": gwT,
            "gbias_bc": gb_bc,
            "wgT": C(wg[es].transpose(0, 2, 1)),
            "wuT": C(wu[es].transpose(0, 2, 1)),
            "wdT": C(wd[es].transpose(0, 2, 1)),
            "sgT": C(sg.T), "suT": C(su.T), "sdT": C(sd.T),
            "masks": masks,
            "iota32_bc": iota32,
            "ident": ident,
            "lsta": lsta,
            "lstaf": lstaf,
            "ones1f": ones1f,
            "rk4": np.full((128, 1), 4.0 * r, np.float32),
            "qoff": np.full((1, 1), (r // 2) * 2048 * 2048 + (r % 2) * 1024,
                            np.uint32),
            "aoff": np.full((1, 1), r * TPC * K, np.uint32),
            "iotacol": np.arange(E, dtype=np.float32).reshape(E, 1),
        })
    return maps


def kernel(**inputs):
    maps = _stage(inputs)
    nc = _build(stop_after=None)
    res = run_bass_kernel_spmd(nc, maps, core_ids=list(range(NCORES)))
    out = np.concatenate([res.results[r]["out_slice"] for r in range(NCORES)], 0)
    return out.reshape(1, T, D).astype(np.float32, copy=False)


# version 06 (final)
# speedup vs baseline: 1.0961x; 1.0284x over previous
import numpy as np

import concourse.bass as bass
import concourse.mybir as mybir
from concourse.bass_utils import run_bass_kernel_spmd
from concourse.bass_core_rust import IndirectOffsetOnAxis

F32 = mybir.dt.float32
BF16 = mybir.dt.bfloat16
U32 = mybir.dt.uint32
AF = mybir.ActivationFunctionType
ALU = mybir.AluOpType
AX = mybir.AxisListType

NCORES = 8
T = 8192
D = 2048
NH = 16
HD = 128
HPC = NH // NCORES
TPC = T // NCORES
E = 32
K = 6
CAP = 768
EPC = E // NCORES
II = 1024
MSCALE = 1.5
SWIGLU_LIMIT = 10.0
EPS = 1e-5
ATT_SCALE = 1.0 / float(np.sqrt(HD))
EXP_SHIFT = -12.0
NA = T * K

ENGS = ("tensor", "scalar", "vector", "gpsimd", "sync")
L_MODE = "split"
CTX_AG_CHUNKS = 4
VN_QUEUE = "scalar"


class _Sem:
    def __init__(self, name, step=1):
        self.name, self.step, self.n, self.h = name, step, 0, None

    def inc(self):
        self.n += self.step
        return self.n


class _Plan:
    def __init__(self, nc):
        self.ops = {e: [] for e in ENGS}
        self.meta = {e: [] for e in ENGS}
        self.eng = {"tensor": nc.tensor, "scalar": nc.scalar,
                    "vector": nc.vector, "gpsimd": nc.gpsimd, "sync": nc.sync}

    def add(self, e, waits, mk, inc=None):
        eng = self.eng[e]
        waits = [(s, v) for (s, v) in waits if v > 0]
        self.meta[e].append((waits, inc))

        def f():
            for (s, v) in waits:
                eng.wait_ge(s.h, v)
            ins = mk()
            if inc is not None:
                ins.then_inc(inc[0].h, inc[0].step)
        self.ops[e].append(f)

    def wait(self, e, waits):
        eng = self.eng[e]
        waits = [(s, v) for (s, v) in waits if v > 0]
        self.meta[e].append((waits, None))

        def f():
            for (s, v) in waits:
                eng.wait_ge(s.h, v)
        self.ops[e].append(f)


def _check_deadlock(plan):
    heads = {e: 0 for e in ENGS}
    cnt = {}
    progress = True
    while progress:
        progress = False
        for e in ENGS:
            while heads[e] < len(plan.meta[e]):
                waits, inc = plan.meta[e][heads[e]]
                if all(cnt.get(s.name, 0) >= v for (s, v) in waits):
                    if inc is not None:
                        cnt[inc[0].name] = cnt.get(inc[0].name, 0) + inc[0].step
                    heads[e] += 1
                    progress = True
                else:
                    break
    stuck = {e: heads[e] for e in ENGS if heads[e] < len(plan.meta[e])}
    if stuck:
        msg = ["DEADLOCK: engines stuck:"]
        for e, h in stuck.items():
            waits, _ = plan.meta[e][h]
            unmet = [(s.name, v, cnt.get(s.name, 0)) for (s, v) in waits
                     if cnt.get(s.name, 0) < v]
            msg.append(f"  {e}[{h}/{len(plan.meta[e])}] waits {unmet}")
        raise RuntimeError("\n".join(msg))


def _build(stop_after=None):
    nc = bass.Bass("TRN2", target_bir_lowering=False)
    NCT, NCS, NCV, NCP, NCY = nc.tensor, nc.scalar, nc.vector, nc.gpsimd, nc.sync

    def din(name, shape, dt=F32):
        return nc.dram_tensor(name, shape, dt, kind="ExternalInput").ap()

    x_slice = din("x_slice", [TPC, D])
    wq = din("wq", [D, HPC * HD])
    wk = din("wk", [D, HPC * HD])
    wv = din("wv", [D, HPC * HD])
    wo = din("wo", [D, D])
    bo_bc = din("bo_bc", [128, D])
    gwT = din("gwT", [D, E])
    gbias_bc = din("gbias_bc", [128, E])
    wgT = din("wgT", [EPC, D, II])
    wuT = din("wuT", [EPC, D, II])
    wdT = din("wdT", [EPC, II, D])
    sgT = din("sgT", [D, II])
    suT = din("suT", [D, II])
    sdT = din("sdT", [II, D])
    masks = din("masks", [4, 128, 512], BF16)
    iota32_bc = din("iota32_bc", [128, E])
    ident = din("ident", [128, 128], BF16)
    lsta = din("lsta", [2, 128, 64], BF16)
    lstaf = din("lstaf", [2, 128, 64])
    ones1f = din("ones1f", [64, 128])
    rk4 = din("rk4", [128, 1])
    qoff = din("qoff", [1, 1], U32)
    aoff = din("aoff", [1, 1], U32)
    iotacol = din("iotacol", [32, 1])
    out_slice = nc.dram_tensor("out_slice", [TPC, D], F32,
                               kind="ExternalOutput").ap()

    def dint(name, shape, dt):
        return nc.dram_tensor(name, shape, dt).ap()

    xbf_loc = dint("xbf_loc", [TPC, D], BF16)
    xbf_full = nc.dram_tensor("xbf_full", [T, D], BF16,
                              addr_space="Shared").ap()
    ctx_locj = [dint(f"ctx_loc{j}", [HPC * HD, 2048], BF16) for j in range(4)]
    ctxfull_h = nc.dram_tensor("ctxfull", [T, D], BF16, addr_space="Shared")
    ctxfull = ctxfull_h.ap()
    h1_dram = dint("h1_dram", [TPC, D], F32)
    h1loc = dint("h1loc", [TPC, D], BF16)
    h1full = nc.dram_tensor("h1full", [T, D], BF16,
                            addr_space="Shared").ap()
    idxloc = dint("idxloc", [TPC, K], F32)
    idxfull_h = nc.dram_tensor("idxfull", [T, K], F32)
    idxfull = idxfull_h.ap()
    posflat_h = nc.dram_tensor("posflat", [1, NA], F32)
    posflat = posflat_h.ap()
    shared_dram = dint("shared_dram", [TPC, D], BF16)
    bar_in = dint("bar_in", [128, 8], F32)
    bar_out = dint("bar_out", [128, 8], F32)
    xb_disp = dint("xb_disp", [EPC * CAP, D], BF16)
    yb_loc = dint("yb_loc", [EPC * CAP, D], BF16)
    ybfull = nc.dram_tensor("ybfull", [E * CAP, D], BF16,
                            addr_space="Shared").ap()

    plan = _Plan(nc)
    regs = {}

    def _alloc_regs():
        regs["disp"] = NCP.alloc_register("rb_disp")
        NCP.reg_mov(regs["disp"], EPC * CAP - 1)
        regs["ret"] = NCP.alloc_register("rb_ret")
        NCP.reg_mov(regs["ret"], E * CAP - 1)
    plan.ops["gpsimd"].append(_alloc_regs)

    sT = _Sem("sT")
    sA = _Sem("sA")
    sV = _Sem("sV")
    sP = _Sem("sP")
    dS = _Sem("dS", 16)
    dG = _Sem("dG", 16)
    dA = _Sem("dA", 16)
    sC = _Sem("sC")
    sems = [sT, sA, sV, sP, dS, dG, dA, sC]

    from contextlib import ExitStack
    stack = ExitStack()

    def SB(name, shape, dt):
        return stack.enter_context(nc.sbuf_tensor(name, shape, dt))

    def PS(name, shape, dt):
        return stack.enter_context(nc.psum_tensor(name, shape, dt))

    ppA = PS("ppA", [128, 1024], F32)
    ppB = PS("ppB", [128, 1024], F32)
    ppC = PS("ppC", [128, 512], F32)
    ppD = PS("ppD", [128, 512], F32)
    ppE = PS("ppE", [128, 512], F32)
    ppT = PS("ppT", [128, 1024], BF16)
    ppDbf = ppD[:, :].bitcast(BF16)

    def _tslot(i):
        bank = ppT if i % 2 == 0 else ppDbf
        sl = (i // 2) % 4
        return bank[:, sl * 128:(sl + 1) * 128]

    AR = SB("AR", [128, 49152], BF16)
    AF32 = SB("AF32", [128, 5 * D], F32)
    xTh = [SB(f"xTh{i}", [128, 8 * 512], BF16) for i in range(2)]
    wqb = SB("wqb", [128, 16 * 256], BF16)
    wkb = SB("wkb", [128, 16 * 256], BF16)
    wvb = SB("wvb", [128, 16 * 256], BF16)
    PTb = [SB(f"PT{i}", [128, 1024], BF16) for i in range(2)]
    PTraw = SB("PTraw", [128, 512], BF16)
    maskb = SB("maskb", [128, 4 * 512], BF16)
    expb = SB("expb", [128, 1], F32)
    lstab = SB("lstab", [128, 2 * 64], BF16)
    lstafb = SB("lstafb", [128, 2 * 64], F32)
    linv = SB("linv", [64, 512], F32)
    lbcs = SB("lbcs", [128, 512], F32)
    ctxst = SB("ctxst", [128, 1024], BF16)
    onesf = SB("onesf", [64, 128], F32)
    bob = SB("bob", [128, D], BF16)
    stat = SB("stat", [128, 16], F32)
    smal = SB("smal", [128, 512], F32)
    idxu = SB("idxu", [128, 512], U32)
    identb = SB("identb", [128, 128], BF16)
    iotacolb = SB("iotacolb", [32, 1], F32)
    iob = SB("iob", [128, 32], F32)
    gbb = SB("gbb", [128, 32], F32)
    rk4b = SB("rk4b", [128, 1], F32)
    aoffs = SB("aoffs", [1, 1], U32)
    arow = SB("arow", [1, 2 * 512], F32)
    posrow = SB("posrow", [1, 512], F32)
    routb = SB("routb", [128, 48], F32)
    routw = SB("routw", [128, 48], F32)
    weff = SB("weff", [128, 48], F32)
    topv = SB("topv", [128, 8], F32)
    idx8u = SB("idx8u", [128, 8], U32)
    idx8f = SB("idx8f", [128, 8], F32)
    carry32 = SB("carry32", [32, 1], F32)
    z32 = SB("z32", [32, 1], F32)

    lacc = [lbcs[:, :], smal[:, :]]
    QT = [AR[:, 0:8192], AR[:, 8192:16384]]
    KT = [AR[:, 16384:24576], AR[:, 24576:32768]]
    VT = [AR[:, 32768:40960], AR[:, 40960:49152]]
    Vn = VT
    wob = AR[:, 0:32768]
    ctxTmy = AR[:, 32768:49152]

    xcast = [AF32[:, 0:D], AF32[:, D:2 * D]]
    zbuf = [AF32[:, 2 * D:3 * D], AF32[:, 3 * D:4 * D]]
    sqscr = AF32[:, 4 * D:5 * D]
    xcb = [xTh[0][:, 0:2048], xTh[1][:, 0:2048]]
    h1hi = [xTh[0][:, 0:2048], xTh[0][:, 2048:4096],
            xTh[1][:, 0:2048], xTh[1][:, 2048:4096],
            wqb[:, 0:2048], wqb[:, 2048:4096],
            wkb[:, 0:2048], wkb[:, 2048:4096]]

    store_tok = {}
    for c in range(8):
        buf = c % 2
        plan.add("sync", [(sV, sV.n)] if c >= 2 else [],
                 (lambda c=c, buf=buf: NCY.dma_start(
                     out=xcast[buf], in_=x_slice[c * 128:(c + 1) * 128, :])),
                 (dS, dS.inc()))
        plan.add("vector",
                 [(dS, dS.n)] + ([(dG, store_tok[c - 2])] if c >= 2 else []),
                 (lambda buf=buf: NCV.tensor_copy(xcb[buf], xcast[buf])),
                 (sV, sV.inc()))
        plan.add("gpsimd", [(sV, sV.n)],
                 (lambda c=c, buf=buf: NCP.dma_start(
                     out=xbf_loc[c * 128:(c + 1) * 128, :], in_=xcb[buf])),
                 (dG, dG.inc()))
        store_tok[c] = dG.n
    plan.add("gpsimd", [(dG, dG.n)],
             (lambda: NCP.collective_compute(
                 "AllGather", ALU.bypass, replica_groups=[list(range(NCORES))],
                 ins=[xbf_loc], outs=[xbf_full])),
             (sC, sC.inc()))

    def _flush_barrier():
        plan.add("gpsimd", [],
                 (lambda: NCP.collective_compute(
                     "AllReduce", ALU.add,
                     replica_groups=[list(range(NCORES))],
                     ins=[bar_in], outs=[bar_out])),
                 (sC, sC.inc()))
        return sC.n

    xag_done = _flush_barrier()

    plan.add("vector", [], lambda: NCV.memset(expb[:, :], EXP_SHIFT),
             (sV, sV.inc()))
    plan.add("sync", [], lambda: NCY.dma_start(
        out=maskb[:, :].rearrange("p (j f) -> p j f", j=4),
        in_=masks.rearrange("j p f -> p j f")),
        (dS, dS.inc()))
    plan.add("sync", [], lambda: NCY.dma_start(
        out=lstab[:, :].rearrange("p (j f) -> p j f", j=2),
        in_=lsta.rearrange("j p f -> p j f")),
        (dS, dS.inc()))
    plan.add("sync", [], lambda: NCY.dma_start(
        out=lstafb[:, :].rearrange("p (j f) -> p j f", j=2),
        in_=lstaf.rearrange("j p f -> p j f")),
        (dS, dS.inc()))
    plan.add("sync", [], lambda: NCY.dma_start(out=onesf[:, :], in_=ones1f),
             (dS, dS.inc()))
    plan.add("sync", [], lambda: NCY.dma_start(out=identb[:, :], in_=ident),
             (dS, dS.inc()))
    ident_tok = dS.n
    plan.add("gpsimd", [], lambda: NCP.dma_start(out=bob[:, :], in_=bo_bc),
             (dG, dG.inc()))
    bob_done = dG.n
    consts_done = dS.n
    for (wsrc, wdst) in ((wq, wqb), (wk, wkb), (wv, wvb)):
        plan.add("gpsimd", [],
                 (lambda wsrc=wsrc, wdst=wdst: NCP.dma_start(
                     out=wdst[:, :].rearrange("p (c m) -> p c m", c=16),
                     in_=wsrc.rearrange("(c p) m -> p c m", p=128))),
                 (dG, dG.inc()))
    wcast_done = dG.n

    pd_list = [(wqb, 0, QT), (wqb, 1, QT), (wkb, 0, KT), (wkb, 1, KT),
               (wvb, 0, VT), (wvb, 1, VT)]
    pd_ps = [ppA[:, 0:512], ppA[:, 512:1024], ppB[:, 0:512],
             ppB[:, 512:1024], ppC[:, :], ppE[:, :]]
    half_mm_tok = {}
    evac_tok_p2 = 0
    vn_tok = {}
    vntr_ev = {}
    for tw in range(16):
        t0 = tw * 512
        for hc in range(2):
            hidx = tw * 2 + hc
            hb = hidx % 2
            w = [(sC, xag_done)] if hidx == 0 else []
            if hidx >= 2:
                w.append((sT, half_mm_tok[hidx - 2]))
            for chl in range(8):
                ch = hc * 8 + chl
                plan.add("sync", w if chl == 0 else [],
                         (lambda t0=t0, ch=ch, hb=hb, chl=chl: NCY.dma_start(
                             out=xTh[hb][:, chl * 512:(chl + 1) * 512],
                             in_=xbf_full[t0:t0 + 512,
                                          ch * 128:(ch + 1) * 128],
                             transpose=True)),
                         (dS, dS.inc()))
            xh_tok = dS.n
            for pdi, (wsb, dt_i, dstT) in enumerate(pd_list):
                for chl in range(8):
                    ch = hc * 8 + chl
                    wts = []
                    if pdi == 0 and chl == 0:
                        wts = [(dS, xh_tok), (dG, wcast_done)]
                        if hc == 0 and evac_tok_p2:
                            wts.append((sV, evac_tok_p2))
                    last = (pdi == 5 and chl == 7)
                    plan.add("tensor", wts,
                             (lambda wsb=wsb, dt_i=dt_i, ch=ch, chl=chl, hb=hb,
                                     pdi=pdi: NCT.matmul(
                                 pd_ps[pdi],
                                 wsb[:, ch * 256 + dt_i * 128:
                                     ch * 256 + (dt_i + 1) * 128],
                                 xTh[hb][:, chl * 512:(chl + 1) * 512],
                                 start=(ch == 0), stop=(ch == 15))),
                             (sT, sT.inc()) if last else None)
            half_mm_tok[hidx] = sT.n
        mm_tok = sT.n
        for pdi, (wsb, dt_i, dstT) in enumerate(pd_list):
            plan.add("vector", [(sT, mm_tok)] if pdi == 0 else [],
                     (lambda pdi=pdi, dt_i=dt_i, dstT=dstT, t0=t0:
                      NCV.tensor_copy(dstT[dt_i][:, t0:t0 + 512], pd_ps[pdi])),
                     (sV, sV.inc()))
        evac_tok_p2 = sV.n
        for h in range(HPC):
            for kb in range(4 * tw, 4 * tw + 4):
                i = tw * 8 + h * 4 + (kb - 4 * tw)
                wts2 = [(sV, evac_tok_p2)] if (h == 0 and kb == 4 * tw) else []
                if i < 2:
                    wts2.append((dS, ident_tok))
                if i >= 2:
                    wts2.append((sV, vntr_ev[i - 2]))
                plan.add("tensor", wts2,
                         (lambda h=h, kb=kb, i=i: NCT.transpose(
                             _tslot(i), VT[h][:, kb * 128:(kb + 1) * 128],
                             identb[:, :])),
                         (sT, sT.inc()))
                plan.add("vector", [(sT, sT.n)],
                         (lambda h=h, kb=kb, i=i: NCV.tensor_copy(
                             Vn[h][:, kb * 128:(kb + 1) * 128], _tslot(i))),
                         (sV, sV.inc()))
                vntr_ev[i] = sV.n
                vn_tok[(h, kb)] = sV.n
    qkv_V = sV.n

    last_ctx_dma = 0
    prev_flash_T = 0
    for s in range(8):
        for h in range(HPC):
            nkb = 8 * (s + 1)
            exp_tok = {}
            mask_tok = {}
            sS_tok = {}
            lacc_tok = {}
            ctx_started = [False, False]
            l_started = [False]
            first_of_stripe = True
            plan.add("vector", [(sV, sV.n), (sT, prev_flash_T)],
                     (lambda: NCV.memset(lacc[0], 0.0)), (sV, sV.inc()))
            plan.add("vector", [(sV, sV.n)],
                     (lambda: NCV.memset(lacc[1], 0.0)), (sV, sV.inc()))

            def active(kb, w, s=s):
                return kb <= 8 * s + 4 * w + 3

            for kb in range(nkb + 1):
                par = kb % 2
                sp = ppA if par == 0 else ppB
                if kb < nkb:
                    wts = []
                    if first_of_stripe:
                        wts = [(sV, vn_tok[(h, nkb - 1)]), (sA, sA.n),
                               (sV, sV.n)]
                        first_of_stripe = False
                    elif kb >= 2:
                        wts = [(sA, exp_tok[kb - 2])]
                    acts = [w for w in range(2) if active(kb, w)]
                    for j, w in enumerate(acts):
                        plan.add("tensor", wts if j == 0 else [],
                                 (lambda h=h, kb=kb, w=w, sp=sp, s=s:
                                  NCT.matmul(
                                     sp[:, w * 512:(w + 1) * 512],
                                     KT[h][:, kb * 128:(kb + 1) * 128],
                                     QT[h][:, 1024 * s + w * 512:
                                            1024 * s + (w + 1) * 512],
                                     start=True, stop=True)),
                                 (sT, sT.inc()) if j == len(acts) - 1 else None)
                    sS_tok[kb] = sT.n

                    sw = None
                    for w in acts:
                        if 0 <= kb - 8 * s - 4 * w <= 3:
                            sw = w
                    dws = [w for w in acts if w != sw]
                    swait = [(sT, sS_tok[kb])]
                    if (kb - 1) in mask_tok:
                        swait.append((sV, mask_tok[kb - 1]))
                    if (kb - 2) in lacc_tok:
                        swait.append((sV, lacc_tok[kb - 2]))
                    if dws:
                        lo, hi = min(dws), max(dws)
                        plan.add("scalar", swait,
                                 (lambda par=par, lo=lo, hi=hi, sp=sp:
                                  NCS.activation(
                                      PTb[par][:, lo * 512:(hi + 1) * 512],
                                      sp[:, lo * 512:(hi + 1) * 512],
                                      AF.Exp, bias=expb[:, 0:1],
                                      scale=ATT_SCALE)),
                                 (sA, sA.inc()) if sw is None else None)
                        swait = []
                    if sw is not None:
                        plan.add("scalar", swait,
                                 (lambda par=par, sw=sw, sp=sp: NCS.activation(
                                     PTraw[:, :],
                                     sp[:, sw * 512:(sw + 1) * 512],
                                     AF.Exp, bias=expb[:, 0:1],
                                     scale=ATT_SCALE)),
                                 (sA, sA.inc()))
                        j = kb - 8 * s - 4 * sw
                        plan.add("vector", [(sA, sA.n)],
                                 (lambda par=par, sw=sw, j=j: NCV.tensor_mul(
                                     PTb[par][:, sw * 512:(sw + 1) * 512],
                                     PTraw[:, :],
                                     maskb[:, j * 512:(j + 1) * 512])),
                                 (sV, sV.inc()))
                        mask_tok[kb] = sV.n
                    exp_tok[kb] = sA.n

                if kb >= 1:
                    kprev = kb - 1
                    parp = kprev % 2
                    actsp = [w for w in range(2) if active(kprev, w)]
                    wts = [(sA, exp_tok[kprev])]
                    if kprev in mask_tok:
                        wts.append((sV, mask_tok[kprev]))
                    first = True
                    for w in actsp:
                        ctx_ps = ppC if w == 0 else ppD
                        is_last_ctx = (kprev == 8 * s + 4 * w + 3)
                        plan.add("tensor", wts if first else [],
                                 (lambda h=h, kprev=kprev, w=w, parp=parp,
                                         ctx_ps=ctx_ps, st=not ctx_started[w],
                                         sp_=is_last_ctx: NCT.matmul(
                                     ctx_ps[:, :],
                                     Vn[h][:, kprev * 128:(kprev + 1) * 128],
                                     PTb[parp][:, w * 512:(w + 1) * 512],
                                     start=st, stop=sp_)),
                                 (sT, sT.inc()) if w == actsp[-1] else None)
                        first = False
                        ctx_started[w] = True
                        if kprev % 2 == 0 and L_MODE == "split":
                            vw = [(sA, exp_tok[kprev]), (sV, sV.n)]
                            if kprev in mask_tok:
                                vw.append((sV, mask_tok[kprev]))
                            plan.add("vector", vw,
                                     (lambda w=w, parp=parp: NCV.tensor_add(
                                         lacc[w], lacc[w],
                                         PTb[parp][:, w * 512:(w + 1) * 512])),
                                     (sV, sV.inc()))
                            lacc_tok[kprev] = sV.n
                        else:
                            plan.add("tensor", [],
                                     (lambda kprev=kprev, w=w, parp=parp,
                                             st=not l_started[0]:
                                      NCT.matmul(
                                          ppE[0:64, :],
                                          lstab[:, w * 64:(w + 1) * 64],
                                          PTb[parp][:, w * 512:(w + 1) * 512],
                                          start=st, stop=False)))
                            l_started[0] = True
            flash_T = sT.n
            lacc_last = sV.n
            for w in range(2):
                plan.add("tensor", [(sV, lacc_last)] if w == 0 else [],
                         (lambda w=w, st=(w == 0 and not l_started[0]):
                          NCT.matmul(
                             ppE[0:64, :], lstafb[:, w * 64:(w + 1) * 64],
                             lacc[w], start=st, stop=(w == 1))),
                         (sT, sT.inc()))
            flash_T = sT.n
            prev_flash_T = flash_T

            plan.add("vector", [(sT, flash_T), (dG, last_ctx_dma)],
                     (lambda: NCV.reciprocal(linv[:, :], ppE[0:64, :])),
                     (sV, sV.inc()))
            for w in range(2):
                plan.add("tensor", [(sV, sV.n)],
                         (lambda w=w: NCT.matmul(
                             ppA[:, 0:512], onesf[32 * w:32 * w + 1, :],
                             linv[32 * w:32 * w + 1, :],
                             start=True, stop=True)),
                         (sT, sT.inc()))
                plan.add("vector", [(sT, sT.n)],
                         (lambda: NCV.tensor_copy(lbcs[:, :], ppA[:, 0:512])),
                         (sV, sV.inc()))
                ctx_ps = ppC if w == 0 else ppD
                plan.add("vector", [(sV, sV.n)],
                         (lambda w=w, ctx_ps=ctx_ps: NCV.tensor_mul(
                             ctxst[:, w * 512:(w + 1) * 512],
                             ctx_ps[:, :], lbcs[:, :])),
                         (sV, sV.inc()))
            evac_tok = sV.n
            plan.add("gpsimd", [(sV, evac_tok)],
                     (lambda s=s, h=h: NCP.dma_start(
                         out=ctx_locj[s // 2][h * 128:(h + 1) * 128,
                                             (s % 2) * 1024:
                                             (s % 2) * 1024 + 1024],
                         in_=ctxst[:, :])),
                     (dG, dG.inc()))
            last_ctx_dma = dG.n
            plan.wait("tensor", [(sV, evac_tok)])
        if s % 2 == 1 and (CTX_AG_CHUNKS == 4 or s == 7):
            js = [s // 2] if CTX_AG_CHUNKS == 4 else [0, 1, 2, 3]
            for j in js:
                plan.add("gpsimd", [(dG, last_ctx_dma)],
                         (lambda j=j: NCP.collective_compute(
                             "AllGather", ALU.bypass,
                             replica_groups=[list(range(NCORES))],
                             ins=[ctx_locj[j]],
                             outs=[ctxfull[j * 2048:(j + 1) * 2048, :]])),
                         (sC, sC.inc()))
    ctx_stored = dG.n
    ctxag_done = _flush_barrier()
    flash_V = sV.n

    qoffs = SB("qoffs", [1, 1], U32)
    plan.add("sync", [], lambda: NCY.dma_start(out=qoffs[:, :], in_=qoff),
             (dS, dS.inc()))
    qoff_tok = dS.n

    def _ld_ctxTmy():
        NCP.wait_ge(dS.h, qoff_tok)
        NCP.wait_ge(sC.h, ctxag_done)
        with NCP.register("rq") as rq:
            NCP.reg_load(rq, qoffs[0:1, 0:1])
            return NCP.dma_start(
                out=ctxTmy.rearrange("p (c m) -> p c m", c=16),
                in_=bass.AP(ctxfull_h, rq,
                            [[2048, 128], [2048 * 128, 16], [1, 1024]]))
    plan.add("gpsimd", [], _ld_ctxTmy, (dG, dG.inc()))
    ctxmy_ready = dG.n
    plan.add("gpsimd", [(dG, ctx_stored)],
             (lambda: NCP.dma_start(
                 out=wob.rearrange("p (c m) -> p c m", c=16),
                 in_=wo.rearrange("(c p) m -> p c m", p=128))),
             (dG, dG.inc()))
    wo_ready = dG.n

    h1hi_tok = [0] * 8
    h1_spill = 0
    for c in range(8):
        buf = c % 2
        plan.add("sync", [],
                 (lambda c=c, buf=buf: NCY.dma_start(
                     out=xcast[buf], in_=x_slice[c * 128:(c + 1) * 128, :])),
                 (dS, dS.inc()))
        xrow_tok = dS.n
        wts = [(dG, max(ctxmy_ready, 0)), (dG, wo_ready)]
        if c == 0:
            wts += [(sA, sA.n), (sV, sV.n)]
        for ch in range(16):
            for nhalf in range(2):
                ps = ppA if nhalf == 0 else ppB
                for nn in range(2):
                    plan.add("tensor",
                             wts if (ch == 0 and nhalf == 0 and nn == 0)
                             else [],
                             (lambda c=c, ch=ch, nhalf=nhalf, nn=nn, ps=ps:
                              NCT.matmul(
                                  ps[:, nn * 512:(nn + 1) * 512],
                                  ctxTmy[:, ch * 1024 + c * 128:
                                         ch * 1024 + (c + 1) * 128],
                                  wob[:, ch * 2048 + nhalf * 1024 + nn * 512:
                                      ch * 2048 + nhalf * 1024 +
                                      (nn + 1) * 512],
                                  start=(ch == 0), stop=(ch == 15))),
                             (sT, sT.inc()) if (ch == 15 and nhalf == 1 and
                                                nn == 1) else None)
            wts = []
        att_tok = sT.n
        for nhalf in range(2):
            ps = ppA if nhalf == 0 else ppB
            plan.add("vector",
                     [(sT, att_tok), (dS, xrow_tok)] +
                     ([(dS, consts_done)] if c == 0 else []),
                     (lambda buf=buf, nhalf=nhalf, ps=ps:
                      NCV.scalar_tensor_tensor(
                          zbuf[buf][:, nhalf * 1024:(nhalf + 1) * 1024],
                          ps[:, :], 1.0,
                          xcast[buf][:, nhalf * 1024:(nhalf + 1) * 1024],
                          ALU.mult, ALU.add)),
                     (sV, sV.inc()))
        plan.add("vector", [(sV, sV.n)] + ([(dG, bob_done)] if c == 0 else []),
                 (lambda buf=buf: NCV.scalar_tensor_tensor(
                     zbuf[buf], zbuf[buf], 1.0, bob[:, :],
                     ALU.mult, ALU.add, accum_out=stat[:, 0:1])),
                 (sV, sV.inc()))
        z_tok = sV.n
        plan.wait("tensor", [(sV, z_tok)])
        plan.add("scalar", [(sV, z_tok)],
                 (lambda buf=buf: NCS.activation(
                     sqscr, zbuf[buf], AF.Square, accum_out=stat[:, 1:2])),
                 (sA, sA.inc()))
        sq_tok = sA.n
        plan.add("vector", [(sA, sq_tok)],
                 (lambda: NCV.tensor_scalar(stat[:, 2:3], stat[:, 0:1],
                                            1.0 / D, 0.0, ALU.mult, ALU.add)),
                 (sV, sV.inc()))
        plan.add("vector", [(sV, sV.n)],
                 (lambda: NCV.tensor_scalar(stat[:, 3:4], stat[:, 1:2],
                                            1.0 / D, 0.0, ALU.mult, ALU.add)),
                 (sV, sV.inc()))
        plan.add("vector", [(sV, sV.n)],
                 (lambda: NCV.tensor_mul(stat[:, 4:5], stat[:, 2:3],
                                         stat[:, 2:3])),
                 (sV, sV.inc()))
        plan.add("vector", [(sV, sV.n)],
                 (lambda: NCV.tensor_sub(stat[:, 5:6], stat[:, 3:4],
                                         stat[:, 4:5])),
                 (sV, sV.inc()))
        plan.add("vector", [(sV, sV.n)],
                 (lambda: NCV.tensor_scalar(stat[:, 6:7], stat[:, 5:6],
                                            1.0, EPS, ALU.mult, ALU.add)),
                 (sV, sV.inc()))
        var_tok = sV.n
        plan.add("scalar", [(sV, var_tok)],
                 (lambda: NCS.activation(stat[:, 7:8], stat[:, 6:7], AF.Ln)),
                 (sA, sA.inc()))
        plan.add("scalar", [(sA, sA.n)],
                 (lambda: NCS.activation(stat[:, 8:9], stat[:, 7:8], AF.Exp,
                                         scale=-0.5)),
                 (sA, sA.inc()))
        rstd_tok = sA.n
        plan.add("vector", [(sA, rstd_tok)],
                 (lambda buf=buf: NCV.tensor_scalar(
                     zbuf[buf], zbuf[buf], stat[:, 2:3], stat[:, 8:9],
                     ALU.subtract, ALU.mult)),
                 (sV, sV.inc()))
        h1_tok = sV.n
        plan.add("vector", [(sV, h1_tok)],
                 (lambda c=c, buf=buf: NCV.tensor_copy(h1hi[c], zbuf[buf])),
                 (sV, sV.inc()))
        h1hi_tok[c] = sV.n
        plan.add("sync", [(sV, sV.n)],
                 (lambda c=c, buf=buf: NCY.dma_start(
                     out=h1_dram[c * 128:(c + 1) * 128, :], in_=zbuf[buf])),
                 (dS, dS.inc()))
        h1_spill = dS.n
    h1_V = sV.n

    if stop_after == "h1":
        for c in range(8):
            plan.add("gpsimd", [(dS, h1_spill)] if c == 0 else [],
                     (lambda c=c: NCP.dma_start(
                         out=out_slice[c * 128:(c + 1) * 128, :],
                         in_=h1_dram[c * 128:(c + 1) * 128, :])),
                     (dG, dG.inc()))
        plan.wait("gpsimd", [(dG, dG.n)])
        _finalize(nc, plan, sems, stack)
        return nc

    h1T = AR[:, 0:16384]
    wstr = AR[:, 16384:32768]
    xbT = AR[:, 32768:45056]
    gsil = AR[:, 0:6144]
    intT = AR[:, 6144:12288]
    shg = AR[:, 32768:40960]
    siT = AR[:, 40960:49152]
    obit1 = AR[0:32, 16384:32768].bitcast(F32)
    obit2 = AR[0:32, 32768:49152].bitcast(F32)
    Obuf = obit1[:, 0:4096]
    prodb = obit1[:, 4096:8192]
    cumb = obit2[:, 0:4096]
    loT = maskb[:, 0:2048]
    lo_nat = [PTb[0], PTb[1]]
    gwhib = ctxst[:, 0:512]
    gwlob = ctxst[:, 512:1024]
    gbuf = [wvb[:, 0:2048], wvb[:, 2048:4096]]
    ybev = maskb[:, 0:2048]
    ebuf = AF32[:, 8192:8576]
    pbuf = AF32[:, 8576:8960]
    m3s1 = AF32[:, 8960:9344]
    m3s2 = AF32[:, 9344:9728]
    m3s3 = AF32[:, 9728:10112]
    sq2 = AF32[:, 8192:10240]
    pmy = lbcs[:, 0:48]
    vldm = lbcs[:, 48:96]
    t1v = lbcs[:, 96:144]
    e4f = lbcs[:, 144:192]
    rowf = lbcs[:, 192:240]
    qv = lbcs[:, 240:288]
    zfin = xcast[1]

    p4_T = sT.n
    for c in range(8):
        plan.add("gpsimd", [(sV, h1hi_tok[c])],
                 (lambda c=c: NCP.dma_start(
                     out=h1loc[c * 128:(c + 1) * 128, :], in_=h1hi[c])),
                 (dG, dG.inc()))
    plan.add("gpsimd", [(dG, dG.n)],
             (lambda: NCP.collective_compute(
                 "AllGather", ALU.bypass, replica_groups=[list(range(NCORES))],
                 ins=[h1loc], outs=[h1full])),
             (sC, sC.inc()))
    h1ag_done = _flush_barrier()

    if stop_after == "m0a":
        plan.add("gpsimd", [(sC, h1ag_done)],
                 (lambda: NCP.dma_start(out=out_slice[0:1024, :],
                                        in_=h1full[3584:4608, :])),
                 (dG, dG.inc()))
        plan.wait("gpsimd", [(dG, dG.n)])
        _finalize(nc, plan, sems, stack)
        return nc

    plan.add("sync", [], lambda: NCY.dma_start(out=iotacolb[:, :], in_=iotacol),
             (dS, dS.inc()))
    plan.add("sync", [], lambda: NCY.dma_start(out=iob[:, :], in_=iota32_bc),
             (dS, dS.inc()))
    plan.add("sync", [], lambda: NCY.dma_start(out=gbb[:, :], in_=gbias_bc),
             (dS, dS.inc()))
    plan.add("sync", [], lambda: NCY.dma_start(out=rk4b[:, :], in_=rk4),
             (dS, dS.inc()))
    plan.add("sync", [], lambda: NCY.dma_start(out=aoffs[:, :], in_=aoff),
             (dS, dS.inc()))
    mconsts = dS.n
    plan.add("sync", [(sV, flash_V)], lambda: NCY.dma_start(
        out=smal[:, :], in_=gwT.rearrange("(c p) m -> p c m", p=128)),
        (dS, dS.inc()))
    gwf_tok = dS.n
    plan.add("vector", [(dS, gwf_tok), (sV, qkv_V)],
             (lambda: NCV.tensor_copy(gwhib, smal[:, :])), (sV, sV.inc()))
    plan.add("vector", [(sV, sV.n)],
             (lambda: NCV.tensor_sub(gwlob, smal[:, :], gwhib)),
             (sV, sV.inc()))
    gw_ready = sV.n

    if stop_after == "m0b":
        plan.add("sync", [(sV, gw_ready), (dS, mconsts)],
                 (lambda: NCY.dma_start(out=out_slice[0:128, 0:512],
                                        in_=smal[:, :])),
                 (dS, dS.inc()))
        plan.wait("gpsimd", [(dS, dS.n), (sC, h1ag_done)])
        _finalize(nc, plan, sems, stack)
        return nc

    tr_ev = {}
    for c in range(8):
        for ch in range(16):
            i = c * 16 + ch
            wts = [(sV, h1hi_tok[c]), (dS, mconsts)]
            if i >= 2:
                wts.append((sV, tr_ev[i - 2]))
            plan.add("tensor", wts,
                     (lambda c=c, ch=ch, i=i: NCT.transpose(
                         _tslot(i),
                         h1hi[c][:, ch * 128:(ch + 1) * 128], identb[:, :])),
                     (sT, sT.inc()))
            plan.add("vector", [(sT, sT.n)] + ([(sT, p4_T)] if i == 0 else []),
                     (lambda c=c, ch=ch, i=i: NCV.tensor_copy(
                         h1T[:, ch * 1024 + c * 128:ch * 1024 + (c + 1) * 128],
                         _tslot(i))),
                     (sV, sV.inc()))
            tr_ev[i] = sV.n
    h1T_done = sV.n

    if stop_after == "m0":
        plan.add("gpsimd", [(sV, h1T_done), (sC, h1ag_done)],
                 (lambda: NCP.dma_start(out=out_slice[0:128, :],
                                        in_=h1T[:, 0:2048])),
                 (dG, dG.inc()))
        plan.wait("gpsimd", [(dG, dG.n)])
        _finalize(nc, plan, sems, stack)
        return nc

    exp_prev = 0
    lt_ev = {}
    for c in range(8):
        buf = c % 2
        plan.add("sync", [(sV, sV.n)],
                 (lambda c=c, buf=buf: NCY.dma_start(
                     out=xcast[buf], in_=h1_dram[c * 128:(c + 1) * 128, :])),
                 (dS, dS.inc()))
        h1f_tok = dS.n
        plan.add("vector", [(dS, h1f_tok)],
                 (lambda c=c, buf=buf: NCV.tensor_sub(
                     zbuf[buf], xcast[buf], h1hi[c])),
                 (sV, sV.inc()))
        plan.add("vector", [(sV, sV.n)],
                 (lambda buf=buf: NCV.tensor_copy(
                     lo_nat[0][:, :], zbuf[buf][:, 0:1024])),
                 (sV, sV.inc()))
        plan.add("vector", [(sV, sV.n)],
                 (lambda buf=buf: NCV.tensor_copy(
                     lo_nat[1][:, :], zbuf[buf][:, 1024:2048])),
                 (sV, sV.inc()))
        lo_tok = sV.n
        for ch in range(16):
            src = lo_nat[0][:, ch * 128:(ch + 1) * 128] if ch < 8 else \
                lo_nat[1][:, (ch - 8) * 128:(ch - 8 + 1) * 128]
            key = c * 16 + ch
            wts = [(sV, lo_tok)]
            if key >= 2:
                wts.append((sV, lt_ev[key - 2]))
            else:
                wts.append((sV, h1T_done))
            plan.add("tensor", wts,
                     (lambda src=src, key=key: NCT.transpose(
                         _tslot(key), src, identb[:, :])),
                     (sT, sT.inc()))
            plan.add("vector", [(sT, sT.n)],
                     (lambda ch=ch, key=key: NCV.tensor_copy(
                         loT[:, ch * 128:(ch + 1) * 128], _tslot(key))),
                     (sV, sV.inc()))
            lt_ev[key] = sV.n
        loT_tok = sV.n
        nmm = 0
        for ps in range(3):
            for ch in range(16):
                wts = []
                if nmm == 0:
                    wts = [(sV, loT_tok), (sV, gw_ready), (sV, h1T_done),
                           (sA, exp_prev)]
                lhs = (h1T[:, ch * 1024 + c * 128:ch * 1024 + (c + 1) * 128]
                       if ps < 2 else loT[:, ch * 128:(ch + 1) * 128])
                rhs = (gwlob if ps == 1 else gwhib)[:, ch * 32:(ch + 1) * 32]
                plan.add("tensor", wts,
                         (lambda lhs=lhs, rhs=rhs, nmm=nmm: NCT.matmul(
                             ppC[:, 0:32], lhs, rhs,
                             start=(nmm == 0), stop=(nmm == 47))),
                         (sT, sT.inc()) if nmm == 47 else None)
                nmm += 1
        gate_T = sT.n
        plan.add("scalar", [(sT, gate_T), (dS, mconsts)],
                 (lambda: NCS.activation(smal[:, 0:32], ppC[:, 0:32], AF.Exp)),
                 (sA, sA.inc()))
        exp_prev = sA.n
        plan.add("vector", [(sA, sA.n)],
                 (lambda: NCV.tensor_scalar(smal[:, 32:64], smal[:, 0:32],
                                            1.0, 1.0, ALU.mult, ALU.add)),
                 (sV, sV.inc()))
        plan.add("scalar", [(sV, sV.n)],
                 (lambda: NCS.activation(smal[:, 64:96], smal[:, 32:64],
                                         AF.Ln)),
                 (sA, sA.inc()))
        plan.add("scalar", [(sA, sA.n)],
                 (lambda: NCS.activation(smal[:, 96:128], smal[:, 64:96],
                                         AF.Ln)),
                 (sA, sA.inc()))
        plan.add("scalar", [(sA, sA.n)],
                 (lambda: NCS.activation(smal[:, 128:160], smal[:, 96:128],
                                         AF.Exp, scale=0.5)),
                 (sA, sA.inc()))
        sc_tok = sA.n
        plan.add("vector", [(sA, sc_tok)],
                 (lambda: NCV.tensor_add(smal[:, 160:192], smal[:, 128:160],
                                         gbb[:, :])),
                 (sV, sV.inc()))
        plan.add("vector", [(sV, sV.n)],
                 (lambda: NCV.max(topv[:, :], smal[:, 160:192])),
                 (sV, sV.inc()))
        plan.add("vector", [(sV, sV.n)],
                 (lambda: NCV.max_index(idx8u[:, :], topv[:, :],
                                        smal[:, 160:192])),
                 (sV, sV.inc()))
        plan.add("vector", [(sV, sV.n)],
                 (lambda: NCV.tensor_copy(idx8f[:, :], idx8u[:, :])),
                 (sV, sV.inc()))
        plan.add("vector", [(sV, sV.n)],
                 (lambda c=c: NCV.tensor_copy(routb[:, c * 6:(c + 1) * 6],
                                              idx8f[:, 0:6])),
                 (sV, sV.inc()))
        for k in range(K):
            plan.add("vector", [(sV, sV.n)],
                     (lambda c=c, k=k: NCV.scalar_tensor_tensor(
                         smal[:, 192:224], iob[:, :], idx8f[:, k:k + 1],
                         smal[:, 128:160], ALU.is_equal, ALU.mult,
                         accum_out=routw[:, c * 6 + k:c * 6 + k + 1])),
                     (sV, sV.inc()))
        plan.add("vector", [(sV, sV.n)],
                 (lambda c=c: NCV.reduce_sum(stat[:, 9:10],
                                             routw[:, c * 6:(c + 1) * 6],
                                             axis=AX.X)),
                 (sV, sV.inc()))
        plan.add("vector", [(sV, sV.n)],
                 (lambda: NCV.reciprocal(stat[:, 10:11], stat[:, 9:10])),
                 (sV, sV.inc()))
        plan.add("vector", [(sV, sV.n)],
                 (lambda c=c: NCV.tensor_scalar(
                     routw[:, c * 6:(c + 1) * 6],
                     routw[:, c * 6:(c + 1) * 6],
                     stat[:, 10:11], MSCALE, ALU.mult, ALU.mult)),
                 (sV, sV.inc()))
        plan.add("gpsimd", [(sV, sV.n)],
                 (lambda c=c: NCP.dma_start(
                     out=idxloc[c * 128:(c + 1) * 128, :],
                     in_=routb[:, c * 6:(c + 1) * 6])),
                 (dG, dG.inc()))
    routw_tok = sV.n
    plan.add("gpsimd", [(dG, dG.n)],
             (lambda: NCP.collective_compute(
                 "AllGather", ALU.bypass, replica_groups=[list(range(NCORES))],
                 ins=[idxloc], outs=[idxfull])),
             (sC, sC.inc()))
    idxag_done = sC.n

    if stop_after == "m1":
        plan.add("vector", [(sV, routw_tok)],
                 (lambda: NCV.tensor_copy(zbuf[0][:, 0:48], routb[:, :])),
                 (sV, sV.inc()))
        plan.add("vector", [(sV, sV.n)],
                 (lambda: NCV.tensor_copy(zbuf[0][:, 48:96], routw[:, :])),
                 (sV, sV.inc()))
        plan.add("sync", [(sV, sV.n), (sC, idxag_done)],
                 (lambda: NCY.dma_start(out=out_slice[0:128, :],
                                        in_=zbuf[0])),
                 (dS, dS.inc()))
        plan.wait("gpsimd", [(dS, dS.n)])
        _finalize(nc, plan, sems, stack)
        return nc

    plan.add("vector", [(sV, sV.n)], lambda: NCV.memset(z32[:, :], 0.0),
             (sV, sV.inc()))
    plan.add("vector", [(sV, sV.n)], lambda: NCV.memset(carry32[:, :], 0.0),
             (sV, sV.inc()))
    CHW = 4096
    NCH = NA // CHW
    pr_tok = [0, 0]
    for chn in range(NCH):
        a0 = chn * CHW
        for n in range(8):
            bps = ppC if n % 2 == 0 else ppD
            ab = n % 2
            plan.add("sync",
                     ([(sC, idxag_done)] if chn == 0 and n == 0 else []) +
                     ([(sT, sT.n)] if (chn > 0 or n >= 2) else []),
                     (lambda a0=a0, n=n, ab=ab: NCY.dma_start(
                         out=arow[:, ab * 512:(ab + 1) * 512],
                         in_=bass.AP(idxfull_h, a0 + n * 512,
                                     [[512, 1], [1, 512]]))),
                     (dS, dS.inc()))
            arow_tok = dS.n
            wts = [(dS, arow_tok)]
            if chn == 0 and n == 0:
                wts.append((sA, exp_prev))
            plan.add("tensor", wts,
                     (lambda n=n, bps=bps, ab=ab: NCT.matmul(
                         bps[0:32, 0:512], onesf[0:1, 0:32],
                         arow[:, ab * 512:(ab + 1) * 512],
                         start=True, stop=True)),
                     (sT, sT.inc()))
            plan.add("vector", [(sT, sT.n)] +
                     ([(sT, p4_T), (sV, h1T_done)] if chn == 0 and n == 0
                      else []),
                     (lambda n=n, bps=bps: NCV.tensor_single_scalar(
                         Obuf[:, n * 512:(n + 1) * 512], bps[0:32, 0:512],
                         iotacolb[:, 0:1], ALU.is_equal)),
                     (sV, sV.inc()))
            plan.wait("tensor", [(sV, sV.n)])
        ob_tok = sV.n
        plan.add("vector", [(sV, ob_tok)],
                 (lambda: NCV.tensor_tensor_scan(
                     cumb[:, :], Obuf[:, :],
                     z32[:, :].broadcast_to([32, CHW]),
                     carry32[:, 0:1], ALU.add, ALU.add)),
                 (sV, sV.inc()))
        plan.add("vector", [(sV, sV.n)],
                 (lambda: NCV.tensor_copy(carry32[:, :],
                                          cumb[:, CHW - 1:CHW])),
                 (sV, sV.inc()))
        plan.add("vector", [(sV, sV.n)],
                 (lambda: NCV.tensor_mul(prodb[:, :], Obuf[:, :], cumb[:, :])),
                 (sV, sV.inc()))
        prod_tok = sV.n
        for n in range(8):
            ab = n % 2
            wts = [(sV, prod_tok)] if n == 0 else []
            plan.add("tensor", wts,
                     (lambda n=n: NCT.matmul(
                         ppE[0:1, 0:512], onesf[0:32, 0:1],
                         prodb[:, n * 512:(n + 1) * 512],
                         start=True, stop=True)),
                     (sT, sT.inc()))
            plan.add("vector", [(sT, sT.n)] +
                     ([(dS, pr_tok[0])] if pr_tok[0] else []),
                     (lambda n=n: NCV.tensor_copy(
                         posrow[:, 0:512], ppE[0:1, 0:512])),
                     (sV, sV.inc()))
            plan.wait("tensor", [(sV, sV.n)])
            plan.add("sync", [(sV, sV.n)],
                     (lambda a0=a0, n=n: NCY.dma_start(
                         out=bass.AP(posflat_h, a0 + n * 512,
                                     [[512, 1], [1, 512]]),
                         in_=posrow[:, 0:512])),
                     (dS, dS.inc()))
            pr_tok[0] = dS.n
    pos_done = dS.n

    plan.add("sync", [(dS, pos_done)],
             (lambda: NCY.dma_start(
                 out=ebuf,
                 in_=bass.AP(idxfull_h, 0, [[6, 128], [768, 64], [1, 6]]))),
             (dS, dS.inc()))
    plan.add("sync", [],
             (lambda: NCY.dma_start(
                 out=pbuf,
                 in_=bass.AP(posflat_h, 0, [[6, 128], [768, 64], [1, 6]]))),
             (dS, dS.inc()))
    ep_tok = dS.n
    plan.add("vector", [(dS, mconsts), (sV, sV.n)],
             (lambda: NCV.tensor_scalar(stat[:, 11:12], rk4b[:, :],
                                        1.0, 3.0, ALU.mult, ALU.add)),
             (sV, sV.inc()))
    plan.add("vector", [(sV, sV.n)],
             (lambda: NCV.tensor_scalar(stat[:, 12:13], rk4b[:, :],
                                        768.0, 1.0, ALU.mult, ALU.add)),
             (sV, sV.inc()))
    plan.add("vector", [(sV, sV.n), (dS, ep_tok)],
             (lambda: NCV.tensor_single_scalar(m3s1, pbuf, 768.0, ALU.is_le)),
             (sV, sV.inc()))
    plan.add("vector", [(sV, sV.n)],
             (lambda: NCV.tensor_scalar(m3s2, ebuf, rk4b[:, 0:1], None,
                                        ALU.is_ge)),
             (sV, sV.inc()))
    plan.add("vector", [(sV, sV.n)],
             (lambda: NCV.tensor_mul(m3s1, m3s1, m3s2)),
             (sV, sV.inc()))
    plan.add("vector", [(sV, sV.n)],
             (lambda: NCV.tensor_scalar(m3s2, ebuf, stat[:, 11:12], None,
                                        ALU.is_le)),
             (sV, sV.inc()))
    plan.add("vector", [(sV, sV.n)],
             (lambda: NCV.tensor_mul(m3s1, m3s1, m3s2)),
             (sV, sV.inc()))
    sel_tok = sV.n
    plan.add("vector", [(sV, sV.n)],
             (lambda: NCV.scalar_tensor_tensor(m3s2, ebuf, 768.0, pbuf,
                                               ALU.mult, ALU.add)),
             (sV, sV.inc()))
    plan.add("vector", [(sV, sV.n)],
             (lambda: NCV.tensor_scalar(m3s2, m3s2, stat[:, 12:13], None,
                                        ALU.subtract)),
             (sV, sV.inc()))
    BIGS = float(1 << 20)
    plan.add("vector", [(sV, sV.n)],
             (lambda: NCV.tensor_scalar(m3s2, m3s2, 1.0, -BIGS,
                                        ALU.mult, ALU.add)),
             (sV, sV.inc()))
    plan.add("vector", [(sV, sV.n)],
             (lambda: NCV.tensor_mul(m3s2, m3s2, m3s1)),
             (sV, sV.inc()))
    plan.add("vector", [(sV, sV.n)],
             (lambda: NCV.tensor_scalar(m3s2, m3s2, 1.0, BIGS,
                                        ALU.mult, ALU.add)),
             (sV, sV.inc()))
    plan.add("vector", [(sV, sV.n)],
             (lambda: NCV.tensor_copy(idxu[:, 0:384], m3s2)),
             (sV, sV.inc()))
    idxd_tok = sV.n

    if stop_after == "m3":
        plan.add("vector", [(sV, idxd_tok)],
                 (lambda: NCV.tensor_copy(zbuf[0][:, 0:48], routb[:, :])),
                 (sV, sV.inc()))
        plan.add("vector", [(sV, sV.n)],
                 (lambda: NCV.tensor_copy(zbuf[0][:, 48:96], routw[:, :])),
                 (sV, sV.inc()))
        plan.add("vector", [(sV, sV.n)],
                 (lambda: NCV.tensor_copy(zbuf[0][:, 96:480], m3s2)),
                 (sV, sV.inc()))
        plan.add("sync", [(sV, sV.n)],
                 (lambda: NCY.dma_start(out=out_slice[0:128, :],
                                        in_=zbuf[0])),
                 (dS, dS.inc()))
        plan.add("sync", [(dS, pos_done)],
                 (lambda: NCY.dma_start(
                     out=out_slice.rearrange("a b -> (a b)")[
                         300000:300000 + NA].rearrange("(a b) -> a b", a=1),
                     in_=posflat)),
                 (dS, dS.inc()))
        plan.wait("gpsimd", [(dS, dS.n)])
        _finalize(nc, plan, sems, stack)
        return nc

    sc_tok = {}
    for c64 in range(64):
        db = c64 % 2
        wts = [(sC, h1ag_done)]
        if c64 >= 1:
            wts.append((dG, sc_tok[c64 - 1]))
        plan.add("sync", wts,
                 (lambda c64=c64, db=db: NCY.dma_start(
                     out=gbuf[db][:, :],
                     in_=h1full[c64 * 128:(c64 + 1) * 128, :])),
                 (dS, dS.inc()))
        src_tok = dS.n
        for k in range(K):
            wts = [(dS, src_tok)] if k == 0 else []
            if c64 == 0 and k == 0:
                wts.append((sV, idxd_tok))
            if k == 0 and c64 >= 1:
                wts.append((dG, sc_tok[c64 - 1]))
            plan.add("gpsimd", wts,
                     (lambda c64=c64, k=k, db=db: NCP.indirect_dma_start(
                         xb_disp,
                         IndirectOffsetOnAxis(idxu[:, c64 * 6 + k:
                                                   c64 * 6 + k + 1], 0),
                         gbuf[db][:, :], None,
                         bounds_check=regs["disp"], oob_is_err=False)),
                     (dG, dG.inc()))
        sc_tok[c64] = dG.n
    disp_done = dG.n

    if stop_after == "m4":
        plan.add("gpsimd", [(dG, disp_done)],
                 (lambda: NCP.dma_start(out=out_slice,
                                        in_=xb_disp[0:TPC, :])),
                 (dG, dG.inc()))
        plan.wait("gpsimd", [(dG, dG.n)])
        _finalize(nc, plan, sems, stack)
        return nc

    m3_done_T = sT.n

    def _wload(srcT, nch, wid, extra_w=()):
        toks = []
        for ch in range(nch):
            bufw = ch % 2
            wts = list(extra_w) if ch == 0 else []
            if ch >= 2:
                wts.append((sV, toks[ch - 2]))
            plan.add("sync", wts + ([(sV, sV.n)] if ch < 2 else []),
                     (lambda srcT=srcT, ch=ch, bufw=bufw: NCY.dma_start(
                         out=xcast[bufw][:, 0:wid] if wid <= D else None,
                         in_=srcT[ch * 128:(ch + 1) * 128, :])),
                     (dS, dS.inc()))
            plan.add("vector", [(dS, dS.n)],
                     (lambda ch=ch, bufw=bufw: NCV.tensor_copy(
                         wstr[:, ch * wid:(ch + 1) * wid],
                         xcast[bufw][:, 0:wid])),
                     (sV, sV.inc()))
            toks.append(sV.n)
        return sV.n

    sg_tok = _wload(sgT, 16, 1024, extra_w=[(sT, m3_done_T), (sV, sV.n)])
    sh_mm = 0
    sil_tok = [0, 0]
    for m in range(8):
        mps = ppA if m % 2 == 0 else ppB
        for qh in range(2):
            for ch in range(16):
                wts = []
                if ch == 0:
                    wts = [(sV, sg_tok)]
                    if m >= 2 or (m >= 1 and qh >= 0):
                        wts.append((sA, sil_tok[m % 2]))
                plan.add("tensor", wts,
                         (lambda m=m, qh=qh, ch=ch, mps=mps: NCT.matmul(
                             mps[:, qh * 512:(qh + 1) * 512],
                             wstr[:, ch * 1024 + m * 128:
                                  ch * 1024 + (m + 1) * 128],
                             h1T[:, ch * 1024 + qh * 512:
                                 ch * 1024 + (qh + 1) * 512],
                             start=(ch == 0), stop=(ch == 15))),
                         (sT, sT.inc()) if ch == 15 else None)
            plan.add("scalar", [(sT, sT.n)],
                     (lambda m=m, qh=qh, mps=mps: NCS.activation(
                         shg[:, m * 1024 + qh * 512:
                             m * 1024 + (qh + 1) * 512],
                         mps[:, qh * 512:(qh + 1) * 512], AF.Silu)),
                     (sA, sA.inc()))
        sil_tok[m % 2] = sA.n
    sg_mm_T = sT.n
    su_tok = _wload(suT, 16, 1024, extra_w=[(sT, sg_mm_T)])
    si_ev = [0, 0]
    for m in range(8):
        mps = ppA if m % 2 == 0 else ppB
        for qh in range(2):
            for ch in range(16):
                wts = []
                if ch == 0:
                    wts = [(sV, su_tok), (sA, sil_tok[1])]
                    if m >= 2:
                        wts.append((sV, si_ev[m % 2]))
                plan.add("tensor", wts,
                         (lambda m=m, qh=qh, ch=ch, mps=mps: NCT.matmul(
                             mps[:, qh * 512:(qh + 1) * 512],
                             wstr[:, ch * 1024 + m * 128:
                                  ch * 1024 + (m + 1) * 128],
                             h1T[:, ch * 1024 + qh * 512:
                                 ch * 1024 + (qh + 1) * 512],
                             start=(ch == 0), stop=(ch == 15))),
                         (sT, sT.inc()) if ch == 15 else None)
            plan.add("vector", [(sT, sT.n), (sV, sV.n)],
                     (lambda m=m, qh=qh, mps=mps: NCV.tensor_mul(
                         smal[:, 0:512],
                         shg[:, m * 1024 + qh * 512:m * 1024 + (qh + 1) * 512],
                         mps[:, qh * 512:(qh + 1) * 512])),
                     (sV, sV.inc()))
            plan.add("vector", [(sV, sV.n)],
                     (lambda m=m, qh=qh: NCV.tensor_scalar(
                         siT[:, m * 1024 + qh * 512:m * 1024 + (qh + 1) * 512],
                         smal[:, 0:512], SWIGLU_LIMIT, -SWIGLU_LIMIT,
                         ALU.min, ALU.max)),
                     (sV, sV.inc()))
        si_ev[m % 2] = sV.n
    su_mm_T = sT.n
    si_done = sV.n
    sd_tok = _wload(sdT, 8, 2048, extra_w=[(sT, su_mm_T)])
    sh_store = 0
    for ct in range(8):
        for ich in range(8):
            for n in range(4):
                ps = ppA if n < 2 else ppB
                wts = []
                if ich == 0 and n == 0:
                    wts = [(sV, sd_tok), (sV, si_done)]
                    if ct >= 1:
                        wts.append((sV, sV.n))
                plan.add("tensor", wts,
                         (lambda ct=ct, ich=ich, n=n, ps=ps: NCT.matmul(
                             ps[:, (n % 2) * 512:(n % 2 + 1) * 512],
                             siT[:, ich * 1024 + ct * 128:
                                 ich * 1024 + (ct + 1) * 128],
                             wstr[:, ich * 2048 + n * 512:
                                  ich * 2048 + (n + 1) * 512],
                             start=(ich == 0), stop=(ich == 7))),
                         (sT, sT.inc()) if (ich == 7 and n == 3) else None)
        shd_T = sT.n
        plan.add("vector", [(sT, shd_T)] +
                 ([(dS, sh_store)] if sh_store else []),
                 (lambda: NCV.tensor_copy(ybev[:, 0:1024], ppA[:, :])),
                 (sV, sV.inc()))
        plan.add("vector", [(sV, sV.n)],
                 (lambda: NCV.tensor_copy(ybev[:, 1024:2048], ppB[:, :])),
                 (sV, sV.inc()))
        plan.add("sync", [(sV, sV.n)],
                 (lambda ct=ct: NCY.dma_start(
                     out=shared_dram[ct * 128:(ct + 1) * 128, :],
                     in_=ybev[:, 0:2048])),
                 (dS, dS.inc()))
        sh_store = dS.n
        plan.wait("tensor", [(sV, sV.n)])
    m2c_T = sT.n

    if stop_after == "m2":
        plan.add("gpsimd", [(dS, sh_store), (dG, disp_done)],
                 (lambda: NCP.dma_start(out=out_slice, in_=shared_dram)),
                 (dG, dG.inc()))
        plan.wait("gpsimd", [(dG, dG.n)])
        _finalize(nc, plan, sems, stack)
        return nc

    _wb1 = [xTh[0], xTh[1], wqb, wkb]

    def _ws(i, ch, wid):
        if i % 2 == 0:
            return wstr[:, ch * wid:(ch + 1) * wid]
        if wid == 1024:
            return _wb1[ch // 4][:, (ch % 4) * 1024:(ch % 4 + 1) * 1024]
        return _wb1[ch // 2][:, (ch % 2) * 2048:(ch % 2 + 1) * 2048]

    def _wload5(i, wts):
        el, mat = i // 3, i % 3
        srcT = [wgT, wuT, wdT][mat]
        if i % 2 == 0:
            cfac = 16 if mat < 2 else 8
            plan.add("gpsimd", wts,
                     (lambda srcT=srcT, el=el, cfac=cfac: NCP.dma_start(
                         out=wstr[:, :].rearrange("p (c m) -> p c m", c=cfac),
                         in_=srcT[el].rearrange("(c p) m -> p c m", p=128))),
                     (dG, dG.inc()))
        else:
            cfac = 4 if mat < 2 else 2
            rpg = cfac * 128
            for g in range(4):
                plan.add("gpsimd", wts if g == 0 else [],
                         (lambda srcT=srcT, el=el, g=g, cfac=cfac, rpg=rpg:
                          NCP.dma_start(
                             out=_wb1[g][:, :].rearrange(
                                 "p (c m) -> p c m", c=cfac),
                             in_=srcT[el][g * rpg:(g + 1) * rpg, :].rearrange(
                                 "(c p) m -> p c m", p=128))),
                         (dG, dG.inc()))
        return dG.n

    wld_tok = {}
    grp_tok = {}
    yb_store = 0
    ag_toks = []
    xb_ev = {}
    xbtr_tok = 0
    wld_tok[0] = _wload5(0, [(sT, m2c_T)])
    wld_tok[1] = _wload5(1, [(sT, m2c_T), (sV, routw_tok)])
    for el in range(EPC):
        for ct in range(6):
            bufp = ct % 2
            wts = [(dG, disp_done)] if (el == 0 and ct == 0) else []
            if el == 0 and ct == 0:
                wts.append((sT, m2c_T))
                wts.append((sV, sV.n))
            if ct >= 2:
                wts.append((sT, sT.n))
            elif el > 0:
                wts.append((sT, xbtr_tok))
            plan.add("sync", wts,
                     (lambda el=el, ct=ct, bufp=bufp: NCY.dma_start(
                         out=lo_nat[bufp][:, :],
                         in_=xb_disp[el * CAP + ct * 128:
                                     el * CAP + (ct + 1) * 128, 0:1024])),
                     (dS, dS.inc()))
            plan.add("sync", [],
                     (lambda el=el, ct=ct, bufp=bufp: NCY.dma_start(
                         out=gbuf[bufp][:, 0:1024],
                         in_=xb_disp[el * CAP + ct * 128:
                                     el * CAP + (ct + 1) * 128, 1024:2048])),
                     (dS, dS.inc()))
            xb_tok = dS.n
            for ch in range(16):
                src = (lo_nat[bufp][:, ch * 128:(ch + 1) * 128] if ch < 8
                       else gbuf[bufp][:, (ch - 8) * 128:(ch - 7) * 128])
                i = el * 96 + ct * 16 + ch
                wts2 = [(dS, xb_tok)] if ch == 0 else []
                if i >= 2 and xb_ev.get(i - 2):
                    wts2.append((sV, xb_ev[i - 2]))
                elif i < 2:
                    wts2.append((sV, lt_ev[127]))
                plan.add("tensor", wts2,
                         (lambda src=src, i=i: NCT.transpose(
                             _tslot(i), src, identb[:, :])),
                         (sT, sT.inc()))
                plan.add("vector", [(sT, sT.n)],
                         (lambda ch=ch, ct=ct, i=i: NCV.tensor_copy(
                             xbT[:, ch * 768 + ct * 128:
                                 ch * 768 + (ct + 1) * 128],
                             _tslot(i))),
                         (sV, sV.inc()))
                xb_ev[i] = sV.n
        xbT_tok = sV.n
        xbtr_tok = sT.n
        li = 3 * el
        gs_tok = [0, 0]
        for m in range(8):
            mps = ppA if m % 2 == 0 else ppB
            for nn in range(2):
                n0, n1 = (0, 512) if nn == 0 else (512, 768)
                for ch in range(16):
                    wts = []
                    if ch == 0 and nn == 0:
                        wts = [(dG, wld_tok[li]), (sV, xbT_tok)]
                        if m >= 2:
                            wts.append((sA, gs_tok[m % 2]))
                    plan.add("tensor", wts,
                             (lambda m=m, ch=ch, n0=n0, n1=n1, mps=mps, li=li:
                              NCT.matmul(
                                  mps[:, n0:n1],
                                  _ws(li, ch, 1024)[:, m * 128:(m + 1) * 128],
                                  xbT[:, ch * 768 + n0:ch * 768 + n1],
                                  start=(ch == 0), stop=(ch == 15))),
                             (sT, sT.inc()) if (ch == 15 and nn == 1)
                             else None)
            plan.add("scalar", [(sT, sT.n)] +
                     ([(sV, h1T_done), (sA, sA.n)] if el == 0 and m == 0
                      else []),
                     (lambda m=m, mps=mps: NCS.activation(
                         gsil[:, m * 768:(m + 1) * 768],
                         mps[:, 0:768], AF.Silu)),
                     (sA, sA.inc()))
            gs_tok[m % 2] = sA.n
        grp_tok[li] = sT.n
        wld_tok[li + 2] = _wload5(li + 2, [(sT, grp_tok[li])])
        int_ev = [0, 0]
        for m in range(8):
            mps = ppA if m % 2 == 0 else ppB
            for nn in range(2):
                n0, n1 = (0, 512) if nn == 0 else (512, 768)
                for ch in range(16):
                    wts = []
                    if ch == 0 and nn == 0:
                        wts = [(dG, wld_tok[li + 1]), (sA, gs_tok[m % 2])]
                        if m >= 2:
                            wts.append((sV, int_ev[m % 2]))
                    plan.add("tensor", wts,
                             (lambda m=m, ch=ch, n0=n0, n1=n1, mps=mps, li=li:
                              NCT.matmul(
                                  mps[:, n0:n1],
                                  _ws(li + 1, ch, 1024)[:, m * 128:
                                                        (m + 1) * 128],
                                  xbT[:, ch * 768 + n0:ch * 768 + n1],
                                  start=(ch == 0), stop=(ch == 15))),
                             (sT, sT.inc()) if (ch == 15 and nn == 1)
                             else None)
            plan.add("vector", [(sT, sT.n)],
                     (lambda m=m, mps=mps: NCV.tensor_mul(
                         intT[:, m * 768:(m + 1) * 768],
                         gsil[:, m * 768:(m + 1) * 768],
                         mps[:, 0:768])),
                     (sV, sV.inc()))
            int_ev[m % 2] = sV.n
        grp_tok[li + 1] = sT.n
        int_tok = sV.n
        if el < EPC - 1:
            wld_tok[li + 3] = _wload5(li + 3, [(sT, grp_tok[li + 1])])
        for ct in range(6):
            for ich in range(8):
                for n in range(4):
                    ps = ppA if n < 2 else ppB
                    wts = []
                    if ich == 0 and n == 0:
                        wts = [(dG, wld_tok[li + 2]), (sV, int_tok)]
                        if ct >= 1:
                            wts.append((sV, sV.n))
                    plan.add("tensor", wts,
                             (lambda ct=ct, ich=ich, n=n, ps=ps, li=li:
                              NCT.matmul(
                                 ps[:, (n % 2) * 512:(n % 2 + 1) * 512],
                                 intT[:, ich * 768 + ct * 128:
                                      ich * 768 + (ct + 1) * 128],
                                 _ws(li + 2, ich, 2048)[:, n * 512:
                                                        (n + 1) * 512],
                                 start=(ich == 0), stop=(ich == 7))),
                             (sT, sT.inc()) if (ich == 7 and n == 3)
                             else None)
            plan.add("vector", [(sT, sT.n)] +
                     ([(dS, yb_store)] if yb_store else []),
                     (lambda: NCV.tensor_copy(ybev[:, 0:1024], ppA[:, :])),
                     (sV, sV.inc()))
            plan.add("vector", [(sV, sV.n)],
                     (lambda: NCV.tensor_copy(ybev[:, 1024:2048], ppB[:, :])),
                     (sV, sV.inc()))
            plan.add("sync", [(sV, sV.n)],
                     (lambda el=el, ct=ct: NCY.dma_start(
                         out=yb_loc[el * CAP + ct * 128:
                                    el * CAP + (ct + 1) * 128, :],
                         in_=ybev[:, 0:2048])),
                     (dS, dS.inc()))
            yb_store = dS.n
            plan.wait("tensor", [(sV, sV.n)])
        grp_tok[li + 2] = sT.n
        plan.add("gpsimd", [(dS, yb_store)],
                 (lambda el=el: NCP.collective_compute(
                     "AllGather", ALU.bypass,
                     replica_groups=[list(range(NCORES))],
                     ins=[yb_loc[el * CAP:(el + 1) * CAP, :]],
                     outs=[ybfull[el * NCORES * CAP:
                                  (el + 1) * NCORES * CAP, :]])),
                 (sC, sC.inc()))
        ag_toks.append(sC.n)
        if el < EPC - 1:
            wld_tok[li + 4] = _wload5(li + 4, [(sT, grp_tok[li + 2])])
    yb_all = _flush_barrier()

    if stop_after == "m5":
        plan.add("gpsimd", [(sC, yb_all)],
                 (lambda: NCP.dma_start(
                     out=out_slice,
                     in_=ybfull.rearrange("(a b) d -> a b d", b=24)[:, 0, :])),
                 (dG, dG.inc()))
        plan.wait("gpsimd", [(dG, dG.n)])
        _finalize(nc, plan, sems, stack)
        return nc

    plan.add("gpsimd", [(dS, pos_done), (dS, mconsts)],
             (lambda: _m6_pmy_load()),
             (dG, dG.inc()))

    def _m6_pmy_load():
        with NCP.register("ra") as ra:
            NCP.reg_load(ra, aoffs[0:1, 0:1])
            return NCP.dma_start(
                out=pmy,
                in_=bass.AP(posflat_h, ra, [[6, 128], [768, 8], [1, 6]]))
    pmy_tok = dG.n
    plan.add("vector", [(dG, pmy_tok), (sV, routw_tok)],
             (lambda: NCV.tensor_single_scalar(vldm, pmy, 768.0, ALU.is_le)),
             (sV, sV.inc()))
    plan.add("vector", [(sV, sV.n)],
             (lambda: NCV.tensor_scalar(qv, routb[:, 0:48], 0.25, -0.375,
                                        ALU.mult, ALU.add)),
             (sV, sV.inc()))
    plan.add("vector", [(sV, sV.n)],
             (lambda: NCV.tensor_copy(idxu[:, 448:496], qv)),
             (sV, sV.inc()))
    plan.add("vector", [(sV, sV.n)],
             (lambda: NCV.tensor_copy(e4f, idxu[:, 448:496])),
             (sV, sV.inc()))
    plan.add("vector", [(sV, sV.n)],
             (lambda: NCV.scalar_tensor_tensor(t1v, routb[:, 0:48], 6144.0,
                                               pmy, ALU.mult, ALU.add)),
             (sV, sV.inc()))
    plan.add("vector", [(sV, sV.n)],
             (lambda: NCV.scalar_tensor_tensor(rowf, e4f, -23808.0, t1v,
                                               ALU.mult, ALU.add)),
             (sV, sV.inc()))
    plan.add("vector", [(sV, sV.n)],
             (lambda: NCV.tensor_scalar(rowf, rowf, 1.0, -1.0 - float(1 << 20),
                                        ALU.mult, ALU.add)),
             (sV, sV.inc()))
    plan.add("vector", [(sV, sV.n)],
             (lambda: NCV.tensor_mul(rowf, rowf, vldm)),
             (sV, sV.inc()))
    plan.add("vector", [(sV, sV.n)],
             (lambda: NCV.tensor_scalar(rowf, rowf, 1.0, float(1 << 20),
                                        ALU.mult, ALU.add)),
             (sV, sV.inc()))
    plan.add("vector", [(sV, sV.n)],
             (lambda: NCV.tensor_copy(idxu[:, 384:432], rowf)),
             (sV, sV.inc()))
    plan.add("vector", [(sV, sV.n)],
             (lambda: NCV.tensor_mul(weff[:, :], routw[:, 0:48], vldm)),
             (sV, sV.inc()))
    idxr_tok = sV.n
    plan.add("vector", [(sV, sV.n)],
             (lambda: NCV.memset(gbuf[0][:, :], 0.0)), (sV, sV.inc()))
    plan.add("vector", [(sV, sV.n)],
             (lambda: NCV.memset(gbuf[1][:, :], 0.0)), (sV, sV.inc()))
    gz_tok = sV.n

    out_tok = 0
    g_done = {}
    comb_tok = {}
    prev_z2 = 0
    for c in range(8):
        plan.add("sync", ([(sV, sV.n)] if c == 0 else
                          [(sV, prev_z2), (dS, out_tok)]),
                 (lambda c=c: NCY.dma_start(
                     out=xcast[0], in_=h1_dram[c * 128:(c + 1) * 128, :])),
                 (dS, dS.inc()))
        plan.add("sync", [(dS, sh_store)] if c == 0 else [],
                 (lambda c=c: NCY.dma_start(
                     out=ybev[:, 0:2048],
                     in_=shared_dram[c * 128:(c + 1) * 128, :])),
                 (dS, dS.inc()))
        ld_tok = dS.n
        for k in range(K):
            kk = c * 6 + k
            wts = []
            if c == 0 and k == 0:
                wts = [(sC, yb_all), (sV, idxr_tok), (sV, gz_tok)]
            if kk >= 2:
                wts.append((sV, comb_tok.get(kk - 2, 0)))
            plan.add("gpsimd", wts,
                     (lambda c=c, k=k, kk=kk: NCP.indirect_dma_start(
                         gbuf[kk % 2][:, :], None,
                         ybfull,
                         IndirectOffsetOnAxis(idxu[:, 384 + kk:385 + kk], 0),
                         bounds_check=regs["ret"], oob_is_err=False)),
                     (dG, dG.inc()))
            g_done[kk] = dG.n
            if k == 0:
                plan.add("vector", [(dG, g_done[kk]), (sA, sA.n)],
                         (lambda c=c, kk=kk: NCV.tensor_scalar(
                             zbuf[0], gbuf[kk % 2][:, :],
                             weff[:, kk:kk + 1], None, ALU.mult)),
                         (sV, sV.inc()))
            else:
                plan.add("vector", [(dG, g_done[kk]), (sV, sV.n)],
                         (lambda c=c, kk=kk: NCV.scalar_tensor_tensor(
                             zbuf[0], gbuf[kk % 2][:, :],
                             weff[:, kk:kk + 1], zbuf[0],
                             ALU.mult, ALU.add)),
                         (sV, sV.inc()))
            comb_tok[kk] = sV.n
        plan.add("vector", [(sV, sV.n), (dS, ld_tok), (dS, out_tok)],
                 (lambda: NCV.scalar_tensor_tensor(
                     zfin, ybev[:, 0:2048], 1.0, zbuf[0],
                     ALU.mult, ALU.add)),
                 (sV, sV.inc()))
        plan.add("vector", [(sV, sV.n)],
                 (lambda: NCV.scalar_tensor_tensor(
                     zbuf[0], xcast[0], 1.0, zfin,
                     ALU.mult, ALU.add, accum_out=stat[:, 0:1])),
                 (sV, sV.inc()))
        z2_tok = sV.n
        prev_z2 = z2_tok
        plan.add("scalar", [(sV, z2_tok)],
                 (lambda: NCS.activation(sq2[:, 0:2048], zbuf[0], AF.Square,
                                         accum_out=stat[:, 1:2])),
                 (sA, sA.inc()))
        plan.add("vector", [(sA, sA.n)],
                 (lambda: NCV.tensor_scalar(stat[:, 2:3], stat[:, 0:1],
                                            1.0 / D, 0.0, ALU.mult, ALU.add)),
                 (sV, sV.inc()))
        plan.add("vector", [(sV, sV.n)],
                 (lambda: NCV.tensor_scalar(stat[:, 3:4], stat[:, 1:2],
                                            1.0 / D, 0.0, ALU.mult, ALU.add)),
                 (sV, sV.inc()))
        plan.add("vector", [(sV, sV.n)],
                 (lambda: NCV.tensor_mul(stat[:, 4:5], stat[:, 2:3],
                                         stat[:, 2:3])),
                 (sV, sV.inc()))
        plan.add("vector", [(sV, sV.n)],
                 (lambda: NCV.tensor_sub(stat[:, 5:6], stat[:, 3:4],
                                         stat[:, 4:5])),
                 (sV, sV.inc()))
        plan.add("vector", [(sV, sV.n)],
                 (lambda: NCV.tensor_scalar(stat[:, 6:7], stat[:, 5:6],
                                            1.0, EPS, ALU.mult, ALU.add)),
                 (sV, sV.inc()))
        plan.add("scalar", [(sV, sV.n)],
                 (lambda: NCS.activation(stat[:, 7:8], stat[:, 6:7], AF.Ln)),
                 (sA, sA.inc()))
        plan.add("scalar", [(sA, sA.n)],
                 (lambda: NCS.activation(stat[:, 8:9], stat[:, 7:8], AF.Exp,
                                         scale=-0.5)),
                 (sA, sA.inc()))
        plan.add("vector", [(sA, sA.n)],
                 (lambda: NCV.tensor_scalar(zfin, zbuf[0], stat[:, 2:3],
                                            stat[:, 8:9],
                                            ALU.subtract, ALU.mult)),
                 (sV, sV.inc()))
        fin_tok = sV.n
        plan.add("sync", [(sV, fin_tok)],
                 (lambda c=c: NCY.dma_start(
                     out=out_slice[c * 128:(c + 1) * 128, :], in_=zfin)),
                 (dS, dS.inc()))
        out_tok = dS.n
        plan.wait("vector", [(dS, out_tok)]) if c == 7 else None
    plan.wait("gpsimd", [(dS, out_tok)])

    _finalize(nc, plan, sems, stack)
    return nc


def _finalize(nc, plan, sems, stack):
    _check_deadlock(plan)
    for s in sems:
        s.h = stack.enter_context(nc.semaphore(s.name))
    with nc.Block() as block:
        @block.tensor
        def _(eng):
            for f in plan.ops["tensor"]:
                f()

        @block.scalar
        def _(eng):
            for f in plan.ops["scalar"]:
                f()

        @block.vector
        def _(eng):
            for f in plan.ops["vector"]:
                f()

        @block.gpsimd
        def _(eng):
            for f in plan.ops["gpsimd"]:
                f()

        @block.sync
        def _(eng):
            for f in plan.ops["sync"]:
                f()
    stack.close()


def _stage(inputs):
    import ml_dtypes
    x = np.asarray(inputs["L0_attn_x"]).reshape(T, D)
    Wq = np.asarray(inputs["L0_attn_Wq"])
    Wk = np.asarray(inputs["L0_attn_Wk"])
    Wv = np.asarray(inputs["L0_attn_Wv"])
    Wo = np.asarray(inputs["L0_attn_Wo"])
    bo = np.asarray(inputs["L0_attn_bo"])
    gw = np.asarray(inputs["L0_ffn_gate_w"])
    gb = np.asarray(inputs["L0_ffn_bias"])
    wg = np.asarray(inputs["L0_ffn_w_gate"])
    wu = np.asarray(inputs["L0_ffn_w_up"])
    wd = np.asarray(inputs["L0_ffn_w_down"])
    sg = np.asarray(inputs["L0_ffn_sg"])
    su = np.asarray(inputs["L0_ffn_su"])
    sd = np.asarray(inputs["L0_ffn_sd"])

    C = np.ascontiguousarray
    mk = np.zeros((4, 128, 512), np.float32)
    for j in range(4):
        kk = 128 * j + np.arange(128)[:, None]
        qq = np.arange(512)[None, :]
        mk[j] = (kk <= qq).astype(np.float32)
    masks = mk.astype(ml_dtypes.bfloat16)
    ident = np.eye(128, dtype=ml_dtypes.bfloat16)
    lsta = np.zeros((2, 128, 64), np.float32)
    lsta[0, :, 0] = 1.0
    lsta[1, :, 32] = 1.0
    lstaf = lsta.astype(np.float32)
    lsta = lsta.astype(ml_dtypes.bfloat16)
    ones1f = np.ones((64, 128), np.float32)
    iota32 = np.broadcast_to(np.arange(E, dtype=np.float32), (128, E)).copy()
    bo_bc = np.broadcast_to(bo.astype(np.float32), (128, D)).copy()
    gb_bc = np.broadcast_to(gb.astype(np.float32), (128, E)).copy()
    gwT = C(np.asarray(gw, np.float32).T)

    maps = []
    for r in range(NCORES):
        hs = slice(r * HPC * HD, (r + 1) * HPC * HD)
        es = slice(r * EPC, (r + 1) * EPC)
        maps.append({
            "x_slice": C(x[r * TPC:(r + 1) * TPC]),
            "wq": C(Wq[:, hs]), "wk": C(Wk[:, hs]), "wv": C(Wv[:, hs]),
            "wo": C(Wo),
            "bo_bc": bo_bc,
            "gwT": gwT,
            "gbias_bc": gb_bc,
            "wgT": C(wg[es].transpose(0, 2, 1)),
            "wuT": C(wu[es].transpose(0, 2, 1)),
            "wdT": C(wd[es].transpose(0, 2, 1)),
            "sgT": C(sg.T), "suT": C(su.T), "sdT": C(sd.T),
            "masks": masks,
            "iota32_bc": iota32,
            "ident": ident,
            "lsta": lsta,
            "lstaf": lstaf,
            "ones1f": ones1f,
            "rk4": np.full((128, 1), 4.0 * r, np.float32),
            "qoff": np.full((1, 1), (r // 2) * 2048 * 2048 + (r % 2) * 1024,
                            np.uint32),
            "aoff": np.full((1, 1), r * TPC * K, np.uint32),
            "iotacol": np.arange(E, dtype=np.float32).reshape(E, 1),
        })
    return maps


def kernel(**inputs):
    maps = _stage(inputs)
    nc = _build(stop_after=None)
    res = run_bass_kernel_spmd(nc, maps, core_ids=list(range(NCORES)))
    out = np.concatenate([res.results[r]["out_slice"] for r in range(NCORES)], 0)
    return out.reshape(1, T, D).astype(np.float32, copy=False)
